# version 01
# baseline (speedup 1.0000x reference)
import numpy as np
import ml_dtypes
import concourse.bass as bass
import concourse.mybir as mybir
from concourse.bass_utils import run_bass_kernel_spmd
from contextlib import ExitStack

F32 = mybir.dt.float32
BF16 = mybir.dt.bfloat16
I32 = mybir.dt.int32
U8 = mybir.dt.uint8
Alu = mybir.AluOpType
AF = mybir.ActivationFunctionType
BF16NP = ml_dtypes.bfloat16

NTCOL = 145
TBL_W = NTCOL + 1

PATK = ["k1a", "k1b", "k1c", "k2", "k3", "kd8", "ku"]
PATR = ["rh", "r1", "r2", "r3", "rd8"]
PATKS = {
    "k1a": ([[-1, 32]], 31),
    "k1b": ([[-1, 64]], 63),
    "k1c": ([[-1, 96]], 95),
    "k2": ([[0, 8], [-1, 16]], 15),
    "k3": ([[0, 16], [-1, 8]], 7),
    "kd8": ([[0, 16], [-1, 8]], 3),
    "ku": (None, None),
}
PATRS = {
    "rh": ([[1, 128]], 0),
    "r1": ([[0, 4], [1, 32]], 0),
    "r2": ([[0, 4], [1, 32]], -16),
    "r3": ([[0, 8], [1, 16]], -8),
    "rd8": ([[0, 16], [1, 8]], -4),
}


def pat_width(dims):
    w = 1
    for _, c in dims:
        w *= c
    return w


KW = sum(pat_width(PATKS[n][0]) if n != "ku" else 128 for n in PATK)
RW = sum(pat_width(PATRS[n][0]) for n in PATR)


def make_cfg(D=4096, T=2048, I=11008, B=4):
    c = {"D": D, "T": T, "I": I, "B": B}
    c["H"] = D // 64
    c["nI"] = D // 128
    c["nJb"] = (D // 2) // 128
    c["GI"] = D // 64
    c["nO"] = I // 128
    c["nJd"] = (I // 2) // 128
    c["GD"] = I // 64
    c["nC"] = D // 128
    c["nN"] = T // 128
    c["TT"] = min(512, T)
    c["nTT"] = T // c["TT"]
    c["TH2"] = T // 2
    c["nT2"] = c["TH2"] // min(512, c["TH2"])
    c["TT2"] = min(512, c["TH2"])
    c["TQ"] = min(512, T)
    c["nQ"] = T // c["TQ"]
    assert D % 256 == 0 and T % 256 == 0 and I % 256 == 0
    assert c["GI"] <= 128
    return c



def pi_perm(n):
    return np.concatenate([np.arange(0, n, 2), np.arange(1, n, 2)])


def host_masks():
    j = np.arange(128)[:, None]
    i = np.arange(128)[None, :]
    m1 = (j // 32 < i // 32)
    m2 = (i // 32 == j // 32) & (j % 32 < 16) & (i % 32 >= 16)
    m3 = (i // 16 == j // 16) & (j % 16 < 8) & (i % 16 >= 8)
    md8 = (i // 8 == j // 8) & (j < i)
    mdiag = (i == j)
    return [np.tile(m.astype(np.uint8), (1, 2)) for m in (m1, m2, m3, md8, mdiag)]


def host_constants(cfg):
    GI, nI, nJb, nN = cfg["GI"], cfg["nI"], cfg["nJb"], cfg["nN"]
    cons = {}
    for nm, m in zip(("c_m1", "c_m2", "c_m3", "c_md8", "c_mdiag"), host_masks()):
        cons[nm] = m
    E4 = np.zeros((4, 128), np.float32)
    for gp in range(4):
        E4[gp, gp * 32:(gp + 1) * 32] = 1.0
    cons["c_E4"] = E4
    p = np.arange(128)
    E16 = np.zeros((nI, 128, GI), np.float32)
    for c in range(nI):
        E16[c, p, 4 * (c % nJb) + p // 32] = 1.0
    cons["c_E16"] = E16.astype(BF16NP)
    E2h = np.zeros((128, 2), np.float32)
    E2h[p, p // 64] = 1.0
    cons["c_E2h"] = E2h.astype(BF16NP)
    F2 = np.zeros((2, 128), np.float32)
    F2[p // 64, p] = 1.0
    cons["c_F2"] = F2
    cons["c_ones1"] = np.ones((1, 128), np.float32)
    cons["c_onesD"] = np.ones((128, 1), np.float32)
    cons["c_Prow"] = np.tile(-np.arange(-16, 129, dtype=np.float32)[None, :], (128, 1))
    cons["c_Pn"] = np.tile(-128.0 * np.arange(nN, dtype=np.float32)[None, :], (128, 1))
    cons["c_idn"] = np.eye(128, dtype=np.float32)
    return cons


def host_core_inputs(cfg, core, inputs, cons):
    D, T, I, nC = cfg["D"], cfg["T"], cfg["I"], cfg["nC"]
    b, hf = core // 2, core % 2
    pi = pi_perm(D)
    pio = pi_perm(I)
    x = np.asarray(inputs["L0_attn_hidden"])
    m = dict(cons)
    xc = np.zeros((D, T + 1), np.float32)
    s0 = hf * T
    xc[:, 1:] = np.ascontiguousarray(x[b, s0:s0 + T].T)[pi]
    if hf == 1:
        xc[:, 0] = x[b, s0 - 1][pi]
    m["xT"] = np.ascontiguousarray(xc)
    for nm, w in (("wkT", "L0_attn_w_key"), ("wvT", "L0_attn_w_value"),
                  ("wrT", "L0_attn_w_receptance"), ("wgT", "L0_attn_w_gate")):
        m[nm] = np.ascontiguousarray(np.asarray(inputs[w]).T[pi])
    m["woT"] = np.ascontiguousarray(np.asarray(inputs["L0_attn_w_output"]).T[:, pi])
    for nm, v in (("tmk", "L0_attn_time_mix_key"), ("tmv", "L0_attn_time_mix_value"),
                  ("tmr", "L0_attn_time_mix_receptance"), ("tmg", "L0_attn_time_mix_gate")):
        m[nm] = np.ascontiguousarray(np.asarray(inputs[v])[pi].reshape(nC, 128).T)
    dec = np.asarray(inputs["L0_attn_time_decay"]).reshape(-1)
    u = np.asarray(inputs["L0_attn_time_faaaa"]).reshape(-1)
    m["dec64"] = np.ascontiguousarray(dec.reshape(nC, 2, 64).transpose(2, 0, 1).reshape(64, 2 * nC))
    m["u64"] = np.ascontiguousarray(u.reshape(nC, 2, 64).transpose(2, 0, 1).reshape(64, 2 * nC))
    m["dec128"] = np.ascontiguousarray(dec.reshape(nC, 128).T)
    m["gnw"] = np.ascontiguousarray(np.asarray(inputs["L0_attn_gn_weight"]).reshape(nC, 128).T)
    m["gnb"] = np.ascontiguousarray(np.asarray(inputs["L0_attn_gn_bias"]).reshape(nC, 128).T)
    m["wgq"] = np.ascontiguousarray(np.asarray(inputs["L0_ffn_gate_Wq"])[pio].T)
    m["wuq"] = np.ascontiguousarray(np.asarray(inputs["L0_ffn_up_Wq"])[pio].T)
    nJb, nJd = cfg["nJb"], cfg["nJd"]
    for pre, snm, znm in (("g", "L0_ffn_gate_scale", "L0_ffn_gate_zero"),
                          ("u", "L0_ffn_up_scale", "L0_ffn_up_zero")):
        sT = np.ascontiguousarray(np.asarray(inputs[snm])[pio].T)
        zT = np.ascontiguousarray(np.asarray(inputs[znm])[pio].T)
        m[pre + "sT"] = sT
        m[pre + "zT"] = zT
        m[pre + "s4"] = np.ascontiguousarray(sT.reshape(nJb, 4, I).transpose(1, 0, 2))
    m["wdq"] = np.ascontiguousarray(np.asarray(inputs["L0_ffn_down_Wq"])[pi].T)
    sTd = np.ascontiguousarray(np.asarray(inputs["L0_ffn_down_scale"])[pi].T)
    zTd = np.ascontiguousarray(np.asarray(inputs["L0_ffn_down_zero"])[pi].T)
    m["ds4"] = np.ascontiguousarray(sTd.reshape(nJd, 4, D).transpose(1, 0, 2))
    m["dz4"] = np.ascontiguousarray(zTd.reshape(nJd, 4, D).transpose(1, 0, 2))
    m["sel64"] = np.full((128, 1), 1.0 if hf == 1 else 0.0, np.float32)
    m["selinv"] = np.full((128, 1), 0.0 if hf == 1 else 1.0, np.float32)
    return m


def host_assemble_output(cfg, results):
    D, T, B = cfg["D"], cfg["T"], cfg["B"]
    pi = pi_perm(D)
    inv = np.empty(D, np.int64)
    inv[pi] = np.arange(D)
    out = np.empty((B, 2 * T, D), np.float32)
    for core in range(2 * B):
        b, hf = core // 2, core % 2
        o = np.asarray(results[core]["out"])
        out[b, hf * T:(hf + 1) * T] = o[inv].T
    return out



class Sem:
    def __init__(self, nc, name):
        self.h = nc.alloc_semaphore(name)
        self.v = 0

    def inc(self, instr, n=1):
        instr.then_inc(self.h, n)
        self.v += n
        return instr

    def wait(self, eng, v=None):
        eng.wait_ge(self.h, self.v if v is None else v)


class Arena:
    def __init__(self, nc, base, size):
        self.nc, self.base, self.size = nc, base, size
        self.off = 0
        self.uid = 0

    def reset(self):
        self.off = 0

    def t(self, name, shape, dtype):
        per = int(np.prod(shape[1:])) * mybir.dt.size(dtype)
        off = (self.off + 31) & ~31
        assert off + per <= self.size, f"arena overflow at {name}: {off}+{per}>{self.size}"
        self.off = off + per
        self.uid += 1
        return self.nc.alloc_sbuf_tensor_at(f"{name}_{self.uid}", list(shape), dtype,
                                            offset=self.base + off)


def build_program(cfg, dump=(), stop_after=None):
    nc = bass.Bass("TRN2", target_bir_lowering=False, debug=False,
                   detect_race_conditions=False)
    D, T, I = cfg["D"], cfg["T"], cfg["I"]
    nI, nJb, GI = cfg["nI"], cfg["nJb"], cfg["GI"]
    nO, nJd = cfg["nO"], cfg["nJd"]
    nC, nN, TT, nTT = cfg["nC"], cfg["nN"], cfg["TT"], cfg["nTT"]
    TH2, nT2, TT2 = cfg["TH2"], cfg["nT2"], cfg["TT2"]
    TQ, nQ = cfg["TQ"], cfg["nQ"]
    dump = set(dump)

    def dram(name, shape, dtype, kind="Internal"):
        if kind == "Internal" and name in dump:
            kind = "ExternalOutput"
        return nc.dram_tensor(name, list(shape), dtype, kind=kind).ap()

    xT = dram("xT", (D, T + 1), F32, "ExternalInput")
    wT = {p: dram(nm, (D, D), F32, "ExternalInput")
          for p, nm in (("k", "wkT"), ("v", "wvT"), ("r", "wrT"), ("g", "wgT"))}
    woT = dram("woT", (D, D), F32, "ExternalInput")
    tmix = {p: dram(nm, (128, nC), F32, "ExternalInput")
            for p, nm in (("k", "tmk"), ("v", "tmv"), ("r", "tmr"), ("g", "tmg"))}
    dec64d = dram("dec64", (64, 2 * nC), F32, "ExternalInput")
    u64d = dram("u64", (64, 2 * nC), F32, "ExternalInput")
    dec128d = dram("dec128", (128, nC), F32, "ExternalInput")
    gnwd = dram("gnw", (128, nC), F32, "ExternalInput")
    gnbd = dram("gnb", (128, nC), F32, "ExternalInput")
    wgq = dram("wgq", (D // 2, I), I32, "ExternalInput")
    wuq = dram("wuq", (D // 2, I), I32, "ExternalInput")
    wdq = dram("wdq", (I // 2, D), I32, "ExternalInput")
    gsT = dram("gsT", (GI, I), F32, "ExternalInput")
    gzT = dram("gzT", (GI, I), F32, "ExternalInput")
    gs4 = dram("gs4", (4, nJb, I), F32, "ExternalInput")
    usT = dram("usT", (GI, I), F32, "ExternalInput")
    uzT = dram("uzT", (GI, I), F32, "ExternalInput")
    us4 = dram("us4", (4, nJb, I), F32, "ExternalInput")
    ds4 = dram("ds4", (4, nJd, D), F32, "ExternalInput")
    dz4 = dram("dz4", (4, nJd, D), F32, "ExternalInput")
    sel64d = dram("sel64", (128, 1), F32, "ExternalInput")
    selinvd = dram("selinv", (128, 1), F32, "ExternalInput")
    c_m = [dram(n, (128, 256), U8, "ExternalInput")
           for n in ("c_m1", "c_m2", "c_m3", "c_md8", "c_mdiag")]
    c_E4 = dram("c_E4", (4, 128), F32, "ExternalInput")
    c_E16 = dram("c_E16", (nI, 128, GI), BF16, "ExternalInput")
    c_E2h = dram("c_E2h", (128, 2), BF16, "ExternalInput")
    c_F2 = dram("c_F2", (2, 128), F32, "ExternalInput")
    c_ones1 = dram("c_ones1", (1, 128), F32, "ExternalInput")
    c_onesD = dram("c_onesD", (128, 1), F32, "ExternalInput")
    c_Prow = dram("c_Prow", (128, NTCOL), F32, "ExternalInput")
    c_Pn = dram("c_Pn", (128, nN), F32, "ExternalInput")
    c_idn = dram("c_idn", (128, 128), F32, "ExternalInput")

    xn_s = dram("xn_s", (D, T + 1), BF16)
    k_s = dram("k_s", (D, T), BF16)
    v_s = dram("v_s", (D, T), BF16)
    r_s = dram("r_s", (D, T), BF16)
    g_s = dram("g_s", (D, T), BF16)
    y_s = dram("y_s", (D, T), BF16)
    part_s = dram("part_s", (D, T), F32)
    h_s = dram("h_s", (I, T), BF16)
    wd_s = dram("wd_s", (I, D), BF16)
    cc_in = dram("cc_in", (D, 64), F32)
    cc_out = dram("cc_out", (D, 64), F32)
    out = dram("out", (D, T), F32, "ExternalOutput")
    dbg1 = dram("dbg1", (128, nJd * 128), F32)
    dbg2 = dram("dbg2", (128, nJd * 128), F32)
    dbg3 = dram("dbg3", (128, nI * 128), BF16)
    dbg4 = dram("dbg4", (128, nI * T), BF16)
    dbg5 = dram("dbg5", (128, nO * 128), BF16)
    dbg6 = dram("dbg6", (128, nO * TQ), BF16)

    sb = lambda name, shape, dtype: nc.alloc_sbuf_tensor(name, list(shape), dtype)

    sb_tmix = {p: sb("s_tm_" + p, (128, nC), F32) for p in "kvrg"}
    sb_gnw = sb("s_gnw", (128, nC), F32)
    sb_gnb = sb("s_gnb", (128, nC), F32)
    sb_sel64 = sb("s_sel64", (128, 1), F32)
    sb_selinv = sb("s_selinv", (128, 1), F32)
    sb_m = [sb("s_m%d" % i, (128, 256), U8) for i in range(5)]
    sb_E4 = sb("s_E4", (4, 128), F32)
    sb_E16 = sb("s_E16", (128, nI * GI), BF16)
    sb_E2h = sb("s_E2h", (128, 2), BF16)
    sb_F2 = sb("s_F2", (2, 128), F32)
    sb_ones1 = sb("s_ones1", (1, 128), F32)
    sb_onesD = sb("s_onesD", (128, 1), F32)
    sb_Prow = sb("s_Prow", (128, NTCOL), F32)
    sb_Pn = sb("s_Pn", (128, nN), F32)
    sb_id = sb("s_id", (128, 128), BF16)
    sb_zero = sb("s_zero", (128, 256), F32)
    sb_eps = sb("s_eps", (128, 1), F32)
    sb_sgrp = sb("s_sgrp", (GI, T), BF16)
    sb_dec64 = sb("s_dec64", (64, 2 * nC), F32)
    sb_u64 = sb("s_u64", (64, 2 * nC), F32)
    sb_dec128 = sb("s_dec128", (128, nC), F32)

    pb = [nc.alloc_psum_tensor("pb%d" % i, [128, 512], F32) for i in range(7)]
    pt = nc.alloc_psum_tensor("pt", [128, 1024], BF16)

    rem = nc.sbuf_bytes_remaining
    ARENA = rem - 2048
    arena_t = nc.alloc_sbuf_tensor("ARENA", [128, ARENA], U8)
    abase = nc.lookup_mloc(arena_t).addr
    AR = Arena(nc, abase, ARENA)

    s_ld = Sem(nc, "ld")
    s_la = Sem(nc, "la")
    s_lg = Sem(nc, "lg")
    s_mm = Sem(nc, "mm")
    s_dv = Sem(nc, "dv")
    s_ac = Sem(nc, "ac")
    s_cc = Sem(nc, "cc")
    sx = [Sem(nc, "sx%d" % i) for i in range(2)]
    wsl = [Sem(nc, "wsl%d" % i) for i in range(2)]
    pks = [Sem(nc, "pks%d" % i) for i in range(2)]
    szs = [Sem(nc, "szs%d" % i) for i in range(2)]
    scs = [Sem(nc, "scs%d" % i) for i in range(2)]
    s_xh = Sem(nc, "xh")
    s_hq = Sem(nc, "hq")
    wds = [Sem(nc, "wds%d" % i) for i in range(2)]

    SY, GP, SC, TE, VE = nc.sync, nc.gpsimd, nc.scalar, nc.tensor, nc.vector
    DMA_SY = [s_ld, s_xh, s_hq] + sx + szs
    DMA_GP = [s_lg] + wsl + pks + wds
    DMA_SC = [s_la] + scs

    def bar():
        for s in DMA_SY:
            SY.wait_ge(s.h, s.v)
        for s in DMA_SC:
            SC.wait_ge(s.h, s.v)
        for s in DMA_GP:
            GP.wait_ge(s.h, s.v)
        nc.all_engine_barrier()

    for dst, src in ((sb_sel64, sel64d), (sb_selinv, selinvd), (sb_E4, c_E4),
                     (sb_E2h, c_E2h), (sb_F2, c_F2), (sb_ones1, c_ones1),
                     (sb_onesD, c_onesD), (sb_Prow, c_Prow), (sb_Pn, c_Pn),
                     (sb_gnw, gnwd), (sb_gnb, gnbd), (sb_dec64, dec64d),
                     (sb_u64, u64d), (sb_dec128, dec128d)):
        s_ld.inc(SY.dma_start(out=dst[:], in_=src), 16)
    for i in range(5):
        s_ld.inc(SY.dma_start(out=sb_m[i][:], in_=c_m[i]), 16)
    s_ld.inc(SY.dma_start(out=sb_E16[:], in_=bass.AP(c_E16.tensor, 0, [[GI, 128], [128 * GI, nI], [1, GI]])), 16)
    for p in "kvrg":
        s_ld.inc(SY.dma_start(out=sb_tmix[p][:], in_=tmix[p]), 16)
    s_lg.inc(GP.dma_start(out=sb_id[:], in_=c_idn), 16)
    s_ld.wait(VE)
    VE.memset(sb_zero[:], 0.0)
    VE.memset(sb_eps[:], 1e-5)
    for _p in pb:
        VE.memset(_p[:], 0.0)
    bar()

    AR.reset()
    p1_x = [AR.t("p1x", (128, TT), F32) for _ in range(2)]
    p1_sq = [AR.t("p1sq", (128, TT), F32) for _ in range(2)]
    p1_row = AR.t("p1row", (1, 4 * TT), F32)
    p1_xn = [AR.t("p1xn", (128, TT), BF16) for _ in range(2)]
    widths = [1] + [TT] * nTT
    offs = np.cumsum([0] + widths).tolist()
    for ti, (o0, w) in enumerate(zip(offs[:-1], widths)):
        SY.wait_ge(s_dv.h, s_dv.v)
        mmv1 = []
        for ic in range(nI):
            buf = p1_x[ic % 2]
            sqb = p1_sq[ic % 2]
            if ic >= 2:
                SY.wait_ge(s_mm.h, mmv1[ic - 2])
                SC.wait_ge(s_mm.h, mmv1[ic - 2])
            sx[ic % 2].inc(SY.dma_start(out=buf[:, 0:w], in_=xT[128 * ic:128 * (ic + 1), o0:o0 + w],
                                        allow_slow_non_contiguous=(w == 1)), 16)
            sx[ic % 2].wait(SC)
            s_ac.inc(SC.activation(sqb[:, 0:w], buf[:, 0:w], AF.Square), 1)
            sx[ic % 2].wait(TE)
            if ic == 0:
                s_dv.wait(TE)
            TE.matmul(pb[4][0:1, 0:w], sb_onesD[:], buf[:, 0:w],
                      start=(ic == 0), stop=(ic == nI - 1))
            s_ac.wait(TE)
            s_mm.inc(TE.matmul(pb[4][32:33, 0:w], sb_onesD[:], sqb[:, 0:w],
                               start=(ic == 0), stop=(ic == nI - 1)), 1)
            mmv1.append(s_mm.v)
        s_mm.wait(VE)
        mrow = p1_row[0:1, 0:w]
        rrow = p1_row[0:1, TT:TT + w]
        trow = p1_row[0:1, 2 * TT:2 * TT + w]
        qrow = p1_row[0:1, 3 * TT:3 * TT + w]
        s_dv.inc(VE.tensor_scalar_mul(mrow, pb[4][0:1, 0:w], 1.0 / D), 1)
        VE.wait_ge(s_dv.h, s_dv.v)
        s_dv.inc(VE.tensor_scalar_mul(qrow, pb[4][32:33, 0:w], 1.0 / D), 1)
        VE.wait_ge(s_dv.h, s_dv.v)
        s_dv.inc(VE.tensor_tensor(trow, mrow, mrow, Alu.mult), 1)
        VE.wait_ge(s_dv.h, s_dv.v)
        s_dv.inc(VE.tensor_sub(trow, qrow, trow), 1)
        VE.wait_ge(s_dv.h, s_dv.v)
        s_dv.wait(SC)
        s_ac.inc(SC.activation(qrow, trow, AF.Sqrt, bias=sb_eps[0:1, 0:1]), 1)
        s_ac.wait(VE)
        s_dv.inc(VE.reciprocal(rrow, qrow), 1)
        VE.wait_ge(s_dv.h, s_dv.v)
        s_dv.wait(TE)
        TE.matmul(pb[5][:, 0:w], sb_ones1[:], mrow, start=True, stop=True)
        s_mm.inc(TE.matmul(pb[6][:, 0:w], sb_ones1[:], rrow, start=True, stop=True), 1)
        SY.wait_ge(s_mm.h, s_mm.v)
        dvv2 = []
        e16v = []
        for ic in range(nI):
            buf = p1_x[ic % 2]
            xnb = p1_xn[ic % 2]
            if ic >= 2:
                SY.wait_ge(s_dv.h, dvv2[ic - 2])
            sx[ic % 2].inc(SY.dma_start(out=buf[:, 0:w], in_=xT[128 * ic:128 * (ic + 1), o0:o0 + w],
                                        allow_slow_non_contiguous=(w == 1)), 16)
            sx[ic % 2].wait(VE)
            s_mm.wait(VE)
            if ic >= 2:
                VE.wait_ge(s_ld.h, s_ld.v)
                if ti > 0:
                    VE.wait_ge(s_mm.h, e16v[ic - 2])
            VE.tensor_sub(buf[:, 0:w], buf[:, 0:w], pb[5][:, 0:w])
            s_dv.inc(VE.tensor_tensor(xnb[:, 0:w], buf[:, 0:w], pb[6][:, 0:w], Alu.mult), 1)
            dvv2.append(s_dv.v)
            s_dv.wait(SY)
            s_ld.inc(SY.dma_start(out=xn_s[128 * ic:128 * (ic + 1), o0:o0 + w], in_=xnb[:, 0:w],
                                  allow_slow_non_contiguous=(w == 1)), 16)
            if ti > 0:
                s_dv.wait(TE)
                s_mm.inc(TE.matmul(pb[4][64:64 + GI, 0:w], sb_E16[:, ic * GI:(ic + 1) * GI],
                                   xnb[:, 0:w], start=(ic == 0), stop=(ic == nI - 1)), 1)
                e16v.append(s_mm.v)
        if ti > 0:
            s_mm.wait(VE)
            s_dv.inc(VE.tensor_copy(sb_sgrp[:, o0 - 1:o0 - 1 + w], pb[4][64:64 + GI, 0:w]), 1)
    bar()

    if stop_after == "P1":
        return nc, ["out"]

    AR.reset()
    bigA = AR.t("p2mix", (128, nI * T), BF16)
    p2_xn = AR.t("p2xn", (128, T + 1), BF16)
    p2_d = AR.t("p2d", (128, T), BF16)
    p2_ws = [AR.t("p2ws", (128, nI * 128), BF16) for _ in range(2)]
    p2_ev = AR.t("p2ev", (128, T), BF16)

    def bigA_c(c):
        return bigA[:, c * T:(c + 1) * T]

    def wslab_load(w_ap, oc, buf):
        src = bass.AP(w_ap.tensor, oc * 128, [[D, 128], [128 * D, nI], [1, 128]])
        wsl[oc % 2].inc(GP.dma_start(out=buf[:].rearrange("p (a o) -> p a o", a=nI), in_=src), 16)
        return wsl[oc % 2].v

    for pi_, p in enumerate("kvrg"):
        for ic in range(nI):
            s_dv.wait(SY)
            s_ld.inc(SY.dma_start(out=p2_xn[:], in_=xn_s[128 * ic:(128 * ic + 128), :]), 16)
            s_ld.wait(VE)
            if pi_ > 0 or ic >= 1:
                TE.wait_ge(s_mm.h, s_mm.v)
                VE.wait_ge(s_mm.h, s_mm.v)
            VE.tensor_sub(p2_d[:], p2_xn[:, 1:T + 1], p2_xn[:, 0:T])
            s_dv.inc(VE.scalar_tensor_tensor(bigA_c(ic), p2_d[:], sb_tmix[p][:, ic:ic + 1],
                                             p2_xn[:, 0:T], Alu.mult, Alu.add), 1)
        lg_vals = {}
        GP.wait_ge(s_mm.h, s_mm.v)
        lg_vals[0] = wslab_load(wT[p], 0, p2_ws[0])
        lg_vals[1] = wslab_load(wT[p], 1, p2_ws[1])
        dst = {"k": k_s, "v": v_s, "r": r_s, "g": g_s}[p]
        for oc in range(nC):
            buf = p2_ws[oc % 2]
            bs = 0
            if p == "k" and oc == nC - 1 and "dbg3" in dump:
                SY.wait_ge(s_lg.h, lg_vals[oc])
                SY.wait_ge(s_dv.h, s_dv.v)
                s_ld.inc(SY.dma_start(out=dbg3, in_=buf[:]), 16)
                s_ld.inc(SY.dma_start(out=dbg4, in_=bigA[:]), 16)
            TE.wait_ge(wsl[oc % 2].h, lg_vals[oc])
            TE.wait_ge(s_dv.h, s_dv.v)
            if p == "g":
                TE.wait_ge(s_ac.h, s_ac.v)
            for a in range(nI):
                for tt in range(nTT):
                    ins = TE.matmul(pb[bs + tt][:, 0:TT], buf[:, a * 128:(a + 1) * 128],
                                    bigA_c(a)[:, tt * TT:(tt + 1) * TT],
                                    start=(a == 0), stop=(a == nI - 1))
            s_mm.inc(ins, 1)
            if oc + 2 < nC:
                GP.wait_ge(s_mm.h, s_mm.v)
                lg_vals[oc + 2] = wslab_load(wT[p], oc + 2, buf)
            s_mm.wait(VE)
            s_mm.wait(SC)
            s_ld.wait(VE)
            s_ld.wait(SC)
            for tt in range(nTT):
                if p == "g":
                    ins = SC.activation(p2_ev[:, tt * TT:(tt + 1) * TT], pb[bs + tt][:, 0:TT], AF.Silu)
                else:
                    ins = VE.tensor_copy(p2_ev[:, tt * TT:(tt + 1) * TT], pb[bs + tt][:, 0:TT])
            (s_ac if p == "g" else s_dv).inc(ins, 1)
            (s_ac if p == "g" else s_dv).wait(SY)
            s_ld.inc(SY.dma_start(out=dst[oc * 128:(oc + 1) * 128, :], in_=p2_ev[:]), 16)
    bar()

    if stop_after == "P2":
        return nc, ["out"]

    AR.reset()
    p3_k64 = AR.t("k64", (64, 2 * T), BF16)
    p3_r64 = AR.t("r64", (64, 2 * T), BF16)
    p3_vfm = AR.t("vfm", (128, T), BF16)
    p3_kfm = AR.t("kfm", (128, T), BF16)
    p3_lw64 = AR.t("lw64", (64, 2), F32)
    p3_lw128 = AR.t("lw128", (128, 1), F32)
    p3_tmp64 = AR.t("tmp64", (64, 2 * NTCOL), F32)
    p3_T64 = AR.t("T64", (64, 2 * TBL_W), F32)
    p3_tmp128 = AR.t("tmp128", (128, NTCOL), F32)
    p3_T128 = AR.t("T128", (128, NTCOL), F32)
    p3_patkf = AR.t("patkf", (64, 2 * KW), F32)
    p3_patrf = AR.t("patrf", (64, 2 * RW), F32)
    p3_patk = AR.t("patk", (64, 2 * KW), BF16)
    p3_patr = AR.t("patr", (64, 2 * RW), BF16)
    p3_sck = AR.t("sck", (64, 2 * nN * KW), BF16)
    p3_scr = AR.t("scr", (64, 2 * nN * RW), BF16)
    p3_khat = AR.t("khat", (128, T), BF16)
    p3_vtok = AR.t("vtok", (128, 256), BF16)
    p3_A2F = AR.t("A2F", (128, 256), F32)
    p3_A2 = AR.t("A2", (128, 256), BF16)
    p3_SA = AR.t("SA", (64, 64), F32)
    p3_SB = AR.t("SB", (64, 64), F32)
    p3_SbfA = AR.t("SbfA", (64, 64), BF16)
    p3_SbfB = AR.t("SbfB", (64, 64), BF16)
    p3_w128 = AR.t("w128", (64, 2), F32)
    p3_yc = AR.t("yc", (128, T), BF16)
    p3_sfin = AR.t("sfin", (128, 64), F32)

    def t64ap(dims, p0):
        return bass.AP(p3_T64, p0 + 16, [[2 * TBL_W, 64], [TBL_W, 2]] + dims)

    def dst_dims(dims):
        o, acc = [], 1
        for s_, c_ in reversed(dims):
            o.append([acc, c_])
            acc *= c_
        return list(reversed(o))

    patk_off = {}
    off = 0
    for nm in PATK:
        w = 128 if nm == "ku" else pat_width(PATKS[nm][0])
        patk_off[nm] = (off, w)
        off += 2 * w
    patr_off = {}
    off = 0
    for nm in PATR:
        w = pat_width(PATRS[nm][0])
        patr_off[nm] = (off, w)
        off += 2 * w

    def sck_ap(nm, h, n, c0=0, w_=None):
        po, w = patk_off[nm]
        w_ = w if w_ is None else w_
        return bass.AP(p3_sck, po * nN + (h * nN + n) * w + c0, [[2 * nN * KW, 64], [1, w_]])

    def scr_ap(nm, h, n, c0=0, w_=None):
        po, w = patr_off[nm]
        w_ = w if w_ is None else w_
        return bass.AP(p3_scr, po * nN + (h * nN + n) * w + c0, [[2 * nN * RW, 64], [1, w_]])

    def r64_ap(h, n):
        return bass.AP(p3_r64, h * T + n * 128, [[2 * T, 64], [1, 128]])

    T64v = p3_T64[:].rearrange("d (h c) -> d h c", h=2, c=TBL_W)
    tmp64v = p3_tmp64[:].rearrange("d (h c) -> d h c", h=2)

    for c in range(nC):
        s_dv.wait(SY)
        for src, dstt in ((k_s, p3_k64), (r_s, p3_r64)):
            ap = bass.AP(src.tensor, c * 128 * T, [[T, 64], [64 * T, 2], [1, T]])
            s_ld.inc(SY.dma_start(out=dstt[:].rearrange("d (h t) -> d h t", h=2), in_=ap), 16)
        s_ld.inc(SY.dma_start(out=p3_vfm[:], in_=v_s[c * 128:(c + 1) * 128, :]), 16)
        s_ld.inc(SY.dma_start(out=p3_kfm[:], in_=k_s[c * 128:(c + 1) * 128, :]), 16)
        SC.wait_ge(s_dv.h, s_dv.v)
        s_ac.inc(SC.activation(p3_lw64[:], sb_dec64[:, 2 * c:2 * c + 2], AF.Exp), 1)
        s_ac.inc(SC.activation(p3_lw128[:], sb_dec128[:, c:c + 1], AF.Exp), 1)
        s_ac.wait(VE)
        VE.tensor_tensor(tmp64v, bass.AP(sb_Prow, 0, [[NTCOL, 64], [0, 2], [1, NTCOL]]),
                         bass.AP(p3_lw64, 0, [[2, 64], [1, 2], [0, NTCOL]]), Alu.mult)
        VE.tensor_tensor(p3_tmp128[:], sb_Prow[:],
                         bass.AP(p3_lw128, 0, [[1, 128], [0, NTCOL]]), Alu.mult)
        s_dv.inc(VE.tensor_copy(T64v[:, :, NTCOL:TBL_W],
                                bass.AP(sb_u64, 2 * c, [[2 * nC, 64], [1, 2], [0, 1]])), 1)
        s_dv.wait(SC)
        s_ac.inc(SC.activation(T64v[:, :, 0:NTCOL], tmp64v, AF.Exp), 1)
        s_ac.inc(SC.activation(p3_T128[:], p3_tmp128[:], AF.Exp), 1)
        s_ac.wait(VE)
        VE.tensor_scalar_min(T64v[:, :, 0:NTCOL], T64v[:, :, 0:NTCOL], 1e30)
        VE.tensor_scalar_min(p3_T128[:], p3_T128[:], 1e30)
        s_dv.inc(VE.tensor_copy(p3_w128[:], bass.AP(p3_T64, 144, [[2 * TBL_W, 64], [TBL_W, 2]])), 1)
        VE.wait_ge(s_dv.h, s_dv.v)
        off = 0
        for nm in PATK:
            if nm == "ku":
                w, dims = 128, [[0, 128]]
                srcp = bass.AP(p3_T64, NTCOL, [[2 * TBL_W, 64], [TBL_W, 2]] + dims)
            else:
                dims, p0 = PATKS[nm]
                w = pat_width(dims)
                srcp = t64ap(dims, p0)
            VE.tensor_copy(bass.AP(p3_patkf, off, [[2 * KW, 64], [w, 2]] + dst_dims(dims)), srcp)
            off += 2 * w
        off = 0
        for nm in PATR:
            dims, p0 = PATRS[nm]
            w = pat_width(dims)
            VE.tensor_copy(bass.AP(p3_patrf, off, [[2 * RW, 64], [w, 2]] + dst_dims(dims)),
                           t64ap(dims, p0))
            off += 2 * w
        VE.tensor_copy(p3_patk[:], p3_patkf[:])
        s_dv.inc(VE.tensor_copy(p3_patr[:], p3_patrf[:]), 1)
        VE.wait_ge(s_dv.h, s_dv.v)
        s_ld.wait(VE)
        for PAT, offs_, dstt, dstW, src64, patt, patW in (
                (PATK, patk_off, p3_sck, KW, p3_k64, p3_patk, KW),
                (PATR, patr_off, p3_scr, RW, p3_r64, p3_patr, RW)):
            for nm in PAT:
                po, w = offs_[nm]
                o_ap = bass.AP(dstt, po * nN, [[2 * nN * dstW, 64], [nN * w, 2], [w, nN], [1, w]])
                i0 = bass.AP(src64, 0, [[2 * T, 64], [T, 2], [128, nN], [1, w]])
                i1 = bass.AP(patt, po, [[2 * patW, 64], [w, 2], [0, nN], [1, w]])
                VE.tensor_tensor(o_ap, i0, i1, Alu.mult)
        VE.tensor_tensor(p3_khat[:].rearrange("p (n j) -> p n j", n=nN),
                         p3_kfm[:].rearrange("p (n j) -> p n j", n=nN),
                         bass.AP(p3_T128, 127 + 16, [[NTCOL, 128], [0, nN], [-1, 128]]),
                         Alu.mult)
        VE.memset(p3_SA[:], 0.0)
        VE.memset(p3_SB[:], 0.0)
        VE.memset(p3_SbfA[:], 0.0)
        s_dv.inc(VE.memset(p3_SbfB[:], 0.0), 1)
        for n in range(nN):
            TE.wait_ge(s_dv.h, s_dv.v)
            TE.transpose(pt[:, 0:128], p3_vfm[:, n * 128:(n + 1) * 128], sb_id[:])
            s_mm.inc(TE.transpose(pt[:, 128:256], p3_khat[:, n * 128:(n + 1) * 128], sb_id[:]), 1)
            s_mm.wait(VE)
            s_dv.inc(VE.tensor_copy(p3_vtok[:], pt[:, 0:256]), 1)
            TE.wait_ge(s_dv.h, s_dv.v)
            for h in range(2):
                c0 = h * 128
                TE.matmul(pb[3][64 * h:64 * h + 64, 0:128], p3_SbfA[:] if h == 0 else p3_SbfB[:],
                          scr_ap("rh", h, n), start=True, stop=False, skip_group_check=True)
                TE.matmul(pb[0][0:32, c0 + 32:c0 + 64], sck_ap("k1a", h, n), scr_ap("r1", h, n, 32, 32), start=True, stop=True)
                TE.matmul(pb[0][0:64, c0 + 64:c0 + 96], sck_ap("k1b", h, n), scr_ap("r1", h, n, 64, 32), start=True, stop=True)
                TE.matmul(pb[0][0:96, c0 + 96:c0 + 128], sck_ap("k1c", h, n), scr_ap("r1", h, n, 96, 32), start=True, stop=True)
                TE.matmul(pb[0][:, 256 + c0:256 + c0 + 128], sck_ap("k2", h, n, w_=128), scr_ap("r2", h, n, w_=128), start=True, stop=True)
                TE.matmul(pb[1][:, c0:c0 + 128], sck_ap("k3", h, n, w_=128), scr_ap("r3", h, n, w_=128), start=True, stop=True)
                TE.matmul(pb[1][:, 256 + c0:256 + c0 + 128], sck_ap("kd8", h, n, w_=128), scr_ap("rd8", h, n, w_=128), start=True, stop=True)
                ins = TE.matmul(pb[2][:, c0:c0 + 128], sck_ap("ku", h, n, w_=128), r64_ap(h, n), start=True, stop=True)
            s_mm.inc(ins, 1)
            s_mm.wait(VE)
            VE.select(p3_A2F[:], sb_m[0][:], pb[0][:, 0:256], sb_zero[:])
            VE.select(p3_A2F[:], sb_m[1][:], pb[0][:, 256:512], p3_A2F[:])
            VE.select(p3_A2F[:], sb_m[2][:], pb[1][:, 0:256], p3_A2F[:])
            VE.select(p3_A2F[:], sb_m[3][:], pb[1][:, 256:512], p3_A2F[:])
            VE.select(p3_A2F[:], sb_m[4][:], pb[2][:, 0:256], p3_A2F[:])
            s_dv.inc(VE.tensor_copy(p3_A2[:], p3_A2F[:]), 1)
            TE.wait_ge(s_dv.h, s_dv.v)
            for h in range(2):
                TE.matmul(pb[3][64 * h:64 * h + 64, 0:128], bass.AP(p3_vtok, 64 * h, [[256, 128], [1, 64]]),
                          p3_A2[:, 128 * h:128 * h + 128], start=False, stop=True, skip_group_check=True)
                ins = TE.matmul(pb[2][0:64, 256 + 64 * h:256 + 64 * h + 64],
                                bass.AP(p3_vtok, 128 + 64 * h, [[256, 128], [1, 64]]),
                                bass.AP(p3_vtok, 64 * h, [[256, 128], [1, 64]]), start=True, stop=True)
            s_mm.inc(ins, 1)
            s_mm.wait(VE)
            VE.tensor_copy(p3_yc[:, n * 128:(n + 1) * 128], pb[3][:, 0:128])
            s_dv.inc(VE.scalar_tensor_tensor(p3_SA[:], p3_SA[:], p3_w128[:, 0:1],
                                             pb[2][0:64, 256:320], Alu.mult, Alu.add), 1)
            VE.wait_ge(s_dv.h, s_dv.v)
            s_dv.inc(VE.scalar_tensor_tensor(p3_SB[:], p3_SB[:], p3_w128[:, 1:2],
                                             pb[2][0:64, 320:384], Alu.mult, Alu.add), 1)
            VE.wait_ge(s_dv.h, s_dv.v)
            VE.tensor_copy(p3_SbfA[:], p3_SA[:])
            s_dv.inc(VE.tensor_copy(p3_SbfB[:], p3_SB[:]), 1)
        s_dv.wait(SY)
        s_ld.inc(SY.dma_start(out=y_s[c * 128:(c + 1) * 128, :], in_=p3_yc[:]), 16)
        VE.wait_ge(s_dv.h, s_dv.v)
        VE.tensor_scalar_mul(p3_sfin[0:64, :], p3_SA[:], sb_selinv[0:64, 0:1])
        s_dv.inc(VE.tensor_scalar_mul(p3_sfin[64:128, :], p3_SB[:], sb_selinv[0:64, 0:1]), 1)
        s_dv.wait(SY)
        s_ld.inc(SY.dma_start(out=cc_in[c * 128:(c + 1) * 128, :], in_=p3_sfin[:]), 16)
    bar()

    if stop_after == "P3":
        return nc, ["out"]

    s_cc.inc(GP.collective_compute("AllReduce", Alu.add,
                                   replica_groups=[[0, 1], [2, 3], [4, 5], [6, 7]],
                                   ins=[cc_in.opt()], outs=[cc_out.opt()]), 1)
    s_cc.wait(SY)
    bar()

    AR.reset()
    bigZ = AR.t("p3z", (128, nI * T), BF16)
    b3_r64 = AR.t("b3r64", (64, 2 * T), BF16)
    b3_yc = AR.t("b3yc", (128, T), BF16)
    b3_g = AR.t("b3g", (128, T), BF16)
    b3_lw64 = AR.t("b3lw64", (64, 2), F32)
    b3_tmp64 = AR.t("b3tmp64", (64, 2 * NTCOL), F32)
    b3_T64 = AR.t("b3T64", (64, 2 * TBL_W), F32)
    b3_tmpn = AR.t("b3tmpn", (64, 2 * nN), F32)
    b3_w128p = AR.t("b3w128p", (64, 2 * nN), F32)
    b3_rhf = AR.t("b3rhf", (64, 2 * 128), F32)
    b3_rh = AR.t("b3rh", (64, 2 * 128), BF16)
    b3_rbar = AR.t("b3rbar", (64, 2 * T), BF16)
    b3_SiA = AR.t("b3SiA", (64, 64), F32)
    b3_SiB = AR.t("b3SiB", (64, 64), F32)
    b3_SiAb = AR.t("b3SiAb", (64, 64), BF16)
    b3_SiBb = AR.t("b3SiBb", (64, 64), BF16)
    b3_sq = AR.t("b3sq", (128, TT), BF16)
    b3_row = AR.t("b3row", (2, 4 * TT), F32)
    b3_t1 = AR.t("b3t1", (128, TT), F32)

    def bigZ_c(c):
        return bigZ[:, c * T:(c + 1) * T]

    T64vb = b3_T64[:].rearrange("d (h c) -> d h c", h=2, c=TBL_W)
    tmp64vb = b3_tmp64[:].rearrange("d (h c) -> d h c", h=2)
    rb_all = bass.AP(b3_rbar, 0, [[2 * T, 64], [T, 2], [128, nN], [1, 128]])
    for c in range(nC):
        s_dv.wait(SY)
        ap = bass.AP(r_s.tensor, c * 128 * T, [[T, 64], [64 * T, 2], [1, T]])
        s_ld.inc(SY.dma_start(out=b3_r64[:].rearrange("d (h t) -> d h t", h=2), in_=ap), 16)
        s_ld.inc(SY.dma_start(out=b3_yc[:], in_=y_s[c * 128:(c + 1) * 128, :]), 16)
        s_ld.inc(SY.dma_start(out=b3_g[:], in_=g_s[c * 128:(c + 1) * 128, :]), 16)
        s_ld.inc(SY.dma_start(out=b3_SiA[:], in_=cc_out[c * 128:c * 128 + 64, :]), 16)
        s_ld.inc(SY.dma_start(out=b3_SiB[:], in_=cc_out[c * 128 + 64:(c + 1) * 128, :]), 16)
        SC.wait_ge(s_dv.h, s_dv.v)
        s_ac.inc(SC.activation(b3_lw64[:], sb_dec64[:, 2 * c:2 * c + 2], AF.Exp), 1)
        s_ac.wait(VE)
        VE.tensor_tensor(tmp64vb, bass.AP(sb_Prow, 0, [[NTCOL, 64], [0, 2], [1, NTCOL]]),
                         bass.AP(b3_lw64, 0, [[2, 64], [1, 2], [0, NTCOL]]), Alu.mult)
        s_dv.inc(VE.tensor_tensor(b3_tmpn[:].rearrange("d (h n) -> d h n", h=2),
                                  bass.AP(sb_Pn, 0, [[nN, 64], [0, 2], [1, nN]]),
                                  bass.AP(b3_lw64, 0, [[2, 64], [1, 2], [0, nN]]), Alu.mult), 1)
        s_dv.wait(SC)
        s_ac.inc(SC.activation(T64vb[:, :, 0:NTCOL], tmp64vb, AF.Exp), 1)
        s_ac.inc(SC.activation(b3_w128p[:], b3_tmpn[:], AF.Exp), 1)
        s_ac.wait(VE)
        VE.tensor_scalar_min(b3_w128p[:], b3_w128p[:], 1e30)
        VE.tensor_copy(bass.AP(b3_rhf, 0, [[2 * 128, 64], [128, 2], [1, 128]]),
                       bass.AP(b3_T64, 16, [[2 * TBL_W, 64], [TBL_W, 2], [1, 128]]))
        s_dv.inc(VE.tensor_copy(b3_rh[:], b3_rhf[:]), 1)
        VE.wait_ge(s_dv.h, s_dv.v)
        s_ld.wait(VE)
        VE.tensor_tensor(rb_all,
                         bass.AP(b3_r64, 0, [[2 * T, 64], [T, 2], [128, nN], [1, 128]]),
                         bass.AP(b3_rh, 0, [[2 * 128, 64], [128, 2], [0, nN], [1, 128]]),
                         Alu.mult)
        s_dv.inc(VE.tensor_tensor(rb_all, rb_all,
                                  bass.AP(b3_w128p, 0, [[2 * nN, 64], [nN, 2], [1, nN], [0, 128]]),
                                  Alu.mult), 1)
        VE.tensor_scalar_mul(b3_SiAb[:], b3_SiA[:], sb_sel64[0:64, 0:1])
        s_dv.inc(VE.tensor_scalar_mul(b3_SiBb[:], b3_SiB[:], sb_sel64[0:64, 0:1]), 1)
        for n in range(nN):
            TE.wait_ge(s_dv.h, s_dv.v)
            TE.matmul(pb[3][0:64, 0:128], b3_SiAb[:],
                      bass.AP(b3_rbar, n * 128, [[2 * T, 64], [1, 128]]), start=True, stop=True)
            s_mm.inc(TE.matmul(pb[3][64:128, 0:128], b3_SiBb[:],
                               bass.AP(b3_rbar, T + n * 128, [[2 * T, 64], [1, 128]]),
                               start=True, stop=True), 1)
            s_mm.wait(VE)
            s_dv.inc(VE.tensor_add(b3_yc[:, n * 128:(n + 1) * 128],
                                   b3_yc[:, n * 128:(n + 1) * 128], pb[3][:, 0:128]), 1)
        for tt in range(nTT):
            t0 = tt * TT
            s_dv.wait(SC)
            SC.wait_ge(s_mm.h, s_mm.v)
            s_ac.inc(SC.activation(b3_sq[:], b3_yc[:, t0:t0 + TT], AF.Square), 1)
            s_dv.wait(TE)
            s_ac.wait(TE)
            TE.matmul(pb[4][0:2, 0:TT], sb_E2h[:], b3_yc[:, t0:t0 + TT], start=True, stop=True)
            s_mm.inc(TE.matmul(pb[4][32:34, 0:TT], sb_E2h[:], b3_sq[:], start=True, stop=True), 1)
            s_mm.wait(VE)
            mrow = b3_row[:, 0:TT]
            rrow = b3_row[:, TT:2 * TT]
            trow = b3_row[:, 2 * TT:3 * TT]
            qrow = b3_row[:, 3 * TT:4 * TT]
            s_dv.inc(VE.tensor_scalar_mul(mrow, pb[4][0:2, 0:TT], 1.0 / 64), 1)
            VE.wait_ge(s_dv.h, s_dv.v)
            s_dv.inc(VE.tensor_scalar_mul(qrow, pb[4][32:34, 0:TT], 1.0 / 64), 1)
            VE.wait_ge(s_dv.h, s_dv.v)
            s_dv.inc(VE.tensor_tensor(trow, mrow, mrow, Alu.mult), 1)
            VE.wait_ge(s_dv.h, s_dv.v)
            s_dv.inc(VE.tensor_sub(trow, qrow, trow), 1)
            VE.wait_ge(s_dv.h, s_dv.v)
            s_dv.wait(SC)
            s_ac.inc(SC.activation(qrow, trow, AF.Sqrt, bias=sb_eps[0:2, 0:1]), 1)
            s_ac.wait(VE)
            s_dv.inc(VE.reciprocal(rrow, qrow), 1)
            VE.wait_ge(s_dv.h, s_dv.v)
            s_dv.wait(TE)
            TE.matmul(pb[5][:, 0:TT], sb_F2[:], mrow, start=True, stop=True)
            s_mm.inc(TE.matmul(pb[6][:, 0:TT], sb_F2[:], rrow, start=True, stop=True), 1)
            s_mm.wait(VE)
            VE.tensor_sub(b3_t1[:], b3_yc[:, t0:t0 + TT], pb[5][:, 0:TT])
            VE.tensor_tensor(b3_t1[:], b3_t1[:], pb[6][:, 0:TT], Alu.mult)
            VE.tensor_scalar(b3_t1[:], b3_t1[:], sb_gnw[:, c:c + 1], sb_gnb[:, c:c + 1],
                             Alu.mult, Alu.add)
            s_dv.inc(VE.tensor_tensor(bigZ_c(c)[:, t0:t0 + TT], b3_t1[:],
                                      b3_g[:, t0:t0 + TT], Alu.mult), 1)
    bar()

    if stop_after == "P3b":
        return nc, ["out"]

    p4_ws = [AR.t("p4ws", (128, nC * 128), BF16) for _ in range(2)]
    p4_x = AR.t("p4x", (128, TT), F32)
    p4_o = AR.t("p4o", (128, TT), F32)

    def woslab_load(ec, buf):
        src = bass.AP(woT.tensor, ec * 128, [[D, 128], [128 * D, nC], [1, 128]])
        wsl[ec % 2].inc(GP.dma_start(out=buf[:].rearrange("p (a o) -> p a o", a=nC), in_=src), 16)
        return wsl[ec % 2].v

    lg_vals = {0: woslab_load(0, p4_ws[0]), 1: woslab_load(1, p4_ws[1])}
    for ec in range(nI):
        buf = p4_ws[ec % 2]
        bs = 0
        TE.wait_ge(wsl[ec % 2].h, lg_vals[ec])
        TE.wait_ge(s_dv.h, s_dv.v)
        for a in range(nC):
            for tt in range(nTT):
                ins = TE.matmul(pb[bs + tt][:, 0:TT], buf[:, a * 128:(a + 1) * 128],
                                bigZ_c(a)[:, tt * TT:(tt + 1) * TT],
                                start=(a == 0), stop=(a == nC - 1))
        s_mm.inc(ins, 1)
        if ec + 2 < nI:
            GP.wait_ge(s_mm.h, s_mm.v)
            lg_vals[ec + 2] = woslab_load(ec + 2, buf)
        for tt in range(nTT):
            SY.wait_ge(s_dv.h, s_dv.v)
            s_ld.inc(SY.dma_start(out=p4_x[:, 0:TT],
                                  in_=xT[ec * 128:(ec + 1) * 128, 1 + tt * TT:1 + (tt + 1) * TT]), 16)
            s_ld.wait(VE)
            s_mm.wait(VE)
            s_dv.inc(VE.tensor_add(p4_o[:], pb[bs + tt][:, 0:TT], p4_x[:, 0:TT]), 1)
            s_dv.wait(SY)
            s_ld.inc(SY.dma_start(out=part_s[ec * 128:(ec + 1) * 128, tt * TT:(tt + 1) * TT],
                                  in_=p4_o[:]), 16)
            s_ld.wait(VE)
    bar()

    if stop_after == "P4":
        return nc, ["out"]

    AR.reset()
    f_xn = AR.t("fxn", (128, nI * TH2), BF16)
    f_pk = [AR.t("fpk", (128, nJb * 128), I32) for _ in range(2)]
    f_int = AR.t("fint", (128, nJb * 128), I32)
    f_sc4 = [AR.t("fsc4", (4, nJb * 128), F32) for _ in range(2)]
    f_slab = [AR.t("fslab", (128, nJb * 128), F32) for _ in range(2)]
    f_sz = [AR.t("fsz", (GI, 256), F32) for _ in range(2)]
    f_zsc = [AR.t("fzsc", (GI, 128), BF16) for _ in range(2)]
    f_wbf = [AR.t("fwbf", (128, 2 * nJb * 128), BF16) for _ in range(2)]
    f_gev = AR.t("fgev", (128, TH2), BF16)
    f_h = AR.t("fh", (128, TH2), BF16)
    NSP = (nJb * 128 + 511) // 512

    def fxn_c(a):
        return f_xn[:, a * TH2:(a + 1) * TH2]

    passes = [(th, oc, pj) for th in range(2) for oc in range(nO) for pj in range(2)]
    pair_of = lambda q: q % 3
    prep_done = [None] * len(passes)
    mmv = [None] * len(passes)
    evd_ac = [0] * len(passes)
    evd_dv = [0] * len(passes)

    def emit_prep(q):
        th, oc, pj = passes[q]
        wq_t = (wgq if pj == 0 else wuq)
        sT_t = (gsT if pj == 0 else usT)
        zT_t = (gzT if pj == 0 else uzT)
        s4_t = (gs4 if pj == 0 else us4)
        pk = f_pk[q % 2]
        sc4 = f_sc4[q % 2]
        slab = f_slab[q % 2]
        sz = f_sz[q % 2]
        zsc = f_zsc[q % 2]
        wbf = f_wbf[q % 2]
        if q >= 2 and mmv[q - 2] is not None:
            GP.wait_ge(s_mm.h, mmv[q - 2])
            SY.wait_ge(s_mm.h, mmv[q - 2])
            SC.wait_ge(s_mm.h, mmv[q - 2])
        pks[q % 2].inc(GP.dma_start(out=pk[:].rearrange("p (a o) -> p a o", a=nJb),
                                     in_=bass.AP(wq_t.tensor, oc * 128, [[I, 128], [128 * I, nJb], [1, 128]])), 16)
        szs[q % 2].inc(SY.dma_start(out=sz[:, 0:128], in_=sT_t[:, oc * 128:(oc + 1) * 128]), 16)
        szs[q % 2].inc(SY.dma_start(out=sz[:, 128:256], in_=zT_t[:, oc * 128:(oc + 1) * 128]), 16)
        scs[q % 2].inc(SC.dma_start(out=sc4[:].rearrange("p (a o) -> p a o", a=nJb),
                                    in_=bass.AP(s4_t.tensor, oc * 128, [[nJb * I, 4], [I, nJb], [1, 128]])), 16)
        ld_here = (pks[q % 2].v, szs[q % 2].v, scs[q % 2].v)
        for b_ in range(NSP):
            wv = min(512, nJb * 128 - b_ * 512)
            TE.wait_ge(scs[q % 2].h, ld_here[2])
            if b_ > 0 or q > 0:
                TE.wait_ge(s_dv.h, s_dv.v)
            s_mm.inc(TE.matmul(pb[6][:, 0:wv], sb_E4[:], sc4[:, b_ * 512:b_ * 512 + wv],
                               start=True, stop=True), 1)
            s_mm.wait(VE)
            s_dv.inc(VE.tensor_copy(slab[:, b_ * 512:b_ * 512 + wv], pb[6][:, 0:wv]), 1)
        VE.wait_ge(szs[q % 2].h, ld_here[1])
        s_dv.inc(VE.scalar_tensor_tensor(zsc[:], sz[:, 128:256], -1.0, sz[:, 0:128],
                                         Alu.mult, Alu.mult), 1)
        VE.wait_ge(pks[q % 2].h, ld_here[0])
        if q >= 2 and mmv[q - 2] is not None:
            VE.wait_ge(s_mm.h, mmv[q - 2])
        VE.tensor_single_scalar(f_int[:], pk[:], 4, Alu.logical_shift_right)
        s_dv.inc(VE.tensor_tensor(wbf[:, 0:nJb * 128], f_int[:], slab[:], Alu.mult), 1)
        VE.tensor_single_scalar(f_int[:], pk[:], 15, Alu.bitwise_and)
        s_dv.inc(VE.tensor_tensor(wbf[:, nJb * 128:2 * nJb * 128], f_int[:], slab[:], Alu.mult), 1)
        prep_done[q] = s_dv.v

    emit_prep(0)
    emit_prep(1)
    for q, (th, oc, pj) in enumerate(passes):
        tb = th * TH2
        if oc == 0 and pj == 0:
            SY.wait_ge(s_mm.h, s_mm.v)
            src = bass.AP(xn_s.tensor, 1 + tb, [[T + 1, 128], [128 * (T + 1), nI], [1, TH2]])
            s_xh.inc(SY.dma_start(out=f_xn[:].rearrange("p (a t) -> p a t", a=nI), in_=src), 16)
        ldx = s_xh.v
        pr = pair_of(q)
        wbf = f_wbf[q % 2]
        zsc = f_zsc[q % 2]
        TE.wait_ge(s_dv.h, prep_done[q])
        TE.wait_ge(s_xh.h, ldx)
        if q >= 3:
            TE.wait_ge(s_ac.h, evd_ac[q - 3])
            TE.wait_ge(s_dv.h, evd_dv[q - 3])
        for tt in range(nT2):
            TE.matmul(pb[2 * pr + tt][:, 0:TT2], zsc[:],
                      sb_sgrp[:, tb + tt * TT2:tb + (tt + 1) * TT2],
                      start=True, stop=False)
        for a in range(2 * nJb):
            for tt in range(nT2):
                ins = TE.matmul(pb[2 * pr + tt][:, 0:TT2], wbf[:, a * 128:(a + 1) * 128],
                                fxn_c(a)[:, tt * TT2:(tt + 1) * TT2],
                                start=False, stop=(a == 2 * nJb - 1))
        s_mm.inc(ins, 1)
        mmv[q] = s_mm.v
        if q + 2 < len(passes):
            emit_prep(q + 2)
        s_mm.wait(SC)
        s_mm.wait(VE)
        if pj == 0:
            if q >= 2:
                SC.wait_ge(s_dv.h, s_dv.v)
            for tt in range(nT2):
                ins = SC.activation(f_gev[:, tt * TT2:(tt + 1) * TT2],
                                    pb[2 * pr + tt][:, 0:TT2], AF.Silu)
            s_ac.inc(ins, 1)
        else:
            s_ac.wait(VE)
            s_ld.wait(VE)
            for tt in range(nT2):
                ins = VE.tensor_tensor(f_h[:, tt * TT2:(tt + 1) * TT2],
                                       f_gev[:, tt * TT2:(tt + 1) * TT2],
                                       pb[2 * pr + tt][:, 0:TT2], Alu.mult)
            s_dv.inc(ins, 1)
            s_dv.wait(SY)
            s_ld.inc(SY.dma_start(out=h_s[oc * 128:(oc + 1) * 128, tb:tb + TH2], in_=f_h[:]), 16)
        evd_ac[q] = s_ac.v
        evd_dv[q] = s_dv.v
    bar()

    if stop_after == "P5a":
        return nc, ["out"]

    AR.reset()
    d_pk = AR.t("dpk", (128, nJd * 128), I32)
    d_int = AR.t("dint", (128, nJd * 128), I32)
    d_t = [AR.t("dt", (128, nJd * 128), F32) for _ in range(2)]
    d_zs = AR.t("dzs", (128, nJd * 128), F32)
    d_c4 = AR.t("dc4", (4, nJd * 128), F32)
    d_wbf = AR.t("dwbf", (128, 2 * nJd * 128), BF16)
    NSPD = (nJd * 128 + 511) // 512
    for ec in range(nI):
        s_dv.wait(SY)
        s_dv.wait(GP)
        SC.wait_ge(s_mm.h, s_mm.v)
        s_lg.inc(GP.dma_start(out=d_pk[:].rearrange("p (a o) -> p a o", a=nJd),
                              in_=bass.AP(wdq.tensor, ec * 128, [[D, 128], [128 * D, nJd], [1, 128]])), 16)
        s_la.inc(SC.dma_start(out=d_c4[:].rearrange("p (a o) -> p a o", a=nJd),
                              in_=bass.AP(dz4.tensor, ec * 128, [[nJd * D, 4], [D, nJd], [1, 128]])), 16)
        s_la.wait(TE)
        for b_ in range(NSPD):
            wv = min(512, nJd * 128 - b_ * 512)
            bank = pb[b_ % 6]
            if ec > 0 or b_ >= 6:
                TE.wait_ge(s_dv.h, s_dv.v)
            s_mm.inc(TE.matmul(bank[:, 0:wv], sb_E4[:], d_c4[:, b_ * 512:b_ * 512 + wv],
                               start=True, stop=True), 1)
            s_mm.wait(VE)
            s_dv.inc(VE.tensor_copy(d_zs[:, b_ * 512:b_ * 512 + wv], bank[:, 0:wv]), 1)
        VE.wait_ge(s_lg.h, s_lg.v)
        VE.tensor_single_scalar(d_int[:], d_pk[:], 4, Alu.logical_shift_right)
        s_dv.inc(VE.tensor_sub(d_t[0][:], d_int[:], d_zs[:]), 1)
        VE.tensor_single_scalar(d_int[:], d_pk[:], 15, Alu.bitwise_and)
        s_dv.inc(VE.tensor_sub(d_t[1][:], d_int[:], d_zs[:]), 1)
        VE.wait_ge(s_dv.h, s_dv.v)
        SC.wait_ge(s_dv.h, s_dv.v)
        s_la.inc(SC.dma_start(out=d_c4[:].rearrange("p (a o) -> p a o", a=nJd),
                              in_=bass.AP(ds4.tensor, ec * 128, [[nJd * D, 4], [D, nJd], [1, 128]])), 16)
        s_la.wait(TE)
        TE.wait_ge(s_dv.h, s_dv.v)
        for b_ in range(NSPD):
            wv = min(512, nJd * 128 - b_ * 512)
            bank = pb[b_ % 6]
            if b_ >= 6:
                TE.wait_ge(s_dv.h, s_dv.v)
            s_mm.inc(TE.matmul(bank[:, 0:wv], sb_E4[:], d_c4[:, b_ * 512:b_ * 512 + wv],
                               start=True, stop=True), 1)
            s_mm.wait(VE)
            s_dv.inc(VE.tensor_copy(d_zs[:, b_ * 512:b_ * 512 + wv], bank[:, 0:wv]), 1)
        VE.wait_ge(s_dv.h, s_dv.v)
        if ec == 0 and "dbg1" in dump:
            s_dv.inc(VE.tensor_copy(d_t[1 - 1][:], d_t[0][:]), 1)
            s_dv.wait(SY)
            s_ld.inc(SY.dma_start(out=dbg1, in_=d_t[0][:]), 16)
            s_ld.inc(SY.dma_start(out=dbg2, in_=d_zs[:]), 16)
            s_ld.wait(VE)
        VE.tensor_tensor(d_wbf[:, 0:nJd * 128], d_t[0][:], d_zs[:], Alu.mult)
        s_dv.inc(VE.tensor_tensor(d_wbf[:, nJd * 128:2 * nJd * 128], d_t[1][:], d_zs[:], Alu.mult), 1)
        s_dv.wait(SY)
        dst0 = bass.AP(wd_s.tensor, ec * 128, [[D, 128], [128 * D, nJd], [1, 128]])
        dst1 = bass.AP(wd_s.tensor, nJd * 128 * D + ec * 128, [[D, 128], [128 * D, nJd], [1, 128]])
        s_ld.inc(SY.dma_start(out=dst0, in_=d_wbf[:, 0:nJd * 128].rearrange("p (a o) -> p a o", a=nJd)), 16)
        s_ld.inc(SY.dma_start(out=dst1, in_=d_wbf[:, nJd * 128:2 * nJd * 128].rearrange("p (a o) -> p a o", a=nJd)), 16)
        s_ld.wait(VE)
    bar()

    if stop_after == "P5c":
        return nc, ["out"]

    AR.reset()
    q_h = AR.t("qh", (128, nO * TQ), BF16)
    q_ws = [AR.t("qws", (128, nO * 128), BF16) for _ in range(2)]
    q_x = AR.t("qx", (128, TQ), F32)
    q_o = AR.t("qo", (128, TQ), F32)

    def qh_c(a):
        return q_h[:, a * TQ:(a + 1) * TQ]

    for qq in range(nQ):
        t0 = qq * TQ
        SY.wait_ge(s_mm.h, s_mm.v)
        src = bass.AP(h_s.tensor, t0, [[T, 128], [128 * T, nO], [1, TQ]])
        s_hq.inc(SY.dma_start(out=q_h[:].rearrange("p (a t) -> p a t", a=nO), in_=src), 16)
        ldh = s_hq.v

        def wdslab_load(ec, buf):
            src2 = bass.AP(wd_s.tensor, ec * 128, [[D, 128], [128 * D, nO], [1, 128]])
            wds[ec % 2].inc(GP.dma_start(out=buf[:].rearrange("p (a o) -> p a o", a=nO), in_=src2), 16)
            return wds[ec % 2].v

        GP.wait_ge(s_mm.h, s_mm.v)
        lgv = {0: wdslab_load(0, q_ws[0]), 1: wdslab_load(1, q_ws[1])}
        for ec in range(nI):
            buf = q_ws[ec % 2]
            bank = pb[ec % 2]
            if qq == 0 and ec == 0 and "dbg5" in dump:
                SY.wait_ge(s_lg.h, lgv[0])
                s_ld.inc(SY.dma_start(out=dbg5, in_=buf[:]), 16)
                s_ld.inc(SY.dma_start(out=dbg6, in_=q_h[:]), 16)
            TE.wait_ge(wds[ec % 2].h, lgv[ec])
            TE.wait_ge(s_hq.h, ldh)
            if ec >= 2:
                TE.wait_ge(s_dv.h, s_dv.v)
            for a in range(nO):
                ins = TE.matmul(bank[:, 0:TQ], buf[:, a * 128:(a + 1) * 128],
                                qh_c(a), start=(a == 0), stop=(a == nO - 1))
            s_mm.inc(ins, 1)
            if ec + 2 < nI:
                GP.wait_ge(s_mm.h, s_mm.v)
                lgv[ec + 2] = wdslab_load(ec + 2, buf)
            SY.wait_ge(s_dv.h, s_dv.v)
            s_ld.inc(SY.dma_start(out=q_x[:, 0:TQ], in_=part_s[ec * 128:(ec + 1) * 128, t0:t0 + TQ]), 16)
            s_ld.wait(VE)
            s_mm.wait(VE)
            s_dv.inc(VE.tensor_add(q_o[:], bank[:, 0:TQ], q_x[:, 0:TQ]), 1)
            s_dv.wait(SY)
            s_ld.inc(SY.dma_start(out=out[ec * 128:(ec + 1) * 128, t0:t0 + TQ], in_=q_o[:]), 16)
            s_ld.wait(VE)
    SY.wait_ge(s_ld.h, s_ld.v)
    bar()

    return nc, ["out"]


def kernel(**inputs):
    cfg = make_cfg()
    cons = host_constants(cfg)
    in_maps = [host_core_inputs(cfg, c, inputs, cons) for c in range(8)]
    nc, _ = build_program(cfg)
    res = run_bass_kernel_spmd(nc, in_maps, core_ids=list(range(8)))
    return host_assemble_output(cfg, res.results)


# version 02
# speedup vs baseline: 1.0052x; 1.0052x over previous
import numpy as np
import ml_dtypes
import concourse.bass as bass
import concourse.mybir as mybir
from concourse.bass_utils import run_bass_kernel_spmd
from contextlib import ExitStack

F32 = mybir.dt.float32
BF16 = mybir.dt.bfloat16
I32 = mybir.dt.int32
U8 = mybir.dt.uint8
Alu = mybir.AluOpType
AF = mybir.ActivationFunctionType
BF16NP = ml_dtypes.bfloat16

NTCOL = 145
TBL_W = NTCOL + 1

PATK = ["k1a", "k1b", "k1c", "k2", "k3", "kd8", "ku"]
PATR = ["rh", "r1", "r2", "r3", "rd8"]
PATKS = {
    "k1a": ([[-1, 32]], 31),
    "k1b": ([[-1, 64]], 63),
    "k1c": ([[-1, 96]], 95),
    "k2": ([[0, 8], [-1, 16]], 15),
    "k3": ([[0, 16], [-1, 8]], 7),
    "kd8": ([[0, 16], [-1, 8]], 3),
    "ku": (None, None),
}
PATRS = {
    "rh": ([[1, 128]], 0),
    "r1": ([[0, 4], [1, 32]], 0),
    "r2": ([[0, 4], [1, 32]], -16),
    "r3": ([[0, 8], [1, 16]], -8),
    "rd8": ([[0, 16], [1, 8]], -4),
}


def pat_width(dims):
    w = 1
    for _, c in dims:
        w *= c
    return w


KW = sum(pat_width(PATKS[n][0]) if n != "ku" else 128 for n in PATK)
RW = sum(pat_width(PATRS[n][0]) for n in PATR)


def make_cfg(D=4096, T=2048, I=11008, B=4):
    c = {"D": D, "T": T, "I": I, "B": B}
    c["H"] = D // 64
    c["nI"] = D // 128
    c["nJb"] = (D // 2) // 128
    c["GI"] = D // 64
    c["nO"] = I // 128
    c["nJd"] = (I // 2) // 128
    c["GD"] = I // 64
    c["nC"] = D // 128
    c["nN"] = T // 128
    c["TT"] = min(512, T)
    c["nTT"] = T // c["TT"]
    c["TH2"] = T // 2
    c["nT2"] = c["TH2"] // min(512, c["TH2"])
    c["TT2"] = min(512, c["TH2"])
    c["TQ"] = min(512, T)
    c["nQ"] = T // c["TQ"]
    assert D % 256 == 0 and T % 256 == 0 and I % 256 == 0
    assert c["GI"] <= 128
    return c



def pi_perm(n):
    return np.concatenate([np.arange(0, n, 2), np.arange(1, n, 2)])


def host_masks():
    j = np.arange(128)[:, None]
    i = np.arange(128)[None, :]
    m1 = (j // 32 < i // 32)
    m2 = (i // 32 == j // 32) & (j % 32 < 16) & (i % 32 >= 16)
    m3 = (i // 16 == j // 16) & (j % 16 < 8) & (i % 16 >= 8)
    md8 = (i // 8 == j // 8) & (j < i)
    mdiag = (i == j)
    return [np.tile(m.astype(np.uint8), (1, 2)) for m in (m1, m2, m3, md8, mdiag)]


def host_constants(cfg):
    GI, nI, nJb, nN = cfg["GI"], cfg["nI"], cfg["nJb"], cfg["nN"]
    cons = {}
    for nm, m in zip(("c_m1", "c_m2", "c_m3", "c_md8", "c_mdiag"), host_masks()):
        cons[nm] = m
    E4 = np.zeros((4, 128), np.float32)
    for gp in range(4):
        E4[gp, gp * 32:(gp + 1) * 32] = 1.0
    cons["c_E4"] = E4
    p = np.arange(128)
    E16 = np.zeros((nI, 128, GI), np.float32)
    for c in range(nI):
        E16[c, p, 4 * (c % nJb) + p // 32] = 1.0
    cons["c_E16"] = E16.astype(BF16NP)
    E2h = np.zeros((128, 2), np.float32)
    E2h[p, p // 64] = 1.0
    cons["c_E2h"] = E2h.astype(BF16NP)
    F2 = np.zeros((2, 128), np.float32)
    F2[p // 64, p] = 1.0
    cons["c_F2"] = F2
    cons["c_ones1"] = np.ones((1, 128), np.float32)
    cons["c_onesD"] = np.ones((128, 1), np.float32)
    cons["c_Prow"] = np.tile(-np.arange(-16, 129, dtype=np.float32)[None, :], (128, 1))
    cons["c_Pn"] = np.tile(-128.0 * np.arange(nN, dtype=np.float32)[None, :], (128, 1))
    cons["c_idn"] = np.eye(128, dtype=np.float32)
    return cons


def host_core_inputs(cfg, core, inputs, cons):
    D, T, I, nC = cfg["D"], cfg["T"], cfg["I"], cfg["nC"]
    b, hf = core // 2, core % 2
    pi = pi_perm(D)
    pio = pi_perm(I)
    x = np.asarray(inputs["L0_attn_hidden"])
    m = dict(cons)
    xc = np.zeros((D, T + 1), np.float32)
    s0 = hf * T
    xc[:, 1:] = np.ascontiguousarray(x[b, s0:s0 + T].T)[pi]
    if hf == 1:
        xc[:, 0] = x[b, s0 - 1][pi]
    m["xT"] = np.ascontiguousarray(xc)
    for nm, w in (("wkT", "L0_attn_w_key"), ("wvT", "L0_attn_w_value"),
                  ("wrT", "L0_attn_w_receptance"), ("wgT", "L0_attn_w_gate")):
        m[nm] = np.ascontiguousarray(np.asarray(inputs[w]).T[pi])
    m["woT"] = np.ascontiguousarray(np.asarray(inputs["L0_attn_w_output"]).T[:, pi])
    for nm, v in (("tmk", "L0_attn_time_mix_key"), ("tmv", "L0_attn_time_mix_value"),
                  ("tmr", "L0_attn_time_mix_receptance"), ("tmg", "L0_attn_time_mix_gate")):
        m[nm] = np.ascontiguousarray(np.asarray(inputs[v])[pi].reshape(nC, 128).T)
    dec = np.asarray(inputs["L0_attn_time_decay"]).reshape(-1)
    u = np.asarray(inputs["L0_attn_time_faaaa"]).reshape(-1)
    m["dec64"] = np.ascontiguousarray(dec.reshape(nC, 2, 64).transpose(2, 0, 1).reshape(64, 2 * nC))
    m["u64"] = np.ascontiguousarray(u.reshape(nC, 2, 64).transpose(2, 0, 1).reshape(64, 2 * nC))
    m["dec128"] = np.ascontiguousarray(dec.reshape(nC, 128).T)
    m["gnw"] = np.ascontiguousarray(np.asarray(inputs["L0_attn_gn_weight"]).reshape(nC, 128).T)
    m["gnb"] = np.ascontiguousarray(np.asarray(inputs["L0_attn_gn_bias"]).reshape(nC, 128).T)
    m["wgq"] = np.ascontiguousarray(np.asarray(inputs["L0_ffn_gate_Wq"])[pio].T)
    m["wuq"] = np.ascontiguousarray(np.asarray(inputs["L0_ffn_up_Wq"])[pio].T)
    nJb, nJd = cfg["nJb"], cfg["nJd"]
    for pre, snm, znm in (("g", "L0_ffn_gate_scale", "L0_ffn_gate_zero"),
                          ("u", "L0_ffn_up_scale", "L0_ffn_up_zero")):
        sT = np.ascontiguousarray(np.asarray(inputs[snm])[pio].T)
        zT = np.ascontiguousarray(np.asarray(inputs[znm])[pio].T)
        m[pre + "sT"] = sT
        m[pre + "zT"] = zT
        m[pre + "s4"] = np.ascontiguousarray(sT.reshape(nJb, 4, I).transpose(1, 0, 2))
    m["wdq"] = np.ascontiguousarray(np.asarray(inputs["L0_ffn_down_Wq"])[pi].T)
    sTd = np.ascontiguousarray(np.asarray(inputs["L0_ffn_down_scale"])[pi].T)
    zTd = np.ascontiguousarray(np.asarray(inputs["L0_ffn_down_zero"])[pi].T)
    m["ds4"] = np.ascontiguousarray(sTd.reshape(nJd, 4, D).transpose(1, 0, 2))
    m["dz4"] = np.ascontiguousarray(zTd.reshape(nJd, 4, D).transpose(1, 0, 2))
    m["sel64"] = np.full((128, 1), 1.0 if hf == 1 else 0.0, np.float32)
    m["selinv"] = np.full((128, 1), 0.0 if hf == 1 else 1.0, np.float32)
    return m


def host_assemble_output(cfg, results):
    D, T, B = cfg["D"], cfg["T"], cfg["B"]
    pi = pi_perm(D)
    inv = np.empty(D, np.int64)
    inv[pi] = np.arange(D)
    out = np.empty((B, 2 * T, D), np.float32)
    for core in range(2 * B):
        b, hf = core // 2, core % 2
        o = np.asarray(results[core]["out"])
        out[b, hf * T:(hf + 1) * T] = o[inv].T
    return out



class Sem:
    def __init__(self, nc, name):
        self.h = nc.alloc_semaphore(name)
        self.v = 0

    def inc(self, instr, n=1):
        instr.then_inc(self.h, n)
        self.v += n
        return instr

    def wait(self, eng, v=None):
        eng.wait_ge(self.h, self.v if v is None else v)


class Arena:
    def __init__(self, nc, base, size):
        self.nc, self.base, self.size = nc, base, size
        self.off = 0
        self.uid = 0

    def reset(self):
        self.off = 0

    def t(self, name, shape, dtype):
        per = int(np.prod(shape[1:])) * mybir.dt.size(dtype)
        off = (self.off + 31) & ~31
        assert off + per <= self.size, f"arena overflow at {name}: {off}+{per}>{self.size}"
        self.off = off + per
        self.uid += 1
        return self.nc.alloc_sbuf_tensor_at(f"{name}_{self.uid}", list(shape), dtype,
                                            offset=self.base + off)


def build_program(cfg, dump=(), stop_after=None):
    nc = bass.Bass("TRN2", target_bir_lowering=False, debug=False,
                   detect_race_conditions=False)
    D, T, I = cfg["D"], cfg["T"], cfg["I"]
    nI, nJb, GI = cfg["nI"], cfg["nJb"], cfg["GI"]
    nO, nJd = cfg["nO"], cfg["nJd"]
    nC, nN, TT, nTT = cfg["nC"], cfg["nN"], cfg["TT"], cfg["nTT"]
    TH2, nT2, TT2 = cfg["TH2"], cfg["nT2"], cfg["TT2"]
    TQ, nQ = cfg["TQ"], cfg["nQ"]
    dump = set(dump)

    def dram(name, shape, dtype, kind="Internal"):
        if kind == "Internal" and name in dump:
            kind = "ExternalOutput"
        return nc.dram_tensor(name, list(shape), dtype, kind=kind).ap()

    xT = dram("xT", (D, T + 1), F32, "ExternalInput")
    wT = {p: dram(nm, (D, D), F32, "ExternalInput")
          for p, nm in (("k", "wkT"), ("v", "wvT"), ("r", "wrT"), ("g", "wgT"))}
    woT = dram("woT", (D, D), F32, "ExternalInput")
    tmix = {p: dram(nm, (128, nC), F32, "ExternalInput")
            for p, nm in (("k", "tmk"), ("v", "tmv"), ("r", "tmr"), ("g", "tmg"))}
    dec64d = dram("dec64", (64, 2 * nC), F32, "ExternalInput")
    u64d = dram("u64", (64, 2 * nC), F32, "ExternalInput")
    dec128d = dram("dec128", (128, nC), F32, "ExternalInput")
    gnwd = dram("gnw", (128, nC), F32, "ExternalInput")
    gnbd = dram("gnb", (128, nC), F32, "ExternalInput")
    wgq = dram("wgq", (D // 2, I), I32, "ExternalInput")
    wuq = dram("wuq", (D // 2, I), I32, "ExternalInput")
    wdq = dram("wdq", (I // 2, D), I32, "ExternalInput")
    gsT = dram("gsT", (GI, I), F32, "ExternalInput")
    gzT = dram("gzT", (GI, I), F32, "ExternalInput")
    gs4 = dram("gs4", (4, nJb, I), F32, "ExternalInput")
    usT = dram("usT", (GI, I), F32, "ExternalInput")
    uzT = dram("uzT", (GI, I), F32, "ExternalInput")
    us4 = dram("us4", (4, nJb, I), F32, "ExternalInput")
    ds4 = dram("ds4", (4, nJd, D), F32, "ExternalInput")
    dz4 = dram("dz4", (4, nJd, D), F32, "ExternalInput")
    sel64d = dram("sel64", (128, 1), F32, "ExternalInput")
    selinvd = dram("selinv", (128, 1), F32, "ExternalInput")
    c_m = [dram(n, (128, 256), U8, "ExternalInput")
           for n in ("c_m1", "c_m2", "c_m3", "c_md8", "c_mdiag")]
    c_E4 = dram("c_E4", (4, 128), F32, "ExternalInput")
    c_E16 = dram("c_E16", (nI, 128, GI), BF16, "ExternalInput")
    c_E2h = dram("c_E2h", (128, 2), BF16, "ExternalInput")
    c_F2 = dram("c_F2", (2, 128), F32, "ExternalInput")
    c_ones1 = dram("c_ones1", (1, 128), F32, "ExternalInput")
    c_onesD = dram("c_onesD", (128, 1), F32, "ExternalInput")
    c_Prow = dram("c_Prow", (128, NTCOL), F32, "ExternalInput")
    c_Pn = dram("c_Pn", (128, nN), F32, "ExternalInput")
    c_idn = dram("c_idn", (128, 128), F32, "ExternalInput")

    xn_s = dram("xn_s", (D, T + 1), BF16)
    k_s = dram("k_s", (D, T), BF16)
    v_s = dram("v_s", (D, T), BF16)
    r_s = dram("r_s", (D, T), BF16)
    g_s = dram("g_s", (D, T), BF16)
    y_s = dram("y_s", (D, T), BF16)
    part_s = dram("part_s", (D, T), F32)
    h_s = dram("h_s", (I, T), BF16)
    wd_s = dram("wd_s", (I, D), BF16)
    cc_in = dram("cc_in", (D, 64), F32)
    cc_out = dram("cc_out", (D, 64), F32)
    out = dram("out", (D, T), F32, "ExternalOutput")
    dbg1 = dram("dbg1", (128, nJd * 128), F32)
    dbg2 = dram("dbg2", (128, nJd * 128), F32)
    dbg3 = dram("dbg3", (128, nI * 128), BF16)
    dbg4 = dram("dbg4", (128, nI * T), BF16)
    dbg5 = dram("dbg5", (128, nO * 128), BF16)
    dbg6 = dram("dbg6", (128, nO * TQ), BF16)

    sb = lambda name, shape, dtype: nc.alloc_sbuf_tensor(name, list(shape), dtype)

    sb_tmix = {p: sb("s_tm_" + p, (128, nC), F32) for p in "kvrg"}
    sb_gnw = sb("s_gnw", (128, nC), F32)
    sb_gnb = sb("s_gnb", (128, nC), F32)
    sb_sel64 = sb("s_sel64", (128, 1), F32)
    sb_selinv = sb("s_selinv", (128, 1), F32)
    sb_m = [sb("s_m%d" % i, (128, 256), U8) for i in range(5)]
    sb_E4 = sb("s_E4", (4, 128), F32)
    sb_E16 = sb("s_E16", (128, nI * GI), BF16)
    sb_E2h = sb("s_E2h", (128, 2), BF16)
    sb_F2 = sb("s_F2", (2, 128), F32)
    sb_ones1 = sb("s_ones1", (1, 128), F32)
    sb_onesD = sb("s_onesD", (128, 1), F32)
    sb_Prow = sb("s_Prow", (128, NTCOL), F32)
    sb_Pn = sb("s_Pn", (128, nN), F32)
    sb_id = sb("s_id", (128, 128), BF16)
    sb_zero = sb("s_zero", (128, 256), F32)
    sb_eps = sb("s_eps", (128, 1), F32)
    sb_sgrp = sb("s_sgrp", (GI, T), BF16)
    sb_dec64 = sb("s_dec64", (64, 2 * nC), F32)
    sb_u64 = sb("s_u64", (64, 2 * nC), F32)
    sb_dec128 = sb("s_dec128", (128, nC), F32)

    pb = [nc.alloc_psum_tensor("pb%d" % i, [128, 512], F32) for i in range(7)]
    pt = nc.alloc_psum_tensor("pt", [128, 1024], BF16)

    rem = nc.sbuf_bytes_remaining
    ARENA = rem - 2048
    arena_t = nc.alloc_sbuf_tensor("ARENA", [128, ARENA], U8)
    abase = nc.lookup_mloc(arena_t).addr
    AR = Arena(nc, abase, ARENA)

    s_ld = Sem(nc, "ld")
    s_la = Sem(nc, "la")
    s_lg = Sem(nc, "lg")
    s_mm = Sem(nc, "mm")
    s_dv = Sem(nc, "dv")
    s_ac = Sem(nc, "ac")
    s_cc = Sem(nc, "cc")
    sx = [Sem(nc, "sx%d" % i) for i in range(2)]
    wsl = [Sem(nc, "wsl%d" % i) for i in range(2)]
    pks = [Sem(nc, "pks%d" % i) for i in range(2)]
    szs = [Sem(nc, "szs%d" % i) for i in range(2)]
    scs = [Sem(nc, "scs%d" % i) for i in range(2)]
    s_xh = Sem(nc, "xh")
    s_hq = Sem(nc, "hq")
    wds = [Sem(nc, "wds%d" % i) for i in range(2)]

    SY, GP, SC, TE, VE = nc.sync, nc.gpsimd, nc.scalar, nc.tensor, nc.vector
    DMA_SY = [s_ld, s_xh, s_hq] + sx + szs
    DMA_GP = [s_lg] + wsl + pks + wds
    DMA_SC = [s_la] + scs

    def bar():
        for s in DMA_SY:
            SY.wait_ge(s.h, s.v)
        for s in DMA_SC:
            SC.wait_ge(s.h, s.v)
        for s in DMA_GP:
            GP.wait_ge(s.h, s.v)
        nc.all_engine_barrier()

    scope_ids = {}
    def scope(nm):
        scope_ids[nm] = nc.enter_named_scope(nm, False)[0]
    def escope(nm):
        nc.leave_named_scope(nm, scope_ids[nm], False)
    scope("P0")
    for dst, src in ((sb_sel64, sel64d), (sb_selinv, selinvd), (sb_E4, c_E4),
                     (sb_E2h, c_E2h), (sb_F2, c_F2), (sb_ones1, c_ones1),
                     (sb_onesD, c_onesD), (sb_Prow, c_Prow), (sb_Pn, c_Pn),
                     (sb_gnw, gnwd), (sb_gnb, gnbd), (sb_dec64, dec64d),
                     (sb_u64, u64d), (sb_dec128, dec128d)):
        s_ld.inc(SY.dma_start(out=dst[:], in_=src), 16)
    for i in range(5):
        s_ld.inc(SY.dma_start(out=sb_m[i][:], in_=c_m[i]), 16)
    s_ld.inc(SY.dma_start(out=sb_E16[:], in_=bass.AP(c_E16.tensor, 0, [[GI, 128], [128 * GI, nI], [1, GI]])), 16)
    for p in "kvrg":
        s_ld.inc(SY.dma_start(out=sb_tmix[p][:], in_=tmix[p]), 16)
    s_lg.inc(GP.dma_start(out=sb_id[:], in_=c_idn), 16)
    s_ld.wait(VE)
    VE.memset(sb_zero[:], 0.0)
    VE.memset(sb_eps[:], 1e-5)
    for _p in pb:
        VE.memset(_p[:], 0.0)
    bar()

    escope("P0")
    scope("P1")
    AR.reset()
    p1_x = [AR.t("p1x", (128, TT), F32) for _ in range(2)]
    p1_sq = [AR.t("p1sq", (128, TT), F32) for _ in range(2)]
    p1_row = AR.t("p1row", (1, 4 * TT), F32)
    p1_xn = [AR.t("p1xn", (128, TT), BF16) for _ in range(2)]
    widths = [1] + [TT] * nTT
    offs = np.cumsum([0] + widths).tolist()
    for ti, (o0, w) in enumerate(zip(offs[:-1], widths)):
        SY.wait_ge(s_dv.h, s_dv.v)
        mmv1 = []
        for ic in range(nI):
            buf = p1_x[ic % 2]
            sqb = p1_sq[ic % 2]
            if ic >= 2:
                SY.wait_ge(s_mm.h, mmv1[ic - 2])
                SC.wait_ge(s_mm.h, mmv1[ic - 2])
            sx[ic % 2].inc(SY.dma_start(out=buf[:, 0:w], in_=xT[128 * ic:128 * (ic + 1), o0:o0 + w],
                                        allow_slow_non_contiguous=(w == 1)), 16)
            sx[ic % 2].wait(SC)
            s_ac.inc(SC.activation(sqb[:, 0:w], buf[:, 0:w], AF.Square), 1)
            sx[ic % 2].wait(TE)
            if ic == 0:
                s_dv.wait(TE)
            TE.matmul(pb[4][0:1, 0:w], sb_onesD[:], buf[:, 0:w],
                      start=(ic == 0), stop=(ic == nI - 1))
            s_ac.wait(TE)
            s_mm.inc(TE.matmul(pb[4][32:33, 0:w], sb_onesD[:], sqb[:, 0:w],
                               start=(ic == 0), stop=(ic == nI - 1)), 1)
            mmv1.append(s_mm.v)
        s_mm.wait(VE)
        mrow = p1_row[0:1, 0:w]
        rrow = p1_row[0:1, TT:TT + w]
        trow = p1_row[0:1, 2 * TT:2 * TT + w]
        qrow = p1_row[0:1, 3 * TT:3 * TT + w]
        s_dv.inc(VE.tensor_scalar_mul(mrow, pb[4][0:1, 0:w], 1.0 / D), 1)
        VE.wait_ge(s_dv.h, s_dv.v)
        s_dv.inc(VE.tensor_scalar_mul(qrow, pb[4][32:33, 0:w], 1.0 / D), 1)
        VE.wait_ge(s_dv.h, s_dv.v)
        s_dv.inc(VE.tensor_tensor(trow, mrow, mrow, Alu.mult), 1)
        VE.wait_ge(s_dv.h, s_dv.v)
        s_dv.inc(VE.tensor_sub(trow, qrow, trow), 1)
        VE.wait_ge(s_dv.h, s_dv.v)
        s_dv.wait(SC)
        s_ac.inc(SC.activation(qrow, trow, AF.Sqrt, bias=sb_eps[0:1, 0:1]), 1)
        s_ac.wait(VE)
        s_dv.inc(VE.reciprocal(rrow, qrow), 1)
        VE.wait_ge(s_dv.h, s_dv.v)
        s_dv.wait(TE)
        TE.matmul(pb[5][:, 0:w], sb_ones1[:], mrow, start=True, stop=True)
        s_mm.inc(TE.matmul(pb[6][:, 0:w], sb_ones1[:], rrow, start=True, stop=True), 1)
        SY.wait_ge(s_mm.h, s_mm.v)
        dvv2 = []
        e16v = []
        for ic in range(nI):
            buf = p1_x[ic % 2]
            xnb = p1_xn[ic % 2]
            if ic >= 2:
                SY.wait_ge(s_dv.h, dvv2[ic - 2])
            sx[ic % 2].inc(SY.dma_start(out=buf[:, 0:w], in_=xT[128 * ic:128 * (ic + 1), o0:o0 + w],
                                        allow_slow_non_contiguous=(w == 1)), 16)
            sx[ic % 2].wait(VE)
            s_mm.wait(VE)
            if ic >= 2:
                VE.wait_ge(s_ld.h, s_ld.v)
                if ti > 0:
                    VE.wait_ge(s_mm.h, e16v[ic - 2])
            VE.tensor_sub(buf[:, 0:w], buf[:, 0:w], pb[5][:, 0:w])
            s_dv.inc(VE.tensor_tensor(xnb[:, 0:w], buf[:, 0:w], pb[6][:, 0:w], Alu.mult), 1)
            dvv2.append(s_dv.v)
            s_dv.wait(SY)
            s_ld.inc(SY.dma_start(out=xn_s[128 * ic:128 * (ic + 1), o0:o0 + w], in_=xnb[:, 0:w],
                                  allow_slow_non_contiguous=(w == 1)), 16)
            if ti > 0:
                s_dv.wait(TE)
                s_mm.inc(TE.matmul(pb[4][64:64 + GI, 0:w], sb_E16[:, ic * GI:(ic + 1) * GI],
                                   xnb[:, 0:w], start=(ic == 0), stop=(ic == nI - 1)), 1)
                e16v.append(s_mm.v)
        if ti > 0:
            s_mm.wait(VE)
            s_dv.inc(VE.tensor_copy(sb_sgrp[:, o0 - 1:o0 - 1 + w], pb[4][64:64 + GI, 0:w]), 1)
    bar()

    if stop_after == "P1":
        return nc, ["out"]

    escope("P1")
    scope("P2")
    AR.reset()
    bigA = AR.t("p2mix", (128, nI * T), BF16)
    p2_xn = AR.t("p2xn", (128, T + 1), BF16)
    p2_d = AR.t("p2d", (128, T), BF16)
    p2_ws = [AR.t("p2ws", (128, nI * 128), BF16) for _ in range(2)]
    p2_ev = AR.t("p2ev", (128, T), BF16)

    def bigA_c(c):
        return bigA[:, c * T:(c + 1) * T]

    def wslab_load(w_ap, oc, buf):
        src = bass.AP(w_ap.tensor, oc * 128, [[D, 128], [128 * D, nI], [1, 128]])
        wsl[oc % 2].inc(GP.dma_start(out=buf[:].rearrange("p (a o) -> p a o", a=nI), in_=src), 16)
        return wsl[oc % 2].v

    for pi_, p in enumerate("kvrg"):
        for ic in range(nI):
            s_dv.wait(SY)
            s_ld.inc(SY.dma_start(out=p2_xn[:], in_=xn_s[128 * ic:(128 * ic + 128), :]), 16)
            s_ld.wait(VE)
            if pi_ > 0 or ic >= 1:
                TE.wait_ge(s_mm.h, s_mm.v)
                VE.wait_ge(s_mm.h, s_mm.v)
            VE.tensor_sub(p2_d[:], p2_xn[:, 1:T + 1], p2_xn[:, 0:T])
            s_dv.inc(VE.scalar_tensor_tensor(bigA_c(ic), p2_d[:], sb_tmix[p][:, ic:ic + 1],
                                             p2_xn[:, 0:T], Alu.mult, Alu.add), 1)
        lg_vals = {}
        GP.wait_ge(s_mm.h, s_mm.v)
        lg_vals[0] = wslab_load(wT[p], 0, p2_ws[0])
        lg_vals[1] = wslab_load(wT[p], 1, p2_ws[1])
        dst = {"k": k_s, "v": v_s, "r": r_s, "g": g_s}[p]
        for oc in range(nC):
            buf = p2_ws[oc % 2]
            bs = 0
            if p == "k" and oc == nC - 1 and "dbg3" in dump:
                SY.wait_ge(s_lg.h, lg_vals[oc])
                SY.wait_ge(s_dv.h, s_dv.v)
                s_ld.inc(SY.dma_start(out=dbg3, in_=buf[:]), 16)
                s_ld.inc(SY.dma_start(out=dbg4, in_=bigA[:]), 16)
            TE.wait_ge(wsl[oc % 2].h, lg_vals[oc])
            TE.wait_ge(s_dv.h, s_dv.v)
            if p == "g":
                TE.wait_ge(s_ac.h, s_ac.v)
            for a in range(nI):
                for tt in range(nTT):
                    ins = TE.matmul(pb[bs + tt][:, 0:TT], buf[:, a * 128:(a + 1) * 128],
                                    bigA_c(a)[:, tt * TT:(tt + 1) * TT],
                                    start=(a == 0), stop=(a == nI - 1))
            s_mm.inc(ins, 1)
            if oc + 2 < nC:
                GP.wait_ge(s_mm.h, s_mm.v)
                lg_vals[oc + 2] = wslab_load(wT[p], oc + 2, buf)
            s_mm.wait(VE)
            s_mm.wait(SC)
            s_ld.wait(VE)
            s_ld.wait(SC)
            for tt in range(nTT):
                if p == "g":
                    ins = SC.activation(p2_ev[:, tt * TT:(tt + 1) * TT], pb[bs + tt][:, 0:TT], AF.Silu)
                else:
                    ins = VE.tensor_copy(p2_ev[:, tt * TT:(tt + 1) * TT], pb[bs + tt][:, 0:TT])
            (s_ac if p == "g" else s_dv).inc(ins, 1)
            (s_ac if p == "g" else s_dv).wait(SY)
            s_ld.inc(SY.dma_start(out=dst[oc * 128:(oc + 1) * 128, :], in_=p2_ev[:]), 16)
    bar()

    if stop_after == "P2":
        return nc, ["out"]

    escope("P2")
    scope("P3")
    AR.reset()
    p3_k64 = AR.t("k64", (64, 2 * T), BF16)
    p3_r64 = AR.t("r64", (64, 2 * T), BF16)
    p3_vfm = AR.t("vfm", (128, T), BF16)
    p3_kfm = AR.t("kfm", (128, T), BF16)
    p3_lw64 = AR.t("lw64", (64, 2), F32)
    p3_lw128 = AR.t("lw128", (128, 1), F32)
    p3_tmp64 = AR.t("tmp64", (64, 2 * NTCOL), F32)
    p3_T64 = AR.t("T64", (64, 2 * TBL_W), F32)
    p3_tmp128 = AR.t("tmp128", (128, NTCOL), F32)
    p3_T128 = AR.t("T128", (128, NTCOL), F32)
    p3_patkf = AR.t("patkf", (64, 2 * KW), F32)
    p3_patrf = AR.t("patrf", (64, 2 * RW), F32)
    p3_patk = AR.t("patk", (64, 2 * KW), BF16)
    p3_patr = AR.t("patr", (64, 2 * RW), BF16)
    p3_sck = AR.t("sck", (64, 2 * nN * KW), BF16)
    p3_scr = AR.t("scr", (64, 2 * nN * RW), BF16)
    p3_khat = AR.t("khat", (128, T), BF16)
    p3_vtok = AR.t("vtok", (128, 256), BF16)
    p3_A2F = AR.t("A2F", (128, 256), F32)
    p3_A2 = AR.t("A2", (128, 256), BF16)
    p3_SA = AR.t("SA", (64, 64), F32)
    p3_SB = AR.t("SB", (64, 64), F32)
    p3_SbfA = AR.t("SbfA", (64, 64), BF16)
    p3_SbfB = AR.t("SbfB", (64, 64), BF16)
    p3_w128 = AR.t("w128", (64, 2), F32)
    p3_yc = AR.t("yc", (128, T), BF16)
    p3_sfin = AR.t("sfin", (128, 64), F32)

    def t64ap(dims, p0):
        return bass.AP(p3_T64, p0 + 16, [[2 * TBL_W, 64], [TBL_W, 2]] + dims)

    def dst_dims(dims):
        o, acc = [], 1
        for s_, c_ in reversed(dims):
            o.append([acc, c_])
            acc *= c_
        return list(reversed(o))

    patk_off = {}
    off = 0
    for nm in PATK:
        w = 128 if nm == "ku" else pat_width(PATKS[nm][0])
        patk_off[nm] = (off, w)
        off += 2 * w
    patr_off = {}
    off = 0
    for nm in PATR:
        w = pat_width(PATRS[nm][0])
        patr_off[nm] = (off, w)
        off += 2 * w

    def sck_ap(nm, h, n, c0=0, w_=None):
        po, w = patk_off[nm]
        w_ = w if w_ is None else w_
        return bass.AP(p3_sck, po * nN + (h * nN + n) * w + c0, [[2 * nN * KW, 64], [1, w_]])

    def scr_ap(nm, h, n, c0=0, w_=None):
        po, w = patr_off[nm]
        w_ = w if w_ is None else w_
        return bass.AP(p3_scr, po * nN + (h * nN + n) * w + c0, [[2 * nN * RW, 64], [1, w_]])

    def r64_ap(h, n):
        return bass.AP(p3_r64, h * T + n * 128, [[2 * T, 64], [1, 128]])

    T64v = p3_T64[:].rearrange("d (h c) -> d h c", h=2, c=TBL_W)
    tmp64v = p3_tmp64[:].rearrange("d (h c) -> d h c", h=2)

    for c in range(nC):
        s_dv.wait(SY)
        for src, dstt in ((k_s, p3_k64), (r_s, p3_r64)):
            ap = bass.AP(src.tensor, c * 128 * T, [[T, 64], [64 * T, 2], [1, T]])
            s_ld.inc(SY.dma_start(out=dstt[:].rearrange("d (h t) -> d h t", h=2), in_=ap), 16)
        s_ld.inc(SY.dma_start(out=p3_vfm[:], in_=v_s[c * 128:(c + 1) * 128, :]), 16)
        s_ld.inc(SY.dma_start(out=p3_kfm[:], in_=k_s[c * 128:(c + 1) * 128, :]), 16)
        SC.wait_ge(s_dv.h, s_dv.v)
        s_ac.inc(SC.activation(p3_lw64[:], sb_dec64[:, 2 * c:2 * c + 2], AF.Exp), 1)
        s_ac.inc(SC.activation(p3_lw128[:], sb_dec128[:, c:c + 1], AF.Exp), 1)
        s_ac.wait(VE)
        VE.tensor_tensor(tmp64v, bass.AP(sb_Prow, 0, [[NTCOL, 64], [0, 2], [1, NTCOL]]),
                         bass.AP(p3_lw64, 0, [[2, 64], [1, 2], [0, NTCOL]]), Alu.mult)
        VE.tensor_tensor(p3_tmp128[:], sb_Prow[:],
                         bass.AP(p3_lw128, 0, [[1, 128], [0, NTCOL]]), Alu.mult)
        s_dv.inc(VE.tensor_copy(T64v[:, :, NTCOL:TBL_W],
                                bass.AP(sb_u64, 2 * c, [[2 * nC, 64], [1, 2], [0, 1]])), 1)
        s_dv.wait(SC)
        s_ac.inc(SC.activation(T64v[:, :, 0:NTCOL], tmp64v, AF.Exp), 1)
        s_ac.inc(SC.activation(p3_T128[:], p3_tmp128[:], AF.Exp), 1)
        s_ac.wait(VE)
        VE.tensor_scalar_min(T64v[:, :, 0:NTCOL], T64v[:, :, 0:NTCOL], 1e30)
        VE.tensor_scalar_min(p3_T128[:], p3_T128[:], 1e30)
        s_dv.inc(VE.tensor_copy(p3_w128[:], bass.AP(p3_T64, 144, [[2 * TBL_W, 64], [TBL_W, 2]])), 1)
        VE.wait_ge(s_dv.h, s_dv.v)
        off = 0
        for nm in PATK:
            if nm == "ku":
                w, dims = 128, [[0, 128]]
                srcp = bass.AP(p3_T64, NTCOL, [[2 * TBL_W, 64], [TBL_W, 2]] + dims)
            else:
                dims, p0 = PATKS[nm]
                w = pat_width(dims)
                srcp = t64ap(dims, p0)
            VE.tensor_copy(bass.AP(p3_patkf, off, [[2 * KW, 64], [w, 2]] + dst_dims(dims)), srcp)
            off += 2 * w
        off = 0
        for nm in PATR:
            dims, p0 = PATRS[nm]
            w = pat_width(dims)
            VE.tensor_copy(bass.AP(p3_patrf, off, [[2 * RW, 64], [w, 2]] + dst_dims(dims)),
                           t64ap(dims, p0))
            off += 2 * w
        VE.tensor_copy(p3_patk[:], p3_patkf[:])
        s_dv.inc(VE.tensor_copy(p3_patr[:], p3_patrf[:]), 1)
        VE.wait_ge(s_dv.h, s_dv.v)
        s_ld.wait(VE)
        for PAT, offs_, dstt, dstW, src64, patt, patW in (
                (PATK, patk_off, p3_sck, KW, p3_k64, p3_patk, KW),
                (PATR, patr_off, p3_scr, RW, p3_r64, p3_patr, RW)):
            for nm in PAT:
                po, w = offs_[nm]
                o_ap = bass.AP(dstt, po * nN, [[2 * nN * dstW, 64], [nN * w, 2], [w, nN], [1, w]])
                i0 = bass.AP(src64, 0, [[2 * T, 64], [T, 2], [128, nN], [1, w]])
                i1 = bass.AP(patt, po, [[2 * patW, 64], [w, 2], [0, nN], [1, w]])
                VE.tensor_tensor(o_ap, i0, i1, Alu.mult)
        VE.tensor_tensor(p3_khat[:].rearrange("p (n j) -> p n j", n=nN),
                         p3_kfm[:].rearrange("p (n j) -> p n j", n=nN),
                         bass.AP(p3_T128, 127 + 16, [[NTCOL, 128], [0, nN], [-1, 128]]),
                         Alu.mult)
        VE.memset(p3_SA[:], 0.0)
        VE.memset(p3_SB[:], 0.0)
        VE.memset(p3_SbfA[:], 0.0)
        s_dv.inc(VE.memset(p3_SbfB[:], 0.0), 1)
        for n in range(nN):
            TE.wait_ge(s_dv.h, s_dv.v)
            TE.transpose(pt[:, 0:128], p3_vfm[:, n * 128:(n + 1) * 128], sb_id[:])
            s_mm.inc(TE.transpose(pt[:, 128:256], p3_khat[:, n * 128:(n + 1) * 128], sb_id[:]), 1)
            s_mm.wait(VE)
            s_dv.inc(VE.tensor_copy(p3_vtok[:], pt[:, 0:256]), 1)
            TE.wait_ge(s_dv.h, s_dv.v)
            for h in range(2):
                c0 = h * 128
                TE.matmul(pb[3][64 * h:64 * h + 64, 0:128], p3_SbfA[:] if h == 0 else p3_SbfB[:],
                          scr_ap("rh", h, n), start=True, stop=False, skip_group_check=True)
                TE.matmul(pb[0][0:32, c0 + 32:c0 + 64], sck_ap("k1a", h, n), scr_ap("r1", h, n, 32, 32), start=True, stop=True)
                TE.matmul(pb[0][0:64, c0 + 64:c0 + 96], sck_ap("k1b", h, n), scr_ap("r1", h, n, 64, 32), start=True, stop=True)
                TE.matmul(pb[0][0:96, c0 + 96:c0 + 128], sck_ap("k1c", h, n), scr_ap("r1", h, n, 96, 32), start=True, stop=True)
                TE.matmul(pb[0][:, 256 + c0:256 + c0 + 128], sck_ap("k2", h, n, w_=128), scr_ap("r2", h, n, w_=128), start=True, stop=True)
                TE.matmul(pb[1][:, c0:c0 + 128], sck_ap("k3", h, n, w_=128), scr_ap("r3", h, n, w_=128), start=True, stop=True)
                TE.matmul(pb[1][:, 256 + c0:256 + c0 + 128], sck_ap("kd8", h, n, w_=128), scr_ap("rd8", h, n, w_=128), start=True, stop=True)
                ins = TE.matmul(pb[2][:, c0:c0 + 128], sck_ap("ku", h, n, w_=128), r64_ap(h, n), start=True, stop=True)
            s_mm.inc(ins, 1)
            s_mm.wait(VE)
            VE.select(p3_A2F[:], sb_m[0][:], pb[0][:, 0:256], sb_zero[:])
            VE.select(p3_A2F[:], sb_m[1][:], pb[0][:, 256:512], p3_A2F[:])
            VE.select(p3_A2F[:], sb_m[2][:], pb[1][:, 0:256], p3_A2F[:])
            VE.select(p3_A2F[:], sb_m[3][:], pb[1][:, 256:512], p3_A2F[:])
            VE.select(p3_A2F[:], sb_m[4][:], pb[2][:, 0:256], p3_A2F[:])
            s_dv.inc(VE.tensor_copy(p3_A2[:], p3_A2F[:]), 1)
            TE.wait_ge(s_dv.h, s_dv.v)
            for h in range(2):
                TE.matmul(pb[3][64 * h:64 * h + 64, 0:128], bass.AP(p3_vtok, 64 * h, [[256, 128], [1, 64]]),
                          p3_A2[:, 128 * h:128 * h + 128], start=False, stop=True, skip_group_check=True)
                ins = TE.matmul(pb[2][0:64, 256 + 64 * h:256 + 64 * h + 64],
                                bass.AP(p3_vtok, 128 + 64 * h, [[256, 128], [1, 64]]),
                                bass.AP(p3_vtok, 64 * h, [[256, 128], [1, 64]]), start=True, stop=True)
            s_mm.inc(ins, 1)
            s_mm.wait(VE)
            VE.tensor_copy(p3_yc[:, n * 128:(n + 1) * 128], pb[3][:, 0:128])
            s_dv.inc(VE.scalar_tensor_tensor(p3_SA[:], p3_SA[:], p3_w128[:, 0:1],
                                             pb[2][0:64, 256:320], Alu.mult, Alu.add), 1)
            VE.wait_ge(s_dv.h, s_dv.v)
            s_dv.inc(VE.scalar_tensor_tensor(p3_SB[:], p3_SB[:], p3_w128[:, 1:2],
                                             pb[2][0:64, 320:384], Alu.mult, Alu.add), 1)
            VE.wait_ge(s_dv.h, s_dv.v)
            VE.tensor_copy(p3_SbfA[:], p3_SA[:])
            s_dv.inc(VE.tensor_copy(p3_SbfB[:], p3_SB[:]), 1)
        s_dv.wait(SY)
        s_ld.inc(SY.dma_start(out=y_s[c * 128:(c + 1) * 128, :], in_=p3_yc[:]), 16)
        VE.wait_ge(s_dv.h, s_dv.v)
        VE.tensor_scalar_mul(p3_sfin[0:64, :], p3_SA[:], sb_selinv[0:64, 0:1])
        s_dv.inc(VE.tensor_scalar_mul(p3_sfin[64:128, :], p3_SB[:], sb_selinv[0:64, 0:1]), 1)
        s_dv.wait(SY)
        s_ld.inc(SY.dma_start(out=cc_in[c * 128:(c + 1) * 128, :], in_=p3_sfin[:]), 16)
    bar()

    if stop_after == "P3":
        return nc, ["out"]

    escope("P3")
    scope("CC")
    s_cc.inc(GP.collective_compute("AllReduce", Alu.add,
                                   replica_groups=[[0, 1], [2, 3], [4, 5], [6, 7]],
                                   ins=[cc_in.opt()], outs=[cc_out.opt()]), 1)
    s_cc.wait(SY)
    bar()

    escope("CC")
    scope("P3b")
    AR.reset()
    bigZ = AR.t("p3z", (128, nI * T), BF16)
    b3_r64 = AR.t("b3r64", (64, 2 * T), BF16)
    b3_yc = AR.t("b3yc", (128, T), BF16)
    b3_g = AR.t("b3g", (128, T), BF16)
    b3_lw64 = AR.t("b3lw64", (64, 2), F32)
    b3_tmp64 = AR.t("b3tmp64", (64, 2 * NTCOL), F32)
    b3_T64 = AR.t("b3T64", (64, 2 * TBL_W), F32)
    b3_tmpn = AR.t("b3tmpn", (64, 2 * nN), F32)
    b3_w128p = AR.t("b3w128p", (64, 2 * nN), F32)
    b3_rhf = AR.t("b3rhf", (64, 2 * 128), F32)
    b3_rh = AR.t("b3rh", (64, 2 * 128), BF16)
    b3_rbar = AR.t("b3rbar", (64, 2 * T), BF16)
    b3_SiA = AR.t("b3SiA", (64, 64), F32)
    b3_SiB = AR.t("b3SiB", (64, 64), F32)
    b3_SiAb = AR.t("b3SiAb", (64, 64), BF16)
    b3_SiBb = AR.t("b3SiBb", (64, 64), BF16)
    b3_sq = AR.t("b3sq", (128, TT), BF16)
    b3_row = AR.t("b3row", (2, 4 * TT), F32)
    b3_t1 = AR.t("b3t1", (128, TT), F32)

    def bigZ_c(c):
        return bigZ[:, c * T:(c + 1) * T]

    T64vb = b3_T64[:].rearrange("d (h c) -> d h c", h=2, c=TBL_W)
    tmp64vb = b3_tmp64[:].rearrange("d (h c) -> d h c", h=2)
    rb_all = bass.AP(b3_rbar, 0, [[2 * T, 64], [T, 2], [128, nN], [1, 128]])
    for c in range(nC):
        s_dv.wait(SY)
        ap = bass.AP(r_s.tensor, c * 128 * T, [[T, 64], [64 * T, 2], [1, T]])
        s_ld.inc(SY.dma_start(out=b3_r64[:].rearrange("d (h t) -> d h t", h=2), in_=ap), 16)
        s_ld.inc(SY.dma_start(out=b3_yc[:], in_=y_s[c * 128:(c + 1) * 128, :]), 16)
        s_ld.inc(SY.dma_start(out=b3_g[:], in_=g_s[c * 128:(c + 1) * 128, :]), 16)
        s_ld.inc(SY.dma_start(out=b3_SiA[:], in_=cc_out[c * 128:c * 128 + 64, :]), 16)
        s_ld.inc(SY.dma_start(out=b3_SiB[:], in_=cc_out[c * 128 + 64:(c + 1) * 128, :]), 16)
        SC.wait_ge(s_dv.h, s_dv.v)
        s_ac.inc(SC.activation(b3_lw64[:], sb_dec64[:, 2 * c:2 * c + 2], AF.Exp), 1)
        s_ac.wait(VE)
        VE.tensor_tensor(tmp64vb, bass.AP(sb_Prow, 0, [[NTCOL, 64], [0, 2], [1, NTCOL]]),
                         bass.AP(b3_lw64, 0, [[2, 64], [1, 2], [0, NTCOL]]), Alu.mult)
        s_dv.inc(VE.tensor_tensor(b3_tmpn[:].rearrange("d (h n) -> d h n", h=2),
                                  bass.AP(sb_Pn, 0, [[nN, 64], [0, 2], [1, nN]]),
                                  bass.AP(b3_lw64, 0, [[2, 64], [1, 2], [0, nN]]), Alu.mult), 1)
        s_dv.wait(SC)
        s_ac.inc(SC.activation(T64vb[:, :, 0:NTCOL], tmp64vb, AF.Exp), 1)
        s_ac.inc(SC.activation(b3_w128p[:], b3_tmpn[:], AF.Exp), 1)
        s_ac.wait(VE)
        VE.tensor_scalar_min(b3_w128p[:], b3_w128p[:], 1e30)
        VE.tensor_copy(bass.AP(b3_rhf, 0, [[2 * 128, 64], [128, 2], [1, 128]]),
                       bass.AP(b3_T64, 16, [[2 * TBL_W, 64], [TBL_W, 2], [1, 128]]))
        s_dv.inc(VE.tensor_copy(b3_rh[:], b3_rhf[:]), 1)
        VE.wait_ge(s_dv.h, s_dv.v)
        s_ld.wait(VE)
        VE.tensor_tensor(rb_all,
                         bass.AP(b3_r64, 0, [[2 * T, 64], [T, 2], [128, nN], [1, 128]]),
                         bass.AP(b3_rh, 0, [[2 * 128, 64], [128, 2], [0, nN], [1, 128]]),
                         Alu.mult)
        s_dv.inc(VE.tensor_tensor(rb_all, rb_all,
                                  bass.AP(b3_w128p, 0, [[2 * nN, 64], [nN, 2], [1, nN], [0, 128]]),
                                  Alu.mult), 1)
        VE.tensor_scalar_mul(b3_SiAb[:], b3_SiA[:], sb_sel64[0:64, 0:1])
        s_dv.inc(VE.tensor_scalar_mul(b3_SiBb[:], b3_SiB[:], sb_sel64[0:64, 0:1]), 1)
        for n in range(nN):
            TE.wait_ge(s_dv.h, s_dv.v)
            TE.matmul(pb[3][0:64, 0:128], b3_SiAb[:],
                      bass.AP(b3_rbar, n * 128, [[2 * T, 64], [1, 128]]), start=True, stop=True)
            s_mm.inc(TE.matmul(pb[3][64:128, 0:128], b3_SiBb[:],
                               bass.AP(b3_rbar, T + n * 128, [[2 * T, 64], [1, 128]]),
                               start=True, stop=True), 1)
            s_mm.wait(VE)
            s_dv.inc(VE.tensor_add(b3_yc[:, n * 128:(n + 1) * 128],
                                   b3_yc[:, n * 128:(n + 1) * 128], pb[3][:, 0:128]), 1)
        for tt in range(nTT):
            t0 = tt * TT
            s_dv.wait(SC)
            SC.wait_ge(s_mm.h, s_mm.v)
            s_ac.inc(SC.activation(b3_sq[:], b3_yc[:, t0:t0 + TT], AF.Square), 1)
            s_dv.wait(TE)
            s_ac.wait(TE)
            TE.matmul(pb[4][0:2, 0:TT], sb_E2h[:], b3_yc[:, t0:t0 + TT], start=True, stop=True)
            s_mm.inc(TE.matmul(pb[4][32:34, 0:TT], sb_E2h[:], b3_sq[:], start=True, stop=True), 1)
            s_mm.wait(VE)
            mrow = b3_row[:, 0:TT]
            rrow = b3_row[:, TT:2 * TT]
            trow = b3_row[:, 2 * TT:3 * TT]
            qrow = b3_row[:, 3 * TT:4 * TT]
            s_dv.inc(VE.tensor_scalar_mul(mrow, pb[4][0:2, 0:TT], 1.0 / 64), 1)
            VE.wait_ge(s_dv.h, s_dv.v)
            s_dv.inc(VE.tensor_scalar_mul(qrow, pb[4][32:34, 0:TT], 1.0 / 64), 1)
            VE.wait_ge(s_dv.h, s_dv.v)
            s_dv.inc(VE.tensor_tensor(trow, mrow, mrow, Alu.mult), 1)
            VE.wait_ge(s_dv.h, s_dv.v)
            s_dv.inc(VE.tensor_sub(trow, qrow, trow), 1)
            VE.wait_ge(s_dv.h, s_dv.v)
            s_dv.wait(SC)
            s_ac.inc(SC.activation(qrow, trow, AF.Sqrt, bias=sb_eps[0:2, 0:1]), 1)
            s_ac.wait(VE)
            s_dv.inc(VE.reciprocal(rrow, qrow), 1)
            VE.wait_ge(s_dv.h, s_dv.v)
            s_dv.wait(TE)
            TE.matmul(pb[5][:, 0:TT], sb_F2[:], mrow, start=True, stop=True)
            s_mm.inc(TE.matmul(pb[6][:, 0:TT], sb_F2[:], rrow, start=True, stop=True), 1)
            s_mm.wait(VE)
            VE.tensor_sub(b3_t1[:], b3_yc[:, t0:t0 + TT], pb[5][:, 0:TT])
            VE.tensor_tensor(b3_t1[:], b3_t1[:], pb[6][:, 0:TT], Alu.mult)
            VE.tensor_scalar(b3_t1[:], b3_t1[:], sb_gnw[:, c:c + 1], sb_gnb[:, c:c + 1],
                             Alu.mult, Alu.add)
            s_dv.inc(VE.tensor_tensor(bigZ_c(c)[:, t0:t0 + TT], b3_t1[:],
                                      b3_g[:, t0:t0 + TT], Alu.mult), 1)
    bar()

    if stop_after == "P3b":
        return nc, ["out"]

    escope("P3b")
    scope("P4")
    p4_ws = [AR.t("p4ws", (128, nC * 128), BF16) for _ in range(2)]
    p4_x = AR.t("p4x", (128, TT), F32)
    p4_o = AR.t("p4o", (128, TT), F32)

    def woslab_load(ec, buf):
        src = bass.AP(woT.tensor, ec * 128, [[D, 128], [128 * D, nC], [1, 128]])
        wsl[ec % 2].inc(GP.dma_start(out=buf[:].rearrange("p (a o) -> p a o", a=nC), in_=src), 16)
        return wsl[ec % 2].v

    lg_vals = {0: woslab_load(0, p4_ws[0]), 1: woslab_load(1, p4_ws[1])}
    for ec in range(nI):
        buf = p4_ws[ec % 2]
        bs = 0
        TE.wait_ge(wsl[ec % 2].h, lg_vals[ec])
        TE.wait_ge(s_dv.h, s_dv.v)
        for a in range(nC):
            for tt in range(nTT):
                ins = TE.matmul(pb[bs + tt][:, 0:TT], buf[:, a * 128:(a + 1) * 128],
                                bigZ_c(a)[:, tt * TT:(tt + 1) * TT],
                                start=(a == 0), stop=(a == nC - 1))
        s_mm.inc(ins, 1)
        if ec + 2 < nI:
            GP.wait_ge(s_mm.h, s_mm.v)
            lg_vals[ec + 2] = woslab_load(ec + 2, buf)
        for tt in range(nTT):
            SY.wait_ge(s_dv.h, s_dv.v)
            s_ld.inc(SY.dma_start(out=p4_x[:, 0:TT],
                                  in_=xT[ec * 128:(ec + 1) * 128, 1 + tt * TT:1 + (tt + 1) * TT]), 16)
            s_ld.wait(VE)
            s_mm.wait(VE)
            s_dv.inc(VE.tensor_add(p4_o[:], pb[bs + tt][:, 0:TT], p4_x[:, 0:TT]), 1)
            s_dv.wait(SY)
            s_ld.inc(SY.dma_start(out=part_s[ec * 128:(ec + 1) * 128, tt * TT:(tt + 1) * TT],
                                  in_=p4_o[:]), 16)
            s_ld.wait(VE)
    bar()

    if stop_after == "P4":
        return nc, ["out"]

    escope("P4")
    scope("P5a")
    AR.reset()
    f_xn = AR.t("fxn", (128, nI * TH2), BF16)
    f_pk = [AR.t("fpk", (128, nJb * 128), I32) for _ in range(2)]
    f_int = AR.t("fint", (128, nJb * 128), I32)
    f_sc4 = [AR.t("fsc4", (4, nJb * 128), F32) for _ in range(2)]
    f_slab = [AR.t("fslab", (128, nJb * 128), F32) for _ in range(2)]
    f_sz = [AR.t("fsz", (GI, 256), F32) for _ in range(2)]
    f_zsc = [AR.t("fzsc", (GI, 128), BF16) for _ in range(2)]
    f_wbf = [AR.t("fwbf", (128, 2 * nJb * 128), BF16) for _ in range(2)]
    f_gev = AR.t("fgev", (128, TH2), BF16)
    f_h = AR.t("fh", (128, TH2), BF16)
    NSP = (nJb * 128 + 511) // 512

    def fxn_c(a):
        return f_xn[:, a * TH2:(a + 1) * TH2]

    passes = [(th, oc, pj) for th in range(2) for oc in range(nO) for pj in range(2)]
    pair_of = lambda q: q % 3
    prep_done = [None] * len(passes)
    prep_doneA = [None] * len(passes)
    mmv = [None] * len(passes)
    evd_ac = [0] * len(passes)
    evd_dv = [0] * len(passes)

    def emit_prep(q):
        th, oc, pj = passes[q]
        wq_t = (wgq if pj == 0 else wuq)
        sT_t = (gsT if pj == 0 else usT)
        zT_t = (gzT if pj == 0 else uzT)
        s4_t = (gs4 if pj == 0 else us4)
        pk = f_pk[q % 2]
        sc4 = f_sc4[q % 2]
        slab = f_slab[q % 2]
        sz = f_sz[q % 2]
        zsc = f_zsc[q % 2]
        wbf = f_wbf[q % 2]
        if q >= 2 and mmv[q - 2] is not None:
            GP.wait_ge(s_mm.h, mmv[q - 2])
            SY.wait_ge(s_mm.h, mmv[q - 2])
            SC.wait_ge(s_mm.h, mmv[q - 2])
        pks[q % 2].inc(GP.dma_start(out=pk[:].rearrange("p (a o) -> p a o", a=nJb),
                                     in_=bass.AP(wq_t.tensor, oc * 128, [[I, 128], [128 * I, nJb], [1, 128]])), 16)
        szs[q % 2].inc(SY.dma_start(out=sz[:, 0:128], in_=sT_t[:, oc * 128:(oc + 1) * 128]), 16)
        szs[q % 2].inc(SY.dma_start(out=sz[:, 128:256], in_=zT_t[:, oc * 128:(oc + 1) * 128]), 16)
        scs[q % 2].inc(SC.dma_start(out=sc4[:].rearrange("p (a o) -> p a o", a=nJb),
                                    in_=bass.AP(s4_t.tensor, oc * 128, [[nJb * I, 4], [I, nJb], [1, 128]])), 16)
        ld_here = (pks[q % 2].v, szs[q % 2].v, scs[q % 2].v)
        for b_ in range(NSP):
            wv = min(512, nJb * 128 - b_ * 512)
            TE.wait_ge(scs[q % 2].h, ld_here[2])
            if b_ > 0 or q > 0:
                TE.wait_ge(s_ac.h, s_ac.v)
            s_mm.inc(TE.matmul(pb[6][:, 0:wv], sb_E4[:], sc4[:, b_ * 512:b_ * 512 + wv],
                               start=True, stop=True), 1)
            s_mm.wait(SC)
            s_ac.inc(SC.activation(slab[:, b_ * 512:b_ * 512 + wv], pb[6][:, 0:wv], AF.Copy), 1)
        VE.wait_ge(szs[q % 2].h, ld_here[1])
        s_dv.inc(VE.scalar_tensor_tensor(zsc[:], sz[:, 128:256], -1.0, sz[:, 0:128],
                                         Alu.mult, Alu.mult), 1)
        VE.wait_ge(pks[q % 2].h, ld_here[0])
        VE.wait_ge(s_ac.h, s_ac.v)
        if q >= 2 and mmv[q - 2] is not None:
            VE.wait_ge(s_mm.h, mmv[q - 2])
        VE.tensor_single_scalar(f_int[:], pk[:], 4, Alu.logical_shift_right)
        s_dv.inc(VE.tensor_tensor(wbf[:, 0:nJb * 128], f_int[:], slab[:], Alu.mult), 1)
        prep_doneA[q] = s_dv.v
        VE.tensor_single_scalar(f_int[:], pk[:], 15, Alu.bitwise_and)
        s_dv.inc(VE.tensor_tensor(wbf[:, nJb * 128:2 * nJb * 128], f_int[:], slab[:], Alu.mult), 1)
        prep_done[q] = s_dv.v

    emit_prep(0)
    emit_prep(1)
    for q, (th, oc, pj) in enumerate(passes):
        tb = th * TH2
        if oc == 0 and pj == 0:
            SY.wait_ge(s_mm.h, s_mm.v)
            src = bass.AP(xn_s.tensor, 1 + tb, [[T + 1, 128], [128 * (T + 1), nI], [1, TH2]])
            s_xh.inc(SY.dma_start(out=f_xn[:].rearrange("p (a t) -> p a t", a=nI), in_=src), 16)
        ldx = s_xh.v
        pr = pair_of(q)
        wbf = f_wbf[q % 2]
        zsc = f_zsc[q % 2]
        TE.wait_ge(s_dv.h, prep_doneA[q])
        TE.wait_ge(s_xh.h, ldx)
        if q >= 3:
            TE.wait_ge(s_ac.h, evd_ac[q - 3])
            TE.wait_ge(s_dv.h, evd_dv[q - 3])
        for tt in range(nT2):
            TE.matmul(pb[2 * pr + tt][:, 0:TT2], zsc[:],
                      sb_sgrp[:, tb + tt * TT2:tb + (tt + 1) * TT2],
                      start=True, stop=False)
        for a in range(2 * nJb):
            if a == nJb:
                TE.wait_ge(s_dv.h, prep_done[q])
            for tt in range(nT2):
                ins = TE.matmul(pb[2 * pr + tt][:, 0:TT2], wbf[:, a * 128:(a + 1) * 128],
                                fxn_c(a)[:, tt * TT2:(tt + 1) * TT2],
                                start=False, stop=(a == 2 * nJb - 1))
        s_mm.inc(ins, 1)
        mmv[q] = s_mm.v
        s_mm.wait(SC)
        s_mm.wait(VE)
        if pj == 0:
            if q >= 2:
                SC.wait_ge(s_dv.h, s_dv.v)
            for tt in range(nT2):
                ins = SC.activation(f_gev[:, tt * TT2:(tt + 1) * TT2],
                                    pb[2 * pr + tt][:, 0:TT2], AF.Silu)
            s_ac.inc(ins, 1)
        else:
            s_ac.wait(VE)
            s_ld.wait(VE)
            for tt in range(nT2):
                ins = VE.tensor_tensor(f_h[:, tt * TT2:(tt + 1) * TT2],
                                       f_gev[:, tt * TT2:(tt + 1) * TT2],
                                       pb[2 * pr + tt][:, 0:TT2], Alu.mult)
            s_dv.inc(ins, 1)
            s_dv.wait(SY)
            s_ld.inc(SY.dma_start(out=h_s[oc * 128:(oc + 1) * 128, tb:tb + TH2], in_=f_h[:]), 16)
        evd_ac[q] = s_ac.v
        evd_dv[q] = s_dv.v
        if q + 2 < len(passes):
            emit_prep(q + 2)
    bar()

    if stop_after == "P5a":
        return nc, ["out"]

    escope("P5a")
    scope("P5c")
    AR.reset()
    d_pk = AR.t("dpk", (128, nJd * 128), I32)
    d_int = AR.t("dint", (128, nJd * 128), I32)
    d_t = [AR.t("dt", (128, nJd * 128), F32) for _ in range(2)]
    d_zs = AR.t("dzs", (128, nJd * 128), F32)
    d_c4 = AR.t("dc4", (4, nJd * 128), F32)
    d_wbf = AR.t("dwbf", (128, 2 * nJd * 128), BF16)
    NSPD = (nJd * 128 + 511) // 512
    for ec in range(nI):
        s_dv.wait(SY)
        s_dv.wait(GP)
        SC.wait_ge(s_mm.h, s_mm.v)
        s_lg.inc(GP.dma_start(out=d_pk[:].rearrange("p (a o) -> p a o", a=nJd),
                              in_=bass.AP(wdq.tensor, ec * 128, [[D, 128], [128 * D, nJd], [1, 128]])), 16)
        s_la.inc(SC.dma_start(out=d_c4[:].rearrange("p (a o) -> p a o", a=nJd),
                              in_=bass.AP(dz4.tensor, ec * 128, [[nJd * D, 4], [D, nJd], [1, 128]])), 16)
        s_la.wait(TE)
        for b_ in range(NSPD):
            wv = min(512, nJd * 128 - b_ * 512)
            bank = pb[b_ % 6]
            if ec > 0 or b_ >= 6:
                TE.wait_ge(s_dv.h, s_dv.v)
            s_mm.inc(TE.matmul(bank[:, 0:wv], sb_E4[:], d_c4[:, b_ * 512:b_ * 512 + wv],
                               start=True, stop=True), 1)
            s_mm.wait(VE)
            s_dv.inc(VE.tensor_copy(d_zs[:, b_ * 512:b_ * 512 + wv], bank[:, 0:wv]), 1)
        VE.wait_ge(s_lg.h, s_lg.v)
        VE.tensor_single_scalar(d_int[:], d_pk[:], 4, Alu.logical_shift_right)
        s_dv.inc(VE.tensor_sub(d_t[0][:], d_int[:], d_zs[:]), 1)
        VE.tensor_single_scalar(d_int[:], d_pk[:], 15, Alu.bitwise_and)
        s_dv.inc(VE.tensor_sub(d_t[1][:], d_int[:], d_zs[:]), 1)
        VE.wait_ge(s_dv.h, s_dv.v)
        SC.wait_ge(s_dv.h, s_dv.v)
        s_la.inc(SC.dma_start(out=d_c4[:].rearrange("p (a o) -> p a o", a=nJd),
                              in_=bass.AP(ds4.tensor, ec * 128, [[nJd * D, 4], [D, nJd], [1, 128]])), 16)
        s_la.wait(TE)
        TE.wait_ge(s_dv.h, s_dv.v)
        for b_ in range(NSPD):
            wv = min(512, nJd * 128 - b_ * 512)
            bank = pb[b_ % 6]
            if b_ >= 6:
                TE.wait_ge(s_dv.h, s_dv.v)
            s_mm.inc(TE.matmul(bank[:, 0:wv], sb_E4[:], d_c4[:, b_ * 512:b_ * 512 + wv],
                               start=True, stop=True), 1)
            s_mm.wait(VE)
            s_dv.inc(VE.tensor_copy(d_zs[:, b_ * 512:b_ * 512 + wv], bank[:, 0:wv]), 1)
        VE.wait_ge(s_dv.h, s_dv.v)
        if ec == 0 and "dbg1" in dump:
            s_dv.inc(VE.tensor_copy(d_t[1 - 1][:], d_t[0][:]), 1)
            s_dv.wait(SY)
            s_ld.inc(SY.dma_start(out=dbg1, in_=d_t[0][:]), 16)
            s_ld.inc(SY.dma_start(out=dbg2, in_=d_zs[:]), 16)
            s_ld.wait(VE)
        VE.tensor_tensor(d_wbf[:, 0:nJd * 128], d_t[0][:], d_zs[:], Alu.mult)
        s_dv.inc(VE.tensor_tensor(d_wbf[:, nJd * 128:2 * nJd * 128], d_t[1][:], d_zs[:], Alu.mult), 1)
        s_dv.wait(SY)
        dst0 = bass.AP(wd_s.tensor, ec * 128, [[D, 128], [128 * D, nJd], [1, 128]])
        dst1 = bass.AP(wd_s.tensor, nJd * 128 * D + ec * 128, [[D, 128], [128 * D, nJd], [1, 128]])
        s_ld.inc(SY.dma_start(out=dst0, in_=d_wbf[:, 0:nJd * 128].rearrange("p (a o) -> p a o", a=nJd)), 16)
        s_ld.inc(SY.dma_start(out=dst1, in_=d_wbf[:, nJd * 128:2 * nJd * 128].rearrange("p (a o) -> p a o", a=nJd)), 16)
        s_ld.wait(VE)
    bar()

    if stop_after == "P5c":
        return nc, ["out"]

    escope("P5c")
    scope("P5b")
    AR.reset()
    q_h = AR.t("qh", (128, nO * TQ), BF16)
    q_ws = [AR.t("qws", (128, nO * 128), BF16) for _ in range(2)]
    q_x = AR.t("qx", (128, TQ), F32)
    q_o = AR.t("qo", (128, TQ), F32)

    def qh_c(a):
        return q_h[:, a * TQ:(a + 1) * TQ]

    for qq in range(nQ):
        t0 = qq * TQ
        SY.wait_ge(s_mm.h, s_mm.v)
        src = bass.AP(h_s.tensor, t0, [[T, 128], [128 * T, nO], [1, TQ]])
        s_hq.inc(SY.dma_start(out=q_h[:].rearrange("p (a t) -> p a t", a=nO), in_=src), 16)
        ldh = s_hq.v

        def wdslab_load(ec, buf):
            src2 = bass.AP(wd_s.tensor, ec * 128, [[D, 128], [128 * D, nO], [1, 128]])
            wds[ec % 2].inc(GP.dma_start(out=buf[:].rearrange("p (a o) -> p a o", a=nO), in_=src2), 16)
            return wds[ec % 2].v

        GP.wait_ge(s_mm.h, s_mm.v)
        lgv = {0: wdslab_load(0, q_ws[0]), 1: wdslab_load(1, q_ws[1])}
        for ec in range(nI):
            buf = q_ws[ec % 2]
            bank = pb[ec % 2]
            if qq == 0 and ec == 0 and "dbg5" in dump:
                SY.wait_ge(s_lg.h, lgv[0])
                s_ld.inc(SY.dma_start(out=dbg5, in_=buf[:]), 16)
                s_ld.inc(SY.dma_start(out=dbg6, in_=q_h[:]), 16)
            TE.wait_ge(wds[ec % 2].h, lgv[ec])
            TE.wait_ge(s_hq.h, ldh)
            if ec >= 2:
                TE.wait_ge(s_dv.h, s_dv.v)
            for a in range(nO):
                ins = TE.matmul(bank[:, 0:TQ], buf[:, a * 128:(a + 1) * 128],
                                qh_c(a), start=(a == 0), stop=(a == nO - 1))
            s_mm.inc(ins, 1)
            if ec + 2 < nI:
                GP.wait_ge(s_mm.h, s_mm.v)
                lgv[ec + 2] = wdslab_load(ec + 2, buf)
            SY.wait_ge(s_dv.h, s_dv.v)
            s_ld.inc(SY.dma_start(out=q_x[:, 0:TQ], in_=part_s[ec * 128:(ec + 1) * 128, t0:t0 + TQ]), 16)
            s_ld.wait(VE)
            s_mm.wait(VE)
            s_dv.inc(VE.tensor_add(q_o[:], bank[:, 0:TQ], q_x[:, 0:TQ]), 1)
            s_dv.wait(SY)
            s_ld.inc(SY.dma_start(out=out[ec * 128:(ec + 1) * 128, t0:t0 + TQ], in_=q_o[:]), 16)
            s_ld.wait(VE)
    SY.wait_ge(s_ld.h, s_ld.v)
    bar()
    escope("P5b")

    return nc, ["out"]


def kernel(**inputs):
    cfg = make_cfg()
    cons = host_constants(cfg)
    in_maps = [host_core_inputs(cfg, c, inputs, cons) for c in range(8)]
    nc, _ = build_program(cfg)
    res = run_bass_kernel_spmd(nc, in_maps, core_ids=list(range(8)))
    return host_assemble_output(cfg, res.results)


# version 03 (final)
# speedup vs baseline: 1.0893x; 1.0837x over previous
import numpy as np
import ml_dtypes
import concourse.bass as bass
import concourse.mybir as mybir
from concourse.bass_utils import run_bass_kernel_spmd
from contextlib import ExitStack

F32 = mybir.dt.float32
BF16 = mybir.dt.bfloat16
I32 = mybir.dt.int32
U8 = mybir.dt.uint8
Alu = mybir.AluOpType
AF = mybir.ActivationFunctionType
BF16NP = ml_dtypes.bfloat16

NTCOL = 145
TBL_W = NTCOL + 1

PATK = ["k1a", "k1b", "k1c", "k2", "k3", "kd8", "ku"]
PATR = ["rh", "r1", "r2", "r3", "rd8"]
PATKS = {
    "k1a": ([[-1, 32]], 31),
    "k1b": ([[-1, 64]], 63),
    "k1c": ([[-1, 96]], 95),
    "k2": ([[0, 8], [-1, 16]], 15),
    "k3": ([[0, 16], [-1, 8]], 7),
    "kd8": ([[0, 16], [-1, 8]], 3),
    "ku": (None, None),
}
PATRS = {
    "rh": ([[1, 128]], 0),
    "r1": ([[0, 4], [1, 32]], 0),
    "r2": ([[0, 4], [1, 32]], -16),
    "r3": ([[0, 8], [1, 16]], -8),
    "rd8": ([[0, 16], [1, 8]], -4),
}


def pat_width(dims):
    w = 1
    for _, c in dims:
        w *= c
    return w


KW = sum(pat_width(PATKS[n][0]) if n != "ku" else 128 for n in PATK)
RW = sum(pat_width(PATRS[n][0]) for n in PATR)


def make_cfg(D=4096, T=2048, I=11008, B=4):
    c = {"D": D, "T": T, "I": I, "B": B}
    c["H"] = D // 64
    c["nI"] = D // 128
    c["nJb"] = (D // 2) // 128
    c["GI"] = D // 64
    c["nO"] = I // 128
    c["nJd"] = (I // 2) // 128
    c["GD"] = I // 64
    c["nC"] = D // 128
    c["nN"] = T // 128
    c["TT"] = min(512, T)
    c["nTT"] = T // c["TT"]
    c["TH2"] = T // 2
    c["nT2"] = c["TH2"] // min(512, c["TH2"])
    c["TT2"] = min(512, c["TH2"])
    c["TQ"] = min(512, T)
    c["nQ"] = T // c["TQ"]
    assert D % 256 == 0 and T % 256 == 0 and I % 256 == 0
    assert c["GI"] <= 128
    return c



def pi_perm(n):
    return np.concatenate([np.arange(0, n, 2), np.arange(1, n, 2)])


def host_masks():
    j = np.arange(128)[:, None]
    i = np.arange(128)[None, :]
    m1 = (j // 32 < i // 32)
    m2 = (i // 32 == j // 32) & (j % 32 < 16) & (i % 32 >= 16)
    m3 = (i // 16 == j // 16) & (j % 16 < 8) & (i % 16 >= 8)
    md8 = (i // 8 == j // 8) & (j < i)
    mdiag = (i == j)
    return [np.tile(m.astype(np.uint8), (1, 2)) for m in (m1, m2, m3, md8, mdiag)]


def host_constants(cfg):
    GI, nI, nJb, nN = cfg["GI"], cfg["nI"], cfg["nJb"], cfg["nN"]
    cons = {}
    for nm, m in zip(("c_m1", "c_m2", "c_m3", "c_md8", "c_mdiag"), host_masks()):
        cons[nm] = m
    E4 = np.zeros((4, 128), np.float32)
    for gp in range(4):
        E4[gp, gp * 32:(gp + 1) * 32] = 1.0
    cons["c_E4"] = E4
    cons["c_E4b"] = E4.astype(BF16NP)
    p = np.arange(128)
    E16 = np.zeros((nI, 128, GI), np.float32)
    for c in range(nI):
        E16[c, p, 4 * (c % nJb) + p // 32] = 1.0
    cons["c_E16"] = E16.astype(BF16NP)
    E2h = np.zeros((128, 2), np.float32)
    E2h[p, p // 64] = 1.0
    cons["c_E2h"] = E2h.astype(BF16NP)
    F2 = np.zeros((2, 128), np.float32)
    F2[p // 64, p] = 1.0
    cons["c_F2"] = F2
    cons["c_ones1"] = np.ones((1, 128), np.float32)
    cons["c_onesD"] = np.ones((128, 1), np.float32)
    cons["c_Prow"] = np.tile(-np.arange(-16, 129, dtype=np.float32)[None, :], (128, 1))
    cons["c_Pn"] = np.tile(-128.0 * np.arange(nN, dtype=np.float32)[None, :], (128, 1))
    cons["c_idn"] = np.eye(128, dtype=np.float32)
    return cons


def host_core_inputs(cfg, core, inputs, cons):
    D, T, I, nC = cfg["D"], cfg["T"], cfg["I"], cfg["nC"]
    b, hf = core // 2, core % 2
    pi = pi_perm(D)
    pio = pi_perm(I)
    x = np.asarray(inputs["L0_attn_hidden"])
    m = dict(cons)
    xc = np.zeros((D, T + 1), np.float32)
    s0 = hf * T
    xc[:, 1:] = np.ascontiguousarray(x[b, s0:s0 + T].T)[pi]
    if hf == 1:
        xc[:, 0] = x[b, s0 - 1][pi]
    m["xT"] = np.ascontiguousarray(xc)
    for nm, w in (("wkT", "L0_attn_w_key"), ("wvT", "L0_attn_w_value"),
                  ("wrT", "L0_attn_w_receptance"), ("wgT", "L0_attn_w_gate")):
        m[nm] = np.ascontiguousarray(np.asarray(inputs[w]).T[pi])
    m["woT"] = np.ascontiguousarray(np.asarray(inputs["L0_attn_w_output"]).T[:, pi])
    for nm, v in (("tmk", "L0_attn_time_mix_key"), ("tmv", "L0_attn_time_mix_value"),
                  ("tmr", "L0_attn_time_mix_receptance"), ("tmg", "L0_attn_time_mix_gate")):
        m[nm] = np.ascontiguousarray(np.asarray(inputs[v])[pi].reshape(nC, 128).T)
    dec = np.asarray(inputs["L0_attn_time_decay"]).reshape(-1)
    u = np.asarray(inputs["L0_attn_time_faaaa"]).reshape(-1)
    m["dec64"] = np.ascontiguousarray(dec.reshape(nC, 2, 64).transpose(2, 0, 1).reshape(64, 2 * nC))
    m["u64"] = np.ascontiguousarray(u.reshape(nC, 2, 64).transpose(2, 0, 1).reshape(64, 2 * nC))
    m["dec128"] = np.ascontiguousarray(dec.reshape(nC, 128).T)
    m["gnw"] = np.ascontiguousarray(np.asarray(inputs["L0_attn_gn_weight"]).reshape(nC, 128).T)
    m["gnb"] = np.ascontiguousarray(np.asarray(inputs["L0_attn_gn_bias"]).reshape(nC, 128).T)
    m["wgq"] = np.ascontiguousarray(np.asarray(inputs["L0_ffn_gate_Wq"])[pio].T)
    m["wuq"] = np.ascontiguousarray(np.asarray(inputs["L0_ffn_up_Wq"])[pio].T)
    nJb, nJd = cfg["nJb"], cfg["nJd"]
    for pre, snm, znm in (("g", "L0_ffn_gate_scale", "L0_ffn_gate_zero"),
                          ("u", "L0_ffn_up_scale", "L0_ffn_up_zero")):
        sT = np.ascontiguousarray(np.asarray(inputs[snm])[pio].T)
        zT = np.ascontiguousarray(np.asarray(inputs[znm])[pio].T)
        m[pre + "sT"] = sT
        m[pre + "zT"] = zT
        m[pre + "s4"] = np.ascontiguousarray(sT.reshape(nJb, 4, I).transpose(1, 0, 2))
    m["wdq"] = np.ascontiguousarray(np.asarray(inputs["L0_ffn_down_Wq"])[pi].T)
    sTd = np.ascontiguousarray(np.asarray(inputs["L0_ffn_down_scale"])[pi].T)
    zTd = np.ascontiguousarray(np.asarray(inputs["L0_ffn_down_zero"])[pi].T)
    m["ds4"] = np.ascontiguousarray(sTd.reshape(nJd, 4, D).transpose(1, 0, 2))
    m["dz4"] = np.ascontiguousarray(zTd.reshape(nJd, 4, D).transpose(1, 0, 2))
    m["sel64"] = np.full((128, 1), 1.0 if hf == 1 else 0.0, np.float32)
    m["selinv"] = np.full((128, 1), 0.0 if hf == 1 else 1.0, np.float32)
    return m


def host_assemble_output(cfg, results):
    D, T, B = cfg["D"], cfg["T"], cfg["B"]
    pi = pi_perm(D)
    inv = np.empty(D, np.int64)
    inv[pi] = np.arange(D)
    out = np.empty((B, 2 * T, D), np.float32)
    for core in range(2 * B):
        b, hf = core // 2, core % 2
        o = np.asarray(results[core]["out"])
        out[b, hf * T:(hf + 1) * T] = o[inv].T
    return out



class Sem:
    def __init__(self, nc, name):
        self.h = nc.alloc_semaphore(name)
        self.v = 0

    def inc(self, instr, n=1):
        instr.then_inc(self.h, n)
        self.v += n
        return instr

    def wait(self, eng, v=None):
        eng.wait_ge(self.h, self.v if v is None else v)


class Arena:
    def __init__(self, nc, base, size):
        self.nc, self.base, self.size = nc, base, size
        self.off = 0
        self.uid = 0

    def reset(self):
        self.off = 0

    def t(self, name, shape, dtype):
        per = int(np.prod(shape[1:])) * mybir.dt.size(dtype)
        off = (self.off + 31) & ~31
        assert off + per <= self.size, f"arena overflow at {name}: {off}+{per}>{self.size}"
        self.off = off + per
        self.uid += 1
        return self.nc.alloc_sbuf_tensor_at(f"{name}_{self.uid}", list(shape), dtype,
                                            offset=self.base + off)


def build_program(cfg, dump=(), stop_after=None):
    nc = bass.Bass("TRN2", target_bir_lowering=False, debug=False,
                   detect_race_conditions=False)
    D, T, I = cfg["D"], cfg["T"], cfg["I"]
    nI, nJb, GI = cfg["nI"], cfg["nJb"], cfg["GI"]
    nO, nJd = cfg["nO"], cfg["nJd"]
    nC, nN, TT, nTT = cfg["nC"], cfg["nN"], cfg["TT"], cfg["nTT"]
    TH2, nT2, TT2 = cfg["TH2"], cfg["nT2"], cfg["TT2"]
    TQ, nQ = cfg["TQ"], cfg["nQ"]
    dump = set(dump)

    def dram(name, shape, dtype, kind="Internal"):
        if kind == "Internal" and name in dump:
            kind = "ExternalOutput"
        return nc.dram_tensor(name, list(shape), dtype, kind=kind).ap()

    xT = dram("xT", (D, T + 1), F32, "ExternalInput")
    wT = {p: dram(nm, (D, D), F32, "ExternalInput")
          for p, nm in (("k", "wkT"), ("v", "wvT"), ("r", "wrT"), ("g", "wgT"))}
    woT = dram("woT", (D, D), F32, "ExternalInput")
    tmix = {p: dram(nm, (128, nC), F32, "ExternalInput")
            for p, nm in (("k", "tmk"), ("v", "tmv"), ("r", "tmr"), ("g", "tmg"))}
    dec64d = dram("dec64", (64, 2 * nC), F32, "ExternalInput")
    u64d = dram("u64", (64, 2 * nC), F32, "ExternalInput")
    dec128d = dram("dec128", (128, nC), F32, "ExternalInput")
    gnwd = dram("gnw", (128, nC), F32, "ExternalInput")
    gnbd = dram("gnb", (128, nC), F32, "ExternalInput")
    wgq = dram("wgq", (D // 2, I), I32, "ExternalInput")
    wuq = dram("wuq", (D // 2, I), I32, "ExternalInput")
    wdq = dram("wdq", (I // 2, D), I32, "ExternalInput")
    gsT = dram("gsT", (GI, I), F32, "ExternalInput")
    gzT = dram("gzT", (GI, I), F32, "ExternalInput")
    gs4 = dram("gs4", (4, nJb, I), F32, "ExternalInput")
    usT = dram("usT", (GI, I), F32, "ExternalInput")
    uzT = dram("uzT", (GI, I), F32, "ExternalInput")
    us4 = dram("us4", (4, nJb, I), F32, "ExternalInput")
    ds4 = dram("ds4", (4, nJd, D), F32, "ExternalInput")
    dz4 = dram("dz4", (4, nJd, D), F32, "ExternalInput")
    sel64d = dram("sel64", (128, 1), F32, "ExternalInput")
    selinvd = dram("selinv", (128, 1), F32, "ExternalInput")
    c_m = [dram(n, (128, 256), U8, "ExternalInput")
           for n in ("c_m1", "c_m2", "c_m3", "c_md8", "c_mdiag")]
    c_E4 = dram("c_E4", (4, 128), F32, "ExternalInput")
    c_E4b = dram("c_E4b", (4, 128), BF16, "ExternalInput")
    c_E16 = dram("c_E16", (nI, 128, GI), BF16, "ExternalInput")
    c_E2h = dram("c_E2h", (128, 2), BF16, "ExternalInput")
    c_F2 = dram("c_F2", (2, 128), F32, "ExternalInput")
    c_ones1 = dram("c_ones1", (1, 128), F32, "ExternalInput")
    c_onesD = dram("c_onesD", (128, 1), F32, "ExternalInput")
    c_Prow = dram("c_Prow", (128, NTCOL), F32, "ExternalInput")
    c_Pn = dram("c_Pn", (128, nN), F32, "ExternalInput")
    c_idn = dram("c_idn", (128, 128), F32, "ExternalInput")

    xn_s = dram("xn_s", (D, T + 1), BF16)
    k_s = dram("k_s", (D, T), BF16)
    v_s = dram("v_s", (D, T), BF16)
    r_s = dram("r_s", (D, T), BF16)
    g_s = dram("g_s", (D, T), BF16)
    y_s = dram("y_s", (D, T), BF16)
    part_s = dram("part_s", (D, T), F32)
    h_s = dram("h_s", (I, T), BF16)
    wd_s = dram("wd_s", (I, D), BF16)
    cc_in = dram("cc_in", (D, 64), F32)
    cc_out = dram("cc_out", (D, 64), F32)
    out = dram("out", (D, T), F32, "ExternalOutput")
    dbg1 = dram("dbg1", (128, nJd * 128), F32)
    dbg2 = dram("dbg2", (128, nJd * 128), F32)
    dbg3 = dram("dbg3", (128, nI * 128), BF16)
    dbg4 = dram("dbg4", (128, nI * T), BF16)
    dbg5 = dram("dbg5", (128, nO * 128), BF16)
    dbg6 = dram("dbg6", (128, nO * TQ), BF16)

    sb = lambda name, shape, dtype: nc.alloc_sbuf_tensor(name, list(shape), dtype)

    sb_tmix = {p: sb("s_tm_" + p, (128, nC), F32) for p in "kvrg"}
    sb_gnw = sb("s_gnw", (128, nC), F32)
    sb_gnb = sb("s_gnb", (128, nC), F32)
    sb_sel64 = sb("s_sel64", (128, 1), F32)
    sb_selinv = sb("s_selinv", (128, 1), F32)
    sb_m = [sb("s_m%d" % i, (128, 256), U8) for i in range(5)]
    sb_E4 = sb("s_E4", (4, 128), F32)
    sb_E4b = sb("s_E4b", (4, 128), BF16)
    sb_E16 = sb("s_E16", (128, nI * GI), BF16)
    sb_E2h = sb("s_E2h", (128, 2), BF16)
    sb_F2 = sb("s_F2", (2, 128), F32)
    sb_ones1 = sb("s_ones1", (1, 128), F32)
    sb_onesD = sb("s_onesD", (128, 1), F32)
    sb_Prow = sb("s_Prow", (128, NTCOL), F32)
    sb_Pn = sb("s_Pn", (128, nN), F32)
    sb_id = sb("s_id", (128, 128), BF16)
    sb_zero = sb("s_zero", (128, 256), F32)
    sb_eps = sb("s_eps", (128, 1), F32)
    sb_sgrp = sb("s_sgrp", (GI, T), BF16)
    sb_dec64 = sb("s_dec64", (64, 2 * nC), F32)
    sb_u64 = sb("s_u64", (64, 2 * nC), F32)
    sb_dec128 = sb("s_dec128", (128, nC), F32)

    pb = [nc.alloc_psum_tensor("pb%d" % i, [128, 512], F32) for i in range(7)]
    pt = nc.alloc_psum_tensor("pt", [128, 1024], BF16)

    rem = nc.sbuf_bytes_remaining
    ARENA = rem - 2048
    arena_t = nc.alloc_sbuf_tensor("ARENA", [128, ARENA], U8)
    abase = nc.lookup_mloc(arena_t).addr
    AR = Arena(nc, abase, ARENA)

    s_ld = Sem(nc, "ld")
    s_la = Sem(nc, "la")
    s_lg = Sem(nc, "lg")
    s_mm = Sem(nc, "mm")
    s_dv = Sem(nc, "dv")
    s_ac = Sem(nc, "ac")
    s_cc = Sem(nc, "cc")
    sx = [Sem(nc, "sx%d" % i) for i in range(2)]
    wsl = [Sem(nc, "wsl%d" % i) for i in range(2)]
    pks = [Sem(nc, "pks%d" % i) for i in range(2)]
    szs = [Sem(nc, "szs%d" % i) for i in range(2)]
    scs = [Sem(nc, "scs%d" % i) for i in range(2)]
    s_xh = Sem(nc, "xh")
    s_hq = Sem(nc, "hq")
    wds = [Sem(nc, "wds%d" % i) for i in range(2)]

    SY, GP, SC, TE, VE = nc.sync, nc.gpsimd, nc.scalar, nc.tensor, nc.vector
    DMA_SY = [s_ld, s_xh, s_hq] + sx + szs
    DMA_GP = [s_lg] + wsl + pks + wds
    DMA_SC = [s_la] + scs

    def bar():
        for s in DMA_SY:
            SY.wait_ge(s.h, s.v)
        for s in DMA_SC:
            SC.wait_ge(s.h, s.v)
        for s in DMA_GP:
            GP.wait_ge(s.h, s.v)
        nc.all_engine_barrier()

    scope_ids = {}
    def scope(nm):
        scope_ids[nm] = nc.enter_named_scope(nm, False)[0]
    def escope(nm):
        nc.leave_named_scope(nm, scope_ids[nm], False)
    scope("P0")
    for dst, src in ((sb_sel64, sel64d), (sb_selinv, selinvd), (sb_E4, c_E4),
                     (sb_E4b, c_E4b), (sb_E2h, c_E2h), (sb_F2, c_F2), (sb_ones1, c_ones1),
                     (sb_onesD, c_onesD), (sb_Prow, c_Prow), (sb_Pn, c_Pn),
                     (sb_gnw, gnwd), (sb_gnb, gnbd), (sb_dec64, dec64d),
                     (sb_u64, u64d), (sb_dec128, dec128d)):
        s_ld.inc(SY.dma_start(out=dst[:], in_=src), 16)
    for i in range(5):
        s_ld.inc(SY.dma_start(out=sb_m[i][:], in_=c_m[i]), 16)
    s_ld.inc(SY.dma_start(out=sb_E16[:], in_=bass.AP(c_E16.tensor, 0, [[GI, 128], [128 * GI, nI], [1, GI]])), 16)
    for p in "kvrg":
        s_ld.inc(SY.dma_start(out=sb_tmix[p][:], in_=tmix[p]), 16)
    s_lg.inc(GP.dma_start(out=sb_id[:], in_=c_idn), 16)
    s_ld.wait(VE)
    VE.memset(sb_zero[:], 0.0)
    VE.memset(sb_eps[:], 1e-5)
    for _p in pb:
        VE.memset(_p[:], 0.0)
    bar()

    escope("P0")
    scope("P1")
    AR.reset()
    p1_x = [AR.t("p1x", (128, TT), F32) for _ in range(2)]
    p1_sq = [AR.t("p1sq", (128, TT), F32) for _ in range(2)]
    p1_row = AR.t("p1row", (1, 4 * TT), F32)
    p1_xn = [AR.t("p1xn", (128, TT), BF16) for _ in range(2)]
    widths = [1] + [TT] * nTT
    offs = np.cumsum([0] + widths).tolist()
    for ti, (o0, w) in enumerate(zip(offs[:-1], widths)):
        SY.wait_ge(s_dv.h, s_dv.v)
        mmv1 = []
        for ic in range(nI):
            buf = p1_x[ic % 2]
            sqb = p1_sq[ic % 2]
            if ic >= 2:
                SY.wait_ge(s_mm.h, mmv1[ic - 2])
                SC.wait_ge(s_mm.h, mmv1[ic - 2])
            sx[ic % 2].inc(SY.dma_start(out=buf[:, 0:w], in_=xT[128 * ic:128 * (ic + 1), o0:o0 + w],
                                        allow_slow_non_contiguous=(w == 1)), 16)
            sx[ic % 2].wait(SC)
            s_ac.inc(SC.activation(sqb[:, 0:w], buf[:, 0:w], AF.Square), 1)
            sx[ic % 2].wait(TE)
            if ic == 0:
                s_dv.wait(TE)
            TE.matmul(pb[4][0:1, 0:w], sb_onesD[:], buf[:, 0:w],
                      start=(ic == 0), stop=(ic == nI - 1))
            s_ac.wait(TE)
            s_mm.inc(TE.matmul(pb[4][32:33, 0:w], sb_onesD[:], sqb[:, 0:w],
                               start=(ic == 0), stop=(ic == nI - 1)), 1)
            mmv1.append(s_mm.v)
        s_mm.wait(VE)
        mrow = p1_row[0:1, 0:w]
        rrow = p1_row[0:1, TT:TT + w]
        trow = p1_row[0:1, 2 * TT:2 * TT + w]
        qrow = p1_row[0:1, 3 * TT:3 * TT + w]
        s_dv.inc(VE.tensor_scalar_mul(mrow, pb[4][0:1, 0:w], 1.0 / D), 1)
        VE.wait_ge(s_dv.h, s_dv.v)
        s_dv.inc(VE.tensor_scalar_mul(qrow, pb[4][32:33, 0:w], 1.0 / D), 1)
        VE.wait_ge(s_dv.h, s_dv.v)
        s_dv.inc(VE.tensor_tensor(trow, mrow, mrow, Alu.mult), 1)
        VE.wait_ge(s_dv.h, s_dv.v)
        s_dv.inc(VE.tensor_sub(trow, qrow, trow), 1)
        VE.wait_ge(s_dv.h, s_dv.v)
        s_dv.wait(SC)
        s_ac.inc(SC.activation(qrow, trow, AF.Sqrt, bias=sb_eps[0:1, 0:1]), 1)
        s_ac.wait(VE)
        s_dv.inc(VE.reciprocal(rrow, qrow), 1)
        VE.wait_ge(s_dv.h, s_dv.v)
        s_dv.wait(TE)
        TE.matmul(pb[5][:, 0:w], sb_ones1[:], mrow, start=True, stop=True)
        s_mm.inc(TE.matmul(pb[6][:, 0:w], sb_ones1[:], rrow, start=True, stop=True), 1)
        SY.wait_ge(s_mm.h, s_mm.v)
        dvv2 = []
        e16v = []
        for ic in range(nI):
            buf = p1_x[ic % 2]
            xnb = p1_xn[ic % 2]
            if ic >= 2:
                SY.wait_ge(s_dv.h, dvv2[ic - 2])
            sx[ic % 2].inc(SY.dma_start(out=buf[:, 0:w], in_=xT[128 * ic:128 * (ic + 1), o0:o0 + w],
                                        allow_slow_non_contiguous=(w == 1)), 16)
            sx[ic % 2].wait(VE)
            s_mm.wait(VE)
            if ic >= 2:
                VE.wait_ge(s_ld.h, s_ld.v)
                if ti > 0:
                    VE.wait_ge(s_mm.h, e16v[ic - 2])
            VE.tensor_sub(buf[:, 0:w], buf[:, 0:w], pb[5][:, 0:w])
            s_dv.inc(VE.tensor_tensor(xnb[:, 0:w], buf[:, 0:w], pb[6][:, 0:w], Alu.mult), 1)
            dvv2.append(s_dv.v)
            s_dv.wait(SY)
            s_ld.inc(SY.dma_start(out=xn_s[128 * ic:128 * (ic + 1), o0:o0 + w], in_=xnb[:, 0:w],
                                  allow_slow_non_contiguous=(w == 1)), 16)
            if ti > 0:
                s_dv.wait(TE)
                s_mm.inc(TE.matmul(pb[4][64:64 + GI, 0:w], sb_E16[:, ic * GI:(ic + 1) * GI],
                                   xnb[:, 0:w], start=(ic == 0), stop=(ic == nI - 1)), 1)
                e16v.append(s_mm.v)
        if ti > 0:
            s_mm.wait(VE)
            s_dv.inc(VE.tensor_copy(sb_sgrp[:, o0 - 1:o0 - 1 + w], pb[4][64:64 + GI, 0:w]), 1)
    bar()

    if stop_after == "P1":
        return nc, ["out"]

    escope("P1")
    scope("P2")
    AR.reset()
    bigA = AR.t("p2mix", (128, nI * T), BF16)
    p2_xn = AR.t("p2xn", (128, T + 1), BF16)
    p2_d = AR.t("p2d", (128, T), BF16)
    p2_ws = [AR.t("p2ws", (128, nI * 128), BF16) for _ in range(2)]
    p2_ev = AR.t("p2ev", (128, T), BF16)

    def bigA_c(c):
        return bigA[:, c * T:(c + 1) * T]

    def wslab_load(w_ap, oc, buf):
        src = bass.AP(w_ap.tensor, oc * 128, [[D, 128], [128 * D, nI], [1, 128]])
        wsl[oc % 2].inc(GP.dma_start(out=buf[:].rearrange("p (a o) -> p a o", a=nI), in_=src), 16)
        return wsl[oc % 2].v

    for pi_, p in enumerate("kvrg"):
        for ic in range(nI):
            s_dv.wait(SY)
            s_ld.inc(SY.dma_start(out=p2_xn[:], in_=xn_s[128 * ic:(128 * ic + 128), :]), 16)
            s_ld.wait(VE)
            if pi_ > 0 or ic >= 1:
                TE.wait_ge(s_mm.h, s_mm.v)
                VE.wait_ge(s_mm.h, s_mm.v)
            VE.tensor_sub(p2_d[:], p2_xn[:, 1:T + 1], p2_xn[:, 0:T])
            s_dv.inc(VE.scalar_tensor_tensor(bigA_c(ic), p2_d[:], sb_tmix[p][:, ic:ic + 1],
                                             p2_xn[:, 0:T], Alu.mult, Alu.add), 1)
        lg_vals = {}
        GP.wait_ge(s_mm.h, s_mm.v)
        lg_vals[0] = wslab_load(wT[p], 0, p2_ws[0])
        lg_vals[1] = wslab_load(wT[p], 1, p2_ws[1])
        dst = {"k": k_s, "v": v_s, "r": r_s, "g": g_s}[p]
        for oc in range(nC):
            buf = p2_ws[oc % 2]
            bs = 0
            if p == "k" and oc == nC - 1 and "dbg3" in dump:
                SY.wait_ge(s_lg.h, lg_vals[oc])
                SY.wait_ge(s_dv.h, s_dv.v)
                s_ld.inc(SY.dma_start(out=dbg3, in_=buf[:]), 16)
                s_ld.inc(SY.dma_start(out=dbg4, in_=bigA[:]), 16)
            TE.wait_ge(wsl[oc % 2].h, lg_vals[oc])
            TE.wait_ge(s_dv.h, s_dv.v)
            if p == "g":
                TE.wait_ge(s_ac.h, s_ac.v)
            for a in range(nI):
                for tt in range(nTT):
                    ins = TE.matmul(pb[bs + tt][:, 0:TT], buf[:, a * 128:(a + 1) * 128],
                                    bigA_c(a)[:, tt * TT:(tt + 1) * TT],
                                    start=(a == 0), stop=(a == nI - 1))
            s_mm.inc(ins, 1)
            if oc + 2 < nC:
                GP.wait_ge(s_mm.h, s_mm.v)
                lg_vals[oc + 2] = wslab_load(wT[p], oc + 2, buf)
            s_mm.wait(VE)
            s_mm.wait(SC)
            s_ld.wait(VE)
            s_ld.wait(SC)
            for tt in range(nTT):
                if p == "g":
                    ins = SC.activation(p2_ev[:, tt * TT:(tt + 1) * TT], pb[bs + tt][:, 0:TT], AF.Silu)
                else:
                    ins = VE.tensor_copy(p2_ev[:, tt * TT:(tt + 1) * TT], pb[bs + tt][:, 0:TT])
            (s_ac if p == "g" else s_dv).inc(ins, 1)
            (s_ac if p == "g" else s_dv).wait(SY)
            s_ld.inc(SY.dma_start(out=dst[oc * 128:(oc + 1) * 128, :], in_=p2_ev[:]), 16)
    bar()

    if stop_after == "P2":
        return nc, ["out"]

    escope("P2")
    scope("P3")
    AR.reset()
    p3_k64 = AR.t("k64", (64, 2 * T), BF16)
    p3_r64 = AR.t("r64", (64, 2 * T), BF16)
    p3_vfm = AR.t("vfm", (128, T), BF16)
    p3_kfm = AR.t("kfm", (128, T), BF16)
    p3_lw64 = AR.t("lw64", (64, 2), F32)
    p3_lw128 = AR.t("lw128", (128, 1), F32)
    p3_tmp64 = AR.t("tmp64", (64, 2 * NTCOL), F32)
    p3_T64 = AR.t("T64", (64, 2 * TBL_W), F32)
    p3_tmp128 = AR.t("tmp128", (128, NTCOL), F32)
    p3_T128 = AR.t("T128", (128, NTCOL), F32)
    p3_patkf = AR.t("patkf", (64, 2 * KW), F32)
    p3_patrf = AR.t("patrf", (64, 2 * RW), F32)
    p3_patk = AR.t("patk", (64, 2 * KW), BF16)
    p3_patr = AR.t("patr", (64, 2 * RW), BF16)
    p3_sck = AR.t("sck", (64, 2 * nN * KW), BF16)
    p3_scr = AR.t("scr", (64, 2 * nN * RW), BF16)
    p3_khat = AR.t("khat", (128, T), BF16)
    p3_vtok = AR.t("vtok", (128, 256), BF16)
    p3_A2F = AR.t("A2F", (128, 256), F32)
    p3_A2 = AR.t("A2", (128, 256), BF16)
    p3_SA = AR.t("SA", (64, 64), F32)
    p3_SB = AR.t("SB", (64, 64), F32)
    p3_SbfA = AR.t("SbfA", (64, 64), BF16)
    p3_SbfB = AR.t("SbfB", (64, 64), BF16)
    p3_w128 = AR.t("w128", (64, 2), F32)
    p3_yc = AR.t("yc", (128, T), BF16)
    p3_sfin = AR.t("sfin", (128, 64), F32)

    def t64ap(dims, p0):
        return bass.AP(p3_T64, p0 + 16, [[2 * TBL_W, 64], [TBL_W, 2]] + dims)

    def dst_dims(dims):
        o, acc = [], 1
        for s_, c_ in reversed(dims):
            o.append([acc, c_])
            acc *= c_
        return list(reversed(o))

    patk_off = {}
    off = 0
    for nm in PATK:
        w = 128 if nm == "ku" else pat_width(PATKS[nm][0])
        patk_off[nm] = (off, w)
        off += 2 * w
    patr_off = {}
    off = 0
    for nm in PATR:
        w = pat_width(PATRS[nm][0])
        patr_off[nm] = (off, w)
        off += 2 * w

    def sck_ap(nm, h, n, c0=0, w_=None):
        po, w = patk_off[nm]
        w_ = w if w_ is None else w_
        return bass.AP(p3_sck, po * nN + (h * nN + n) * w + c0, [[2 * nN * KW, 64], [1, w_]])

    def scr_ap(nm, h, n, c0=0, w_=None):
        po, w = patr_off[nm]
        w_ = w if w_ is None else w_
        return bass.AP(p3_scr, po * nN + (h * nN + n) * w + c0, [[2 * nN * RW, 64], [1, w_]])

    def r64_ap(h, n):
        return bass.AP(p3_r64, h * T + n * 128, [[2 * T, 64], [1, 128]])

    T64v = p3_T64[:].rearrange("d (h c) -> d h c", h=2, c=TBL_W)
    tmp64v = p3_tmp64[:].rearrange("d (h c) -> d h c", h=2)

    for c in range(nC):
        s_dv.wait(SY)
        for src, dstt in ((k_s, p3_k64), (r_s, p3_r64)):
            ap = bass.AP(src.tensor, c * 128 * T, [[T, 64], [64 * T, 2], [1, T]])
            s_ld.inc(SY.dma_start(out=dstt[:].rearrange("d (h t) -> d h t", h=2), in_=ap), 16)
        s_ld.inc(SY.dma_start(out=p3_vfm[:], in_=v_s[c * 128:(c + 1) * 128, :]), 16)
        s_ld.inc(SY.dma_start(out=p3_kfm[:], in_=k_s[c * 128:(c + 1) * 128, :]), 16)
        SC.wait_ge(s_dv.h, s_dv.v)
        s_ac.inc(SC.activation(p3_lw64[:], sb_dec64[:, 2 * c:2 * c + 2], AF.Exp), 1)
        s_ac.inc(SC.activation(p3_lw128[:], sb_dec128[:, c:c + 1], AF.Exp), 1)
        s_ac.wait(VE)
        VE.tensor_tensor(tmp64v, bass.AP(sb_Prow, 0, [[NTCOL, 64], [0, 2], [1, NTCOL]]),
                         bass.AP(p3_lw64, 0, [[2, 64], [1, 2], [0, NTCOL]]), Alu.mult)
        VE.tensor_tensor(p3_tmp128[:], sb_Prow[:],
                         bass.AP(p3_lw128, 0, [[1, 128], [0, NTCOL]]), Alu.mult)
        s_dv.inc(VE.tensor_copy(T64v[:, :, NTCOL:TBL_W],
                                bass.AP(sb_u64, 2 * c, [[2 * nC, 64], [1, 2], [0, 1]])), 1)
        s_dv.wait(SC)
        s_ac.inc(SC.activation(T64v[:, :, 0:NTCOL], tmp64v, AF.Exp), 1)
        s_ac.inc(SC.activation(p3_T128[:], p3_tmp128[:], AF.Exp), 1)
        s_ac.wait(VE)
        VE.tensor_scalar_min(T64v[:, :, 0:NTCOL], T64v[:, :, 0:NTCOL], 1e30)
        VE.tensor_scalar_min(p3_T128[:], p3_T128[:], 1e30)
        s_dv.inc(VE.tensor_copy(p3_w128[:], bass.AP(p3_T64, 144, [[2 * TBL_W, 64], [TBL_W, 2]])), 1)
        VE.wait_ge(s_dv.h, s_dv.v)
        off = 0
        for nm in PATK:
            if nm == "ku":
                w, dims = 128, [[0, 128]]
                srcp = bass.AP(p3_T64, NTCOL, [[2 * TBL_W, 64], [TBL_W, 2]] + dims)
            else:
                dims, p0 = PATKS[nm]
                w = pat_width(dims)
                srcp = t64ap(dims, p0)
            VE.tensor_copy(bass.AP(p3_patkf, off, [[2 * KW, 64], [w, 2]] + dst_dims(dims)), srcp)
            off += 2 * w
        off = 0
        for nm in PATR:
            dims, p0 = PATRS[nm]
            w = pat_width(dims)
            VE.tensor_copy(bass.AP(p3_patrf, off, [[2 * RW, 64], [w, 2]] + dst_dims(dims)),
                           t64ap(dims, p0))
            off += 2 * w
        VE.tensor_copy(p3_patk[:], p3_patkf[:])
        s_dv.inc(VE.tensor_copy(p3_patr[:], p3_patrf[:]), 1)
        VE.wait_ge(s_dv.h, s_dv.v)
        s_ld.wait(VE)
        for PAT, offs_, dstt, dstW, src64, patt, patW in (
                (PATK, patk_off, p3_sck, KW, p3_k64, p3_patk, KW),
                (PATR, patr_off, p3_scr, RW, p3_r64, p3_patr, RW)):
            for nm in PAT:
                po, w = offs_[nm]
                o_ap = bass.AP(dstt, po * nN, [[2 * nN * dstW, 64], [nN * w, 2], [w, nN], [1, w]])
                i0 = bass.AP(src64, 0, [[2 * T, 64], [T, 2], [128, nN], [1, w]])
                i1 = bass.AP(patt, po, [[2 * patW, 64], [w, 2], [0, nN], [1, w]])
                VE.tensor_tensor(o_ap, i0, i1, Alu.mult)
        VE.tensor_tensor(p3_khat[:].rearrange("p (n j) -> p n j", n=nN),
                         p3_kfm[:].rearrange("p (n j) -> p n j", n=nN),
                         bass.AP(p3_T128, 127 + 16, [[NTCOL, 128], [0, nN], [-1, 128]]),
                         Alu.mult)
        VE.memset(p3_SA[:], 0.0)
        VE.memset(p3_SB[:], 0.0)
        VE.memset(p3_SbfA[:], 0.0)
        s_dv.inc(VE.memset(p3_SbfB[:], 0.0), 1)
        for n in range(nN):
            TE.wait_ge(s_dv.h, s_dv.v)
            TE.transpose(pt[:, 0:128], p3_vfm[:, n * 128:(n + 1) * 128], sb_id[:])
            s_mm.inc(TE.transpose(pt[:, 128:256], p3_khat[:, n * 128:(n + 1) * 128], sb_id[:]), 1)
            s_mm.wait(VE)
            s_dv.inc(VE.tensor_copy(p3_vtok[:], pt[:, 0:256]), 1)
            TE.wait_ge(s_dv.h, s_dv.v)
            for h in range(2):
                c0 = h * 128
                TE.matmul(pb[3][64 * h:64 * h + 64, 0:128], p3_SbfA[:] if h == 0 else p3_SbfB[:],
                          scr_ap("rh", h, n), start=True, stop=False, skip_group_check=True)
                TE.matmul(pb[0][0:32, c0 + 32:c0 + 64], sck_ap("k1a", h, n), scr_ap("r1", h, n, 32, 32), start=True, stop=True)
                TE.matmul(pb[0][0:64, c0 + 64:c0 + 96], sck_ap("k1b", h, n), scr_ap("r1", h, n, 64, 32), start=True, stop=True)
                TE.matmul(pb[0][0:96, c0 + 96:c0 + 128], sck_ap("k1c", h, n), scr_ap("r1", h, n, 96, 32), start=True, stop=True)
                TE.matmul(pb[0][:, 256 + c0:256 + c0 + 128], sck_ap("k2", h, n, w_=128), scr_ap("r2", h, n, w_=128), start=True, stop=True)
                TE.matmul(pb[1][:, c0:c0 + 128], sck_ap("k3", h, n, w_=128), scr_ap("r3", h, n, w_=128), start=True, stop=True)
                TE.matmul(pb[1][:, 256 + c0:256 + c0 + 128], sck_ap("kd8", h, n, w_=128), scr_ap("rd8", h, n, w_=128), start=True, stop=True)
                ins = TE.matmul(pb[2][:, c0:c0 + 128], sck_ap("ku", h, n, w_=128), r64_ap(h, n), start=True, stop=True)
            s_mm.inc(ins, 1)
            s_mm.wait(VE)
            VE.select(p3_A2F[:], sb_m[0][:], pb[0][:, 0:256], sb_zero[:])
            VE.select(p3_A2F[:], sb_m[1][:], pb[0][:, 256:512], p3_A2F[:])
            VE.select(p3_A2F[:], sb_m[2][:], pb[1][:, 0:256], p3_A2F[:])
            VE.select(p3_A2F[:], sb_m[3][:], pb[1][:, 256:512], p3_A2F[:])
            VE.select(p3_A2F[:], sb_m[4][:], pb[2][:, 0:256], p3_A2F[:])
            s_dv.inc(VE.tensor_copy(p3_A2[:], p3_A2F[:]), 1)
            TE.wait_ge(s_dv.h, s_dv.v)
            for h in range(2):
                TE.matmul(pb[3][64 * h:64 * h + 64, 0:128], bass.AP(p3_vtok, 64 * h, [[256, 128], [1, 64]]),
                          p3_A2[:, 128 * h:128 * h + 128], start=False, stop=True, skip_group_check=True)
                ins = TE.matmul(pb[2][0:64, 256 + 64 * h:256 + 64 * h + 64],
                                bass.AP(p3_vtok, 128 + 64 * h, [[256, 128], [1, 64]]),
                                bass.AP(p3_vtok, 64 * h, [[256, 128], [1, 64]]), start=True, stop=True)
            s_mm.inc(ins, 1)
            s_mm.wait(VE)
            VE.tensor_copy(p3_yc[:, n * 128:(n + 1) * 128], pb[3][:, 0:128])
            s_dv.inc(VE.scalar_tensor_tensor(p3_SA[:], p3_SA[:], p3_w128[:, 0:1],
                                             pb[2][0:64, 256:320], Alu.mult, Alu.add), 1)
            VE.wait_ge(s_dv.h, s_dv.v)
            s_dv.inc(VE.scalar_tensor_tensor(p3_SB[:], p3_SB[:], p3_w128[:, 1:2],
                                             pb[2][0:64, 320:384], Alu.mult, Alu.add), 1)
            VE.wait_ge(s_dv.h, s_dv.v)
            VE.tensor_copy(p3_SbfA[:], p3_SA[:])
            s_dv.inc(VE.tensor_copy(p3_SbfB[:], p3_SB[:]), 1)
        s_dv.wait(SY)
        s_ld.inc(SY.dma_start(out=y_s[c * 128:(c + 1) * 128, :], in_=p3_yc[:]), 16)
        VE.wait_ge(s_dv.h, s_dv.v)
        VE.tensor_scalar_mul(p3_sfin[0:64, :], p3_SA[:], sb_selinv[0:64, 0:1])
        s_dv.inc(VE.tensor_scalar_mul(p3_sfin[64:128, :], p3_SB[:], sb_selinv[0:64, 0:1]), 1)
        s_dv.wait(SY)
        s_ld.inc(SY.dma_start(out=cc_in[c * 128:(c + 1) * 128, :], in_=p3_sfin[:]), 16)
    bar()

    if stop_after == "P3":
        return nc, ["out"]

    escope("P3")
    scope("CC")
    s_cc.inc(GP.collective_compute("AllReduce", Alu.add,
                                   replica_groups=[[0, 1], [2, 3], [4, 5], [6, 7]],
                                   ins=[cc_in.opt()], outs=[cc_out.opt()]), 1)
    s_cc.wait(SY)
    bar()

    escope("CC")
    scope("P3b")
    AR.reset()
    bigZ = AR.t("p3z", (128, nI * T), BF16)
    b3_r64 = AR.t("b3r64", (64, 2 * T), BF16)
    b3_yc = AR.t("b3yc", (128, T), BF16)
    b3_g = AR.t("b3g", (128, T), BF16)
    b3_lw64 = AR.t("b3lw64", (64, 2), F32)
    b3_tmp64 = AR.t("b3tmp64", (64, 2 * NTCOL), F32)
    b3_T64 = AR.t("b3T64", (64, 2 * TBL_W), F32)
    b3_tmpn = AR.t("b3tmpn", (64, 2 * nN), F32)
    b3_w128p = AR.t("b3w128p", (64, 2 * nN), F32)
    b3_rhf = AR.t("b3rhf", (64, 2 * 128), F32)
    b3_rh = AR.t("b3rh", (64, 2 * 128), BF16)
    b3_rbar = AR.t("b3rbar", (64, 2 * T), BF16)
    b3_SiA = AR.t("b3SiA", (64, 64), F32)
    b3_SiB = AR.t("b3SiB", (64, 64), F32)
    b3_SiAb = AR.t("b3SiAb", (64, 64), BF16)
    b3_SiBb = AR.t("b3SiBb", (64, 64), BF16)
    b3_sq = AR.t("b3sq", (128, TT), BF16)
    b3_row = AR.t("b3row", (2, 4 * TT), F32)
    b3_t1 = AR.t("b3t1", (128, TT), F32)

    def bigZ_c(c):
        return bigZ[:, c * T:(c + 1) * T]

    T64vb = b3_T64[:].rearrange("d (h c) -> d h c", h=2, c=TBL_W)
    tmp64vb = b3_tmp64[:].rearrange("d (h c) -> d h c", h=2)
    rb_all = bass.AP(b3_rbar, 0, [[2 * T, 64], [T, 2], [128, nN], [1, 128]])
    for c in range(nC):
        s_dv.wait(SY)
        ap = bass.AP(r_s.tensor, c * 128 * T, [[T, 64], [64 * T, 2], [1, T]])
        s_ld.inc(SY.dma_start(out=b3_r64[:].rearrange("d (h t) -> d h t", h=2), in_=ap), 16)
        s_ld.inc(SY.dma_start(out=b3_yc[:], in_=y_s[c * 128:(c + 1) * 128, :]), 16)
        s_ld.inc(SY.dma_start(out=b3_g[:], in_=g_s[c * 128:(c + 1) * 128, :]), 16)
        s_ld.inc(SY.dma_start(out=b3_SiA[:], in_=cc_out[c * 128:c * 128 + 64, :]), 16)
        s_ld.inc(SY.dma_start(out=b3_SiB[:], in_=cc_out[c * 128 + 64:(c + 1) * 128, :]), 16)
        SC.wait_ge(s_dv.h, s_dv.v)
        s_ac.inc(SC.activation(b3_lw64[:], sb_dec64[:, 2 * c:2 * c + 2], AF.Exp), 1)
        s_ac.wait(VE)
        VE.tensor_tensor(tmp64vb, bass.AP(sb_Prow, 0, [[NTCOL, 64], [0, 2], [1, NTCOL]]),
                         bass.AP(b3_lw64, 0, [[2, 64], [1, 2], [0, NTCOL]]), Alu.mult)
        s_dv.inc(VE.tensor_tensor(b3_tmpn[:].rearrange("d (h n) -> d h n", h=2),
                                  bass.AP(sb_Pn, 0, [[nN, 64], [0, 2], [1, nN]]),
                                  bass.AP(b3_lw64, 0, [[2, 64], [1, 2], [0, nN]]), Alu.mult), 1)
        s_dv.wait(SC)
        s_ac.inc(SC.activation(T64vb[:, :, 0:NTCOL], tmp64vb, AF.Exp), 1)
        s_ac.inc(SC.activation(b3_w128p[:], b3_tmpn[:], AF.Exp), 1)
        s_ac.wait(VE)
        VE.tensor_scalar_min(b3_w128p[:], b3_w128p[:], 1e30)
        VE.tensor_copy(bass.AP(b3_rhf, 0, [[2 * 128, 64], [128, 2], [1, 128]]),
                       bass.AP(b3_T64, 16, [[2 * TBL_W, 64], [TBL_W, 2], [1, 128]]))
        s_dv.inc(VE.tensor_copy(b3_rh[:], b3_rhf[:]), 1)
        VE.wait_ge(s_dv.h, s_dv.v)
        s_ld.wait(VE)
        VE.tensor_tensor(rb_all,
                         bass.AP(b3_r64, 0, [[2 * T, 64], [T, 2], [128, nN], [1, 128]]),
                         bass.AP(b3_rh, 0, [[2 * 128, 64], [128, 2], [0, nN], [1, 128]]),
                         Alu.mult)
        s_dv.inc(VE.tensor_tensor(rb_all, rb_all,
                                  bass.AP(b3_w128p, 0, [[2 * nN, 64], [nN, 2], [1, nN], [0, 128]]),
                                  Alu.mult), 1)
        VE.tensor_scalar_mul(b3_SiAb[:], b3_SiA[:], sb_sel64[0:64, 0:1])
        s_dv.inc(VE.tensor_scalar_mul(b3_SiBb[:], b3_SiB[:], sb_sel64[0:64, 0:1]), 1)
        for n in range(nN):
            TE.wait_ge(s_dv.h, s_dv.v)
            TE.matmul(pb[3][0:64, 0:128], b3_SiAb[:],
                      bass.AP(b3_rbar, n * 128, [[2 * T, 64], [1, 128]]), start=True, stop=True)
            s_mm.inc(TE.matmul(pb[3][64:128, 0:128], b3_SiBb[:],
                               bass.AP(b3_rbar, T + n * 128, [[2 * T, 64], [1, 128]]),
                               start=True, stop=True), 1)
            s_mm.wait(VE)
            s_dv.inc(VE.tensor_add(b3_yc[:, n * 128:(n + 1) * 128],
                                   b3_yc[:, n * 128:(n + 1) * 128], pb[3][:, 0:128]), 1)
        for tt in range(nTT):
            t0 = tt * TT
            s_dv.wait(SC)
            SC.wait_ge(s_mm.h, s_mm.v)
            s_ac.inc(SC.activation(b3_sq[:], b3_yc[:, t0:t0 + TT], AF.Square), 1)
            s_dv.wait(TE)
            s_ac.wait(TE)
            TE.matmul(pb[4][0:2, 0:TT], sb_E2h[:], b3_yc[:, t0:t0 + TT], start=True, stop=True)
            s_mm.inc(TE.matmul(pb[4][32:34, 0:TT], sb_E2h[:], b3_sq[:], start=True, stop=True), 1)
            s_mm.wait(VE)
            mrow = b3_row[:, 0:TT]
            rrow = b3_row[:, TT:2 * TT]
            trow = b3_row[:, 2 * TT:3 * TT]
            qrow = b3_row[:, 3 * TT:4 * TT]
            s_dv.inc(VE.tensor_scalar_mul(mrow, pb[4][0:2, 0:TT], 1.0 / 64), 1)
            VE.wait_ge(s_dv.h, s_dv.v)
            s_dv.inc(VE.tensor_scalar_mul(qrow, pb[4][32:34, 0:TT], 1.0 / 64), 1)
            VE.wait_ge(s_dv.h, s_dv.v)
            s_dv.inc(VE.tensor_tensor(trow, mrow, mrow, Alu.mult), 1)
            VE.wait_ge(s_dv.h, s_dv.v)
            s_dv.inc(VE.tensor_sub(trow, qrow, trow), 1)
            VE.wait_ge(s_dv.h, s_dv.v)
            s_dv.wait(SC)
            s_ac.inc(SC.activation(qrow, trow, AF.Sqrt, bias=sb_eps[0:2, 0:1]), 1)
            s_ac.wait(VE)
            s_dv.inc(VE.reciprocal(rrow, qrow), 1)
            VE.wait_ge(s_dv.h, s_dv.v)
            s_dv.wait(TE)
            TE.matmul(pb[5][:, 0:TT], sb_F2[:], mrow, start=True, stop=True)
            s_mm.inc(TE.matmul(pb[6][:, 0:TT], sb_F2[:], rrow, start=True, stop=True), 1)
            s_mm.wait(VE)
            VE.tensor_sub(b3_t1[:], b3_yc[:, t0:t0 + TT], pb[5][:, 0:TT])
            VE.tensor_tensor(b3_t1[:], b3_t1[:], pb[6][:, 0:TT], Alu.mult)
            VE.tensor_scalar(b3_t1[:], b3_t1[:], sb_gnw[:, c:c + 1], sb_gnb[:, c:c + 1],
                             Alu.mult, Alu.add)
            s_dv.inc(VE.tensor_tensor(bigZ_c(c)[:, t0:t0 + TT], b3_t1[:],
                                      b3_g[:, t0:t0 + TT], Alu.mult), 1)
    bar()

    if stop_after == "P3b":
        return nc, ["out"]

    escope("P3b")
    scope("P4")
    p4_ws = [AR.t("p4ws", (128, nC * 128), BF16) for _ in range(2)]
    p4_x = AR.t("p4x", (128, TT), F32)
    p4_o = AR.t("p4o", (128, TT), F32)

    def woslab_load(ec, buf):
        src = bass.AP(woT.tensor, ec * 128, [[D, 128], [128 * D, nC], [1, 128]])
        wsl[ec % 2].inc(GP.dma_start(out=buf[:].rearrange("p (a o) -> p a o", a=nC), in_=src), 16)
        return wsl[ec % 2].v

    lg_vals = {0: woslab_load(0, p4_ws[0]), 1: woslab_load(1, p4_ws[1])}
    for ec in range(nI):
        buf = p4_ws[ec % 2]
        bs = 0
        TE.wait_ge(wsl[ec % 2].h, lg_vals[ec])
        TE.wait_ge(s_dv.h, s_dv.v)
        for a in range(nC):
            for tt in range(nTT):
                ins = TE.matmul(pb[bs + tt][:, 0:TT], buf[:, a * 128:(a + 1) * 128],
                                bigZ_c(a)[:, tt * TT:(tt + 1) * TT],
                                start=(a == 0), stop=(a == nC - 1))
        s_mm.inc(ins, 1)
        if ec + 2 < nI:
            GP.wait_ge(s_mm.h, s_mm.v)
            lg_vals[ec + 2] = woslab_load(ec + 2, buf)
        for tt in range(nTT):
            SY.wait_ge(s_dv.h, s_dv.v)
            s_ld.inc(SY.dma_start(out=p4_x[:, 0:TT],
                                  in_=xT[ec * 128:(ec + 1) * 128, 1 + tt * TT:1 + (tt + 1) * TT]), 16)
            s_ld.wait(VE)
            s_mm.wait(VE)
            s_dv.inc(VE.tensor_add(p4_o[:], pb[bs + tt][:, 0:TT], p4_x[:, 0:TT]), 1)
            s_dv.wait(SY)
            s_ld.inc(SY.dma_start(out=part_s[ec * 128:(ec + 1) * 128, tt * TT:(tt + 1) * TT],
                                  in_=p4_o[:]), 16)
            s_ld.wait(VE)
    bar()

    if stop_after == "P4":
        return nc, ["out"]

    escope("P4")
    scope("P5a")
    AR.reset()
    f_xn = AR.t("fxn", (128, nI * TH2), BF16)
    f_pk = [AR.t("fpk", (128, nJb * 128), I32) for _ in range(2)]
    f_int = AR.t("fint", (128, nJb * 128), I32)
    f_sc4 = [AR.t("fsc4", (4, nJb * 128), BF16) for _ in range(2)]
    f_slab = [AR.t("fslab", (128, nJb * 128), F32) for _ in range(2)]
    f_sz = [AR.t("fsz", (GI, 256), F32) for _ in range(2)]
    f_zsc = [AR.t("fzsc", (GI, 128), BF16) for _ in range(2)]
    f_wbf = [AR.t("fwbf", (128, 2 * nJb * 128), BF16) for _ in range(2)]
    f_gev = AR.t("fgev", (128, TH2), BF16)
    f_h = AR.t("fh", (128, TH2), BF16)
    NSP = (nJb * 128 + 511) // 512

    def fxn_c(a):
        return f_xn[:, a * TH2:(a + 1) * TH2]

    passes = [(th, oc, pj) for th in range(2) for oc in range(nO) for pj in range(2)]
    pair_of = lambda q: q % 3
    prep_done = [None] * len(passes)
    prep_doneA = [None] * len(passes)
    mmv = [None] * len(passes)
    evd_ac = [0] * len(passes)
    evd_dv = [0] * len(passes)

    ld_pks = [None] * (len(passes) + 2)
    ld_szs = [None] * (len(passes) + 2)
    ld_scs = [None] * (len(passes) + 2)

    def emit_loads(q, gate_dv=None, gate_mm=None):
        th, oc, pj = passes[q]
        wq_t = (wgq if pj == 0 else wuq)
        sT_t = (gsT if pj == 0 else usT)
        zT_t = (gzT if pj == 0 else uzT)
        s4_t = (gs4 if pj == 0 else us4)
        if gate_dv is not None:
            GP.wait_ge(s_dv.h, gate_dv)
            SY.wait_ge(s_dv.h, gate_dv)
        if gate_mm is not None:
            GP.wait_ge(s_mm.h, gate_mm)
        pks[q % 2].inc(GP.dma_start(out=f_pk[q % 2][:].rearrange("p (a o) -> p a o", a=nJb),
                                     in_=bass.AP(wq_t.tensor, oc * 128, [[I, 128], [128 * I, nJb], [1, 128]])), 16)
        szs[q % 2].inc(SY.dma_start(out=f_sz[q % 2][:, 0:128], in_=sT_t[:, oc * 128:(oc + 1) * 128]), 16)
        szs[q % 2].inc(SY.dma_start(out=f_sz[q % 2][:, 128:256], in_=zT_t[:, oc * 128:(oc + 1) * 128]), 16)
        scs[q % 2].inc(GP.dma_start(out=f_sc4[q % 2][:].rearrange("p (a o) -> p a o", a=nJb),
                                    in_=bass.AP(s4_t.tensor, oc * 128, [[nJb * I, 4], [I, nJb], [1, 128]])), 16)
        ld_pks[q] = pks[q % 2].v
        ld_szs[q] = szs[q % 2].v
        ld_scs[q] = scs[q % 2].v

    def emit_build(q):
        pk = f_pk[q % 2]
        sc4 = f_sc4[q % 2]
        slab = f_slab[q % 2]
        sz = f_sz[q % 2]
        zsc = f_zsc[q % 2]
        wbf = f_wbf[q % 2]
        for b_ in range(NSP):
            wv = min(512, nJb * 128 - b_ * 512)
            TE.wait_ge(scs[q % 2].h, ld_scs[q])
            if b_ > 0 or q > 0:
                TE.wait_ge(s_ac.h, s_ac.v)
            s_mm.inc(TE.matmul(pb[6][:, 0:wv], sb_E4b[:], sc4[:, b_ * 512:b_ * 512 + wv],
                               start=True, stop=True), 1)
            s_mm.wait(SC)
            s_ac.inc(SC.activation(slab[:, b_ * 512:b_ * 512 + wv], pb[6][:, 0:wv], AF.Copy), 1)
        VE.wait_ge(szs[q % 2].h, ld_szs[q])
        s_dv.inc(VE.scalar_tensor_tensor(zsc[:], sz[:, 128:256], -1.0, sz[:, 0:128],
                                         Alu.mult, Alu.mult), 1)
        VE.wait_ge(pks[q % 2].h, ld_pks[q])
        VE.wait_ge(s_ac.h, s_ac.v)
        if q >= 2 and mmv[q - 2] is not None:
            VE.wait_ge(s_mm.h, mmv[q - 2])
        VE.tensor_single_scalar(f_int[:], pk[:], 4, Alu.logical_shift_right)
        s_dv.inc(VE.tensor_tensor(wbf[:, 0:nJb * 128], f_int[:], slab[:], Alu.mult), 1)
        prep_doneA[q] = s_dv.v
        VE.tensor_single_scalar(f_int[:], pk[:], 15, Alu.bitwise_and)
        s_dv.inc(VE.tensor_tensor(wbf[:, nJb * 128:2 * nJb * 128], f_int[:], slab[:], Alu.mult), 1)
        prep_done[q] = s_dv.v

    emit_loads(0)
    emit_loads(1)
    emit_build(0)
    emit_build(1)
    for q, (th, oc, pj) in enumerate(passes):
        tb = th * TH2
        if q + 2 < len(passes):
            emit_loads(q + 2, gate_dv=prep_done[q],
                       gate_mm=(mmv[q - 1] if q >= 1 else s_mm.v))
        if oc == 0 and pj == 0:
            SY.wait_ge(s_mm.h, s_mm.v)
            src = bass.AP(xn_s.tensor, 1 + tb, [[T + 1, 128], [128 * (T + 1), nI], [1, TH2]])
            s_xh.inc(SY.dma_start(out=f_xn[:].rearrange("p (a t) -> p a t", a=nI), in_=src), 16)
        ldx = s_xh.v
        pr = pair_of(q)
        wbf = f_wbf[q % 2]
        zsc = f_zsc[q % 2]
        TE.wait_ge(s_dv.h, prep_doneA[q])
        TE.wait_ge(s_xh.h, ldx)
        if q >= 3:
            TE.wait_ge(s_ac.h, evd_ac[q - 3])
            TE.wait_ge(s_dv.h, evd_dv[q - 3])
        for tt in range(nT2):
            TE.matmul(pb[2 * pr + tt][:, 0:TT2], zsc[:],
                      sb_sgrp[:, tb + tt * TT2:tb + (tt + 1) * TT2],
                      start=True, stop=False)
        for a in range(2 * nJb):
            if a == nJb:
                TE.wait_ge(s_dv.h, prep_done[q])
            for tt in range(nT2):
                ins = TE.matmul(pb[2 * pr + tt][:, 0:TT2], wbf[:, a * 128:(a + 1) * 128],
                                fxn_c(a)[:, tt * TT2:(tt + 1) * TT2],
                                start=False, stop=(a == 2 * nJb - 1))
        s_mm.inc(ins, 1)
        mmv[q] = s_mm.v
        s_mm.wait(SC)
        s_mm.wait(VE)
        if pj == 0:
            if q >= 2:
                SC.wait_ge(s_dv.h, s_dv.v)
            for tt in range(nT2):
                ins = SC.activation(f_gev[:, tt * TT2:(tt + 1) * TT2],
                                    pb[2 * pr + tt][:, 0:TT2], AF.Silu)
            s_ac.inc(ins, 1)
        else:
            s_ac.wait(VE)
            s_ld.wait(VE)
            for tt in range(nT2):
                ins = VE.tensor_tensor(f_h[:, tt * TT2:(tt + 1) * TT2],
                                       f_gev[:, tt * TT2:(tt + 1) * TT2],
                                       pb[2 * pr + tt][:, 0:TT2], Alu.mult)
            s_dv.inc(ins, 1)
            s_dv.wait(SY)
            s_ld.inc(SY.dma_start(out=h_s[oc * 128:(oc + 1) * 128, tb:tb + TH2], in_=f_h[:]), 16)
        evd_ac[q] = s_ac.v
        evd_dv[q] = s_dv.v
        if q + 2 < len(passes):
            emit_build(q + 2)
    bar()

    if stop_after == "P5a":
        return nc, ["out"]

    escope("P5a")
    scope("P5c")
    AR.reset()
    d_pk = AR.t("dpk", (128, nJd * 128), I32)
    d_int = AR.t("dint", (128, nJd * 128), I32)
    d_t = [AR.t("dt", (128, nJd * 128), F32) for _ in range(2)]
    d_zs = AR.t("dzs", (128, nJd * 128), F32)
    d_c4 = AR.t("dc4", (4, nJd * 128), F32)
    d_wbf = AR.t("dwbf", (128, 2 * nJd * 128), BF16)
    NSPD = (nJd * 128 + 511) // 512
    for ec in range(nI):
        s_dv.wait(SY)
        s_dv.wait(GP)
        SC.wait_ge(s_mm.h, s_mm.v)
        s_lg.inc(GP.dma_start(out=d_pk[:].rearrange("p (a o) -> p a o", a=nJd),
                              in_=bass.AP(wdq.tensor, ec * 128, [[D, 128], [128 * D, nJd], [1, 128]])), 16)
        s_la.inc(SC.dma_start(out=d_c4[:].rearrange("p (a o) -> p a o", a=nJd),
                              in_=bass.AP(dz4.tensor, ec * 128, [[nJd * D, 4], [D, nJd], [1, 128]])), 16)
        s_la.wait(TE)
        for b_ in range(NSPD):
            wv = min(512, nJd * 128 - b_ * 512)
            bank = pb[b_ % 6]
            if ec > 0 or b_ >= 6:
                TE.wait_ge(s_dv.h, s_dv.v)
            s_mm.inc(TE.matmul(bank[:, 0:wv], sb_E4[:], d_c4[:, b_ * 512:b_ * 512 + wv],
                               start=True, stop=True), 1)
            s_mm.wait(VE)
            s_dv.inc(VE.tensor_copy(d_zs[:, b_ * 512:b_ * 512 + wv], bank[:, 0:wv]), 1)
        VE.wait_ge(s_lg.h, s_lg.v)
        VE.tensor_single_scalar(d_int[:], d_pk[:], 4, Alu.logical_shift_right)
        s_dv.inc(VE.tensor_sub(d_t[0][:], d_int[:], d_zs[:]), 1)
        VE.tensor_single_scalar(d_int[:], d_pk[:], 15, Alu.bitwise_and)
        s_dv.inc(VE.tensor_sub(d_t[1][:], d_int[:], d_zs[:]), 1)
        VE.wait_ge(s_dv.h, s_dv.v)
        SC.wait_ge(s_dv.h, s_dv.v)
        s_la.inc(SC.dma_start(out=d_c4[:].rearrange("p (a o) -> p a o", a=nJd),
                              in_=bass.AP(ds4.tensor, ec * 128, [[nJd * D, 4], [D, nJd], [1, 128]])), 16)
        s_la.wait(TE)
        TE.wait_ge(s_dv.h, s_dv.v)
        for b_ in range(NSPD):
            wv = min(512, nJd * 128 - b_ * 512)
            bank = pb[b_ % 6]
            if b_ >= 6:
                TE.wait_ge(s_dv.h, s_dv.v)
            s_mm.inc(TE.matmul(bank[:, 0:wv], sb_E4[:], d_c4[:, b_ * 512:b_ * 512 + wv],
                               start=True, stop=True), 1)
            s_mm.wait(VE)
            s_dv.inc(VE.tensor_copy(d_zs[:, b_ * 512:b_ * 512 + wv], bank[:, 0:wv]), 1)
        VE.wait_ge(s_dv.h, s_dv.v)
        if ec == 0 and "dbg1" in dump:
            s_dv.inc(VE.tensor_copy(d_t[1 - 1][:], d_t[0][:]), 1)
            s_dv.wait(SY)
            s_ld.inc(SY.dma_start(out=dbg1, in_=d_t[0][:]), 16)
            s_ld.inc(SY.dma_start(out=dbg2, in_=d_zs[:]), 16)
            s_ld.wait(VE)
        VE.tensor_tensor(d_wbf[:, 0:nJd * 128], d_t[0][:], d_zs[:], Alu.mult)
        s_dv.inc(VE.tensor_tensor(d_wbf[:, nJd * 128:2 * nJd * 128], d_t[1][:], d_zs[:], Alu.mult), 1)
        s_dv.wait(SY)
        dst0 = bass.AP(wd_s.tensor, ec * 128, [[D, 128], [128 * D, nJd], [1, 128]])
        dst1 = bass.AP(wd_s.tensor, nJd * 128 * D + ec * 128, [[D, 128], [128 * D, nJd], [1, 128]])
        s_ld.inc(SY.dma_start(out=dst0, in_=d_wbf[:, 0:nJd * 128].rearrange("p (a o) -> p a o", a=nJd)), 16)
        s_ld.inc(SY.dma_start(out=dst1, in_=d_wbf[:, nJd * 128:2 * nJd * 128].rearrange("p (a o) -> p a o", a=nJd)), 16)
        s_ld.wait(VE)
    bar()

    if stop_after == "P5c":
        return nc, ["out"]

    escope("P5c")
    scope("P5b")
    AR.reset()
    q_h = AR.t("qh", (128, nO * TQ), BF16)
    q_ws = [AR.t("qws", (128, nO * 128), BF16) for _ in range(2)]
    q_x = AR.t("qx", (128, TQ), F32)
    q_o = AR.t("qo", (128, TQ), F32)

    def qh_c(a):
        return q_h[:, a * TQ:(a + 1) * TQ]

    for qq in range(nQ):
        t0 = qq * TQ
        SY.wait_ge(s_mm.h, s_mm.v)
        src = bass.AP(h_s.tensor, t0, [[T, 128], [128 * T, nO], [1, TQ]])
        s_hq.inc(SY.dma_start(out=q_h[:].rearrange("p (a t) -> p a t", a=nO), in_=src), 16)
        ldh = s_hq.v

        def wdslab_load(ec, buf):
            src2 = bass.AP(wd_s.tensor, ec * 128, [[D, 128], [128 * D, nO], [1, 128]])
            wds[ec % 2].inc(GP.dma_start(out=buf[:].rearrange("p (a o) -> p a o", a=nO), in_=src2), 16)
            return wds[ec % 2].v

        GP.wait_ge(s_mm.h, s_mm.v)
        lgv = {0: wdslab_load(0, q_ws[0]), 1: wdslab_load(1, q_ws[1])}
        for ec in range(nI):
            buf = q_ws[ec % 2]
            bank = pb[ec % 2]
            if qq == 0 and ec == 0 and "dbg5" in dump:
                SY.wait_ge(s_lg.h, lgv[0])
                s_ld.inc(SY.dma_start(out=dbg5, in_=buf[:]), 16)
                s_ld.inc(SY.dma_start(out=dbg6, in_=q_h[:]), 16)
            TE.wait_ge(wds[ec % 2].h, lgv[ec])
            TE.wait_ge(s_hq.h, ldh)
            if ec >= 2:
                TE.wait_ge(s_dv.h, s_dv.v)
            for a in range(nO):
                ins = TE.matmul(bank[:, 0:TQ], buf[:, a * 128:(a + 1) * 128],
                                qh_c(a), start=(a == 0), stop=(a == nO - 1))
            s_mm.inc(ins, 1)
            if ec + 2 < nI:
                GP.wait_ge(s_mm.h, s_mm.v)
                lgv[ec + 2] = wdslab_load(ec + 2, buf)
            SY.wait_ge(s_dv.h, s_dv.v)
            s_ld.inc(SY.dma_start(out=q_x[:, 0:TQ], in_=part_s[ec * 128:(ec + 1) * 128, t0:t0 + TQ]), 16)
            s_ld.wait(VE)
            s_mm.wait(VE)
            s_dv.inc(VE.tensor_add(q_o[:], bank[:, 0:TQ], q_x[:, 0:TQ]), 1)
            s_dv.wait(SY)
            s_ld.inc(SY.dma_start(out=out[ec * 128:(ec + 1) * 128, t0:t0 + TQ], in_=q_o[:]), 16)
            s_ld.wait(VE)
    SY.wait_ge(s_ld.h, s_ld.v)
    bar()
    escope("P5b")

    return nc, ["out"]


def kernel(**inputs):
    cfg = make_cfg()
    cons = host_constants(cfg)
    in_maps = [host_core_inputs(cfg, c, inputs, cons) for c in range(8)]
    nc, _ = build_program(cfg)
    res = run_bass_kernel_spmd(nc, in_maps, core_ids=list(range(8)))
    return host_assemble_output(cfg, res.results)
